# version 20
# speedup vs baseline: 1.2623x; 1.0093x over previous
import numpy as np
import ml_dtypes
from contextlib import ExitStack
import concourse.bass as bass
import concourse.mybir as mybir
from concourse.bass_utils import run_bass_kernel_spmd

F32 = mybir.dt.float32
BF16 = mybir.dt.bfloat16
I32 = mybir.dt.int32
AF = mybir.ActivationFunctionType
ALU = mybir.AluOpType
AX = mybir.AxisListType
IOA = bass.IndirectOffsetOnAxis

STAGE = 3
DEBUG = False

B, T, D = 16, 2048, 1024
TOK, NT = 2 * T, 32
E, H, CAP = 8, 4096, 2048
NH, HD, RD, QR, KVR = 16, 64, 32, 512, 256
FC, HCN, DCN = 8, 32, 8
SENT = 100000.0
GELU_A = 0.044715
GELU_2C = 2.0 * 0.7978845608028654
ENGS = ("sync", "tensor", "vector", "scalar", "gpsimd")


class Sem:
    def __init__(self, ctx, nc, name):
        self.h = ctx.enter_context(nc.semaphore(name))
        self.n = 0


class Phase:

    def __init__(self, nc, kb):
        self.nc = nc
        self.kb = kb
        self.items = {g: [] for g in ENGS}
        self.last = None

    def emit(self, eng, fn, waits=(), inc=None, inck=1):
        val = None
        if inc is not None:
            inc.n += inck
            val = inc.n
            self.last = (inc, val)
        self.items[eng].append((list(waits), fn, inc, inck, val))
        return self.last

    def seq(self, eng, fn, extra_waits=()):
        w = list(extra_waits)
        if self.last is not None:
            w.append(self.last)
        sem = {"sync": self.kb.LD, "gpsimd": self.kb.GP,
               "tensor": self.kb.TE, "vector": self.kb.VE,
               "scalar": self.kb.SE}[eng]
        inck = 16 if eng in ("sync", "gpsimd") else 1
        return self.emit(eng, fn, waits=w, inc=sem, inck=inck)

    def run(self):
        nc, kb = self.nc, self.kb
        drains = [(kb.LD, kb.LD.n)] + [(s, s.n) for s in
                  getattr(kb, "DRAINS", ()) if s.n]
        self.items["sync"].append((drains, None, None, 0, None))
        gdrains = [(kb.GP, kb.GP.n)] + [(s, s.n) for s in
                   getattr(kb, "SDRAINS", ()) if s.n]
        self.items["gpsimd"].append((gdrains, None, None, 0, None))
        with nc.Block() as bk:
            def mk(eng):
                auto = kb.AUTO[eng] if eng in ("vector", "scalar") else None

                def body(e):
                    prev = None
                    for waits, fn, sem, inck, val in self.items[eng]:
                        for (ws, wv) in waits:
                            e.wait_ge(ws.h, wv)
                        if fn is not None:
                            if auto is not None and prev is not None:
                                e.wait_ge(prev[0].h, prev[1])
                            inst = fn(e)
                            if inst is None:
                                if sem is not None and inck > 0:
                                    e.nop().then_inc(sem.h, inck)
                                continue
                            if sem is not None and inck > 0:
                                inst.then_inc(sem.h, inck)
                                if auto is not None:
                                    prev = (sem, val)
                            elif auto is not None:
                                auto.n += 1
                                inst.then_inc(auto.h, 1)
                                prev = (auto, auto.n)
                        elif sem is not None and inck > 0:
                            e.nop().then_inc(sem.h, inck)
                return body
            for eng in ENGS:
                if self.items[eng]:
                    getattr(bk, eng)(mk(eng))


class KBx:
    pass


def build(nph=99):
    nc = bass.Bass("TRN2", target_bir_lowering=False)
    dt_ = nc.dram_tensor

    _names = {}
    _osb, _ops = nc.sbuf_tensor, nc.psum_tensor

    def _uniq(name):
        i = _names.get(name, 0)
        _names[name] = i + 1
        return f"{name}_r{i}" if i else name

    nc.sbuf_tensor = lambda name, *a, **k: _osb(_uniq(name), *a, **k)
    nc.psum_tensor = lambda name, *a, **k: _ops(_uniq(name), *a, **k)

    x_in = dt_("x", [TOK, D], F32, kind="ExternalInput").ap()
    out = dt_("out", [TOK, D], F32, kind="ExternalOutput").ap()
    wg = dt_("wg", [D, E], F32, kind="ExternalInput").ap()
    w1 = dt_("w1", [D, H], F32, kind="ExternalInput").ap()
    b1 = dt_("b1", [H], F32, kind="ExternalInput").ap()
    w2 = dt_("w2", [H, D], F32, kind="ExternalInput").ap()
    b2 = dt_("b2", [D], F32, kind="ExternalInput").ap()
    wkv = dt_("wkv", [D, KVR], F32, kind="ExternalInput").ap()
    gkv = dt_("gkv", [KVR], F32, kind="ExternalInput").ap()
    wkdec = dt_("wkdec", [KVR, D], F32, kind="ExternalInput").ap()
    wvdec = dt_("wvdec", [KVR, D], F32, kind="ExternalInput").ap()
    wq = dt_("wq", [D, QR], F32, kind="ExternalInput").ap()
    wqdec = dt_("wqdec", [QR, D], F32, kind="ExternalInput").ap()
    wkropep = dt_("wkropep", [D, NH * RD], F32, kind="ExternalInput").ap()
    wqropep = dt_("wqropep", [QR, NH * RD], F32, kind="ExternalInput").ap()
    wo = dt_("wo", [D, D], F32, kind="ExternalInput").ap()
    cut128 = dt_("cut128", [128, 128], F32, kind="ExternalInput").ap()
    cident = dt_("cident", [128, 128], F32, kind="ExternalInput").ap()
    csuts = dt_("csuts", [32, 32], F32, kind="ExternalInput").ap()
    csutc = dt_("csutc", [32, 32], F32, kind="ExternalInput").ap()
    csut8 = dt_("csut8", [8, 8], F32, kind="ExternalInput").ap()
    cones128 = dt_("cones128", [1, 128], F32, kind="ExternalInput").ap()
    conescol = dt_("conescol", [128, 1], F32, kind="ExternalInput").ap()
    cindAr = dt_("cindAr", [1, 32], F32, kind="ExternalInput").ap()
    cindBr = dt_("cindBr", [1, 32], F32, kind="ExternalInput").ap()
    cindAc = dt_("cindAc", [32, 1], F32, kind="ExternalInput").ap()
    cindBc = dt_("cindBc", [32, 1], F32, kind="ExternalInput").ap()
    cones32r = dt_("cones32r", [1, 32], F32, kind="ExternalInput").ap()
    cones8c = dt_("cones8c", [8, 1], F32, kind="ExternalInput").ap()
    cones8r = dt_("cones8r", [1, 8], F32, kind="ExternalInput").ap()
    cltc = dt_("cltc", [8, 1], F32, kind="ExternalInput").ap()
    cmyhot = dt_("cmyhot", [8, 1], F32, kind="ExternalInput").ap()
    cebase = dt_("cebase", [1, 8], F32, kind="ExternalInput").ap()
    conesrowb = dt_("conesrowb", [1, 512], BF16, kind="ExternalInput").ap()
    ctrimask = dt_("ctrimask", [128, 128], F32, kind="ExternalInput").ap()
    ccosr = dt_("ccosr", [128, T], F32, kind="ExternalInput").ap()
    csinr = dt_("csinr", [128, T], F32, kind="ExternalInput").ap()
    cqmask = dt_("cqmask", [128, 1], F32, kind="ExternalInput").ap()
    conesb2048 = dt_("conesb2048", [1, T], BF16, kind="ExternalInput").ap()

    xdram = dt_("xdram", [TOK, D], F32).ap()
    lntm = dt_("lntm", [TOK, D], BF16).ap()
    lnlo = dt_("lnlo", [TOK, D], BF16).ap()
    stage = dt_("stg", [E * CAP, D], BF16).ap()
    rsout = dt_("rsout", [CAP, D], BF16).ap()
    yfm = dt_("yfm", [D, CAP], BF16).ap()
    yagin = dt_("yagin", [CAP, D], BF16).ap()
    yall = dt_("yall", [E * CAP, D], BF16).ap()
    aghin = dt_("aghin", [1, 32], F32).ap()
    aghout = dt_("aghout", [8, 32], F32).ap()
    bco = dt_("bco", [32, 8], F32).ap()
    bcoflat = bco.rearrange("c e -> (c e)").unsqueeze(0)
    contribfm = dt_("contribfm", [D, TOK], BF16).ap()
    kroped = dt_("kroped", [512, T], BF16).ap()
    qroped = dt_("qroped", [512, T], BF16).ap()
    vfmd = dt_("vfmd", [D, T], BF16).ap()
    mrowd = dt_("mrowd", [1, T], BF16).ap()

    dbgmap = {}
    if DEBUG:
        for nm in ["didx0", "didx1", "dcw0", "dcw1"]:
            dbgmap[nm] = dt_(nm, [128, 32], F32, kind="ExternalOutput").ap()
        dbgmap["dlg"] = dt_("dlg", [128, 256], F32, kind="ExternalOutput").ap()
        dbgmap["dtab"] = dt_("dtab", [8, 32], F32, kind="ExternalOutput").ap()
        dbgmap["drows"] = dt_("drows", [1, 64], F32, kind="ExternalOutput").ap()
        dbgmap["dsco"] = dt_("dsco", [32, 8], F32, kind="ExternalOutput").ap()
        dbgmap["dstk"] = dt_("dstk", [32, 8], F32, kind="ExternalOutput").ap()
        dbgmap["drs"] = dt_("drs", [CAP, D], BF16, kind="ExternalOutput").ap()
        dbgmap["dm16"] = dt_("dm16", [1, 16], F32, kind="ExternalOutput").ap()
        dbgmap["dqsr"] = dt_("dqsr", [1, T], F32, kind="ExternalOutput").ap()
        dbgmap["dksr"] = dt_("dksr", [1, 4], F32, kind="ExternalOutput").ap()
        dbgmap["dqh"] = dt_("dqh", [128, T], BF16, kind="ExternalOutput").ap()
        dbgmap["dkh"] = dt_("dkh", [128, T], BF16, kind="ExternalOutput").ap()
        dbgmap["dao0"] = dt_("dao0", [128, T], BF16, kind="ExternalOutput").ap()
        dbgmap["dpt"] = dt_("dpt", [128, 512], BF16, kind="ExternalOutput").ap()
        dbgmap["dkvn"] = dt_("dkvn", [128, 2 * T], BF16, kind="ExternalOutput").ap()
        dbgmap["dqc"] = dt_("dqc", [128, 4 * T], BF16, kind="ExternalOutput").ap()
        dbgmap["dqc2"] = dt_("dqc2", [128, 4 * T], BF16, kind="ExternalOutput").ap()
        dbgmap["dqcF"] = dt_("dqcF", [128, 4 * T], BF16, kind="ExternalOutput").ap()
        dbgmap["dwq"] = dt_("dwq", [128, FC * QR], BF16, kind="ExternalOutput").ap()
        dbgmap["dwkv"] = dt_("dwkv", [128, FC * KVR], BF16, kind="ExternalOutput").ap()
        dbgmap["dkrop"] = dt_("dkrop", [512, T], BF16, kind="ExternalOutput").ap()
        dbgmap["dqrop"] = dt_("dqrop", [512, T], BF16, kind="ExternalOutput").ap()
        dbgmap["dyag"] = dt_("dyag", [CAP, D], BF16, kind="ExternalOutput").ap()

    ctx = ExitStack()
    with ctx:
        def sb(name, shape, dtype):
            return ctx.enter_context(nc.sbuf_tensor(name, shape, dtype))

        kb = KBx()
        kb.LD = Sem(ctx, nc, "LD")
        kb.GP = Sem(ctx, nc, "GP")
        kb.TE = Sem(ctx, nc, "TE")
        kb.VE = Sem(ctx, nc, "VE")
        kb.SE = Sem(ctx, nc, "SE")
        kb.CC = [Sem(ctx, nc, f"CC{i}") for i in range(8)]
        kb.AUTO = {"vector": Sem(ctx, nc, "AUV"), "scalar": Sem(ctx, nc, "AUS")}
        kb.PLS = Sem(ctx, nc, "PLS")
        kb.WGH = Sem(ctx, nc, "WGH")
        kb.XS = [Sem(ctx, nc, "XS0"), Sem(ctx, nc, "XS1")]
        kb.LNS = [Sem(ctx, nc, "LNS0"), Sem(ctx, nc, "LNS1")]
        kb.XL = [Sem(ctx, nc, "XL0"), Sem(ctx, nc, "XL1")]
        kb.TT = [Sem(ctx, nc, "TT0"), Sem(ctx, nc, "TT1")]
        kb.TS = [Sem(ctx, nc, "TS0"), Sem(ctx, nc, "TS1")]
        kb.SC = [Sem(ctx, nc, "SC0"), Sem(ctx, nc, "SC1")]
        kb.DRAINS = kb.XS + kb.LNS + kb.XL + kb.TT + kb.TS + [kb.WGH]
        kb.SDRAINS = kb.SC
        kb.icc = 0
        LD, GP, TE, VE, SE = kb.LD, kb.GP, kb.TE, kb.VE, kb.SE

        def next_cc():
            s = kb.CC[kb.icc]
            kb.icc += 1
            return s

        ut128 = sb("ut128", [128, 128], F32)
        ident = sb("ident", [128, 128], F32)
        suts = sb("suts", [32, 32], F32)
        sutc = sb("sutc", [32, 32], F32)
        sut8 = sb("sut8", [8, 8], F32)
        ones128 = sb("ones128", [1, 128], F32)
        onescol = sb("onescol", [128, 1], F32)
        indAr = sb("indAr", [1, 32], F32)
        indBr = sb("indBr", [1, 32], F32)
        indAc = sb("indAc", [32, 1], F32)
        indBc = sb("indBc", [32, 1], F32)
        ones32r = sb("ones32r", [1, 32], F32)
        ones8c = sb("ones8c", [8, 1], F32)
        ones8r = sb("ones8r", [1, 8], F32)
        ltc = sb("ltc", [8, 1], F32)
        myhot = sb("myhot", [8, 1], F32)
        ebase = sb("ebase", [1, 8], F32)
        onesrowb = sb("onesrowb", [1, 512], BF16)
        wghi = sb("wghi", [128, FC, E], BF16)
        wglo = sb("wglo", [128, FC, E], BF16)
        gkvc = sb("gkvc", [128, 2], F32)
        trimaskb = sb("trimaskb", [128, 128], F32)
        ztile = sb("ztile", [128, D], BF16)
        qmaskc = sb("qmaskc", [128, 1], F32)
        idx0i = sb("idx0i", [128, 32], I32)
        idx1i = sb("idx1i", [128, 32], I32)
        cw0 = sb("cw0", [128, 32], F32)
        cw1 = sb("cw1", [128, 32], F32)
        lgp = sb("lgp", [128, 32, E], F32)

        def phase_init():
            with ExitStack() as pc:
                def psb(n, s, d):
                    return pc.enter_context(nc.sbuf_tensor(n, s, d))

                ph = Phase(nc, kb)
                wgf = psb("wgf", [128, FC, E], F32)
                gkvr = psb("gkvr", [2, 128], F32)
                gps = pc.enter_context(nc.psum_tensor("gps", [128, 2], F32))
                consts = [(ut128, cut128), (ident, cident),
                          (suts, csuts), (sutc, csutc),
                          (sut8, csut8), (ones128, cones128),
                          (onescol, conescol), (indAr, cindAr),
                          (indBr, cindBr), (indAc, cindAc), (indBc, cindBc),
                          (ones32r, cones32r), (ones8c, cones8c),
                          (ones8r, cones8r), (ltc, cltc), (myhot, cmyhot),
                          (ebase, cebase), (onesrowb, conesrowb)]
                for dst, src in consts:
                    ph.emit("sync", lambda e, d=dst, s=src:
                            e.dma_start(d[:], s), inc=LD, inck=16)
                ph.emit("sync", lambda e: e.dma_start(
                    wgf[:], wg.rearrange("(f p) e -> p f e", p=128)),
                    inc=LD, inck=16)
                ph.emit("sync", lambda e: e.dma_start(
                    trimaskb[:], ctrimask), inc=LD, inck=16)
                ph.emit("sync", lambda e: e.dma_start(
                    qmaskc[:], cqmask), inc=LD, inck=16)
                ph.emit("sync", lambda e: e.dma_start(
                    gkvr[:], gkv.rearrange("(c p) -> c p", p=128)),
                    inc=LD, inck=16)
                ldA = LD.n
                ph.emit("gpsimd", lambda e: e.dma_start(
                    wghi[:], wg.rearrange("(f p) e -> p f e", p=128)),
                    inc=kb.WGH, inck=16)
                wgA = kb.WGH.n

                ph.emit("gpsimd", lambda e: e.memset(ztile[:], 0.0),
                        inc=kb.PLS)
                ph.emit("gpsimd", lambda e: e.nop(),
                        waits=[(kb.PLS, kb.PLS.n)])
                sv = stage.rearrange("(g p) d -> g p d", p=128)
                for g in range(E * CAP // 128):
                    ph.emit("gpsimd", lambda e, g=g:
                            e.dma_start(sv[g], ztile[:]), inc=GP, inck=16)
                ph.emit("vector", lambda e: e.tensor_tensor(
                    wglo[:], wgf[:], wghi[:], ALU.subtract),
                    waits=[(LD, ldA), (kb.WGH, wgA)], inc=VE)
                ph.emit("tensor", lambda e: e.transpose(
                    gps[:], gkvr[:], ident[0:2, 0:2]),
                    waits=[(LD, ldA)], inc=TE)
                ph.emit("scalar", lambda e: e.copy(gkvc[:], gps[:]),
                        waits=[(TE, TE.n)], inc=SE)
                ph.run()

        def phase_boundary(tag, src_x, dst_x, write_ln, gather=False,
                           attn=False, gscale=0.5):
            with ExitStack() as pc:
                def psb(n, s, d):
                    return pc.enter_context(nc.sbuf_tensor(n, s, d))

                ph = Phase(nc, kb)
                xt = [psb(f"x{tag}{i}", [128, D], F32) for i in range(2)]
                ct = [psb(f"c{tag}{i}", [128, D], BF16) for i in range(2)] \
                    if attn else None
                lnf = psb(f"lf{tag}", [128, D], F32)
                lhi = [psb(f"lh{tag}{i}", [128, D], BF16) for i in range(2)]
                llo = [psb(f"ll{tag}{i}", [128, D], BF16) for i in range(2)]
                mean = psb(f"mn{tag}", [128, 1], F32)
                ssq = psb(f"sq{tag}", [128, 1], F32)
                rstd = psb(f"rs{tag}", [128, 1], F32)
                sqs = psb(f"sqs{tag}", [128, D], F32)
                if gather:
                    g0t = psb(f"g0{tag}", [128, D], F32)
                    g1t = psb(f"g1{tag}", [128, D], F32)
                xv = src_x.rearrange("(n p) d -> n p d", p=128)
                dv = dst_x.rearrange("(n p) d -> n p d", p=128) \
                    if dst_x is not None else None
                lv = lntm.rearrange("(n p) d -> n p d", p=128)
                lov = lnlo.rearrange("(n p) d -> n p d", p=128)

                nve = 4 if write_ln else 1
                ve0, se0, gp0 = VE.n, SE.n, GP.n
                xl_loads, xs_storex, lns_val = [], [], []

                for i in range(NT):
                    w = []
                    if i >= 2:
                        w.append((VE, ve0 + nve * (i - 1)))
                        if dst_x is not None:
                            w.append((kb.XS[i % 2], xs_storex[i - 2]))
                    ph.emit("sync", lambda e, i=i: e.dma_start(
                        xt[i % 2][:], xv[i]), waits=w,
                        inc=kb.XL[i % 2], inck=16)
                    if attn:
                        ph.emit("sync", lambda e, i=i: e.dma_start(
                            ct[i % 2][:], contribfm[:, i * 128:(i + 1) * 128],
                            transpose=True), inc=kb.XL[i % 2], inck=16)
                    xl_loads.append(kb.XL[i % 2].n)
                    if dst_x is not None:
                        ph.emit("sync", lambda e, i=i: e.dma_start(
                            dv[i], xt[i % 2][:]),
                            waits=[(VE, ve0 + nve * i + 1)],
                            inc=kb.XS[i % 2], inck=16)
                    xs_storex.append(kb.XS[i % 2].n)
                    if write_ln:
                        ph.emit("sync", lambda e, i=i: e.dma_start(
                            lv[i], lhi[i % 2][:]),
                            waits=[(VE, ve0 + nve * i + 3)],
                            inc=kb.LNS[i % 2], inck=16)
                        ph.emit("sync", lambda e, i=i: e.dma_start(
                            lov[i], llo[i % 2][:]),
                            waits=[(VE, ve0 + nve * i + 4)],
                            inc=kb.LNS[i % 2], inck=16)
                    lns_val.append(kb.LNS[i % 2].n)

                if gather:
                    bcr = {}
                    ph.emit("gpsimd", lambda e: bcr.__setitem__(
                        "r", e.to_reg(E * CAP - 1)))
                    ph.emit("gpsimd", lambda e: e.memset(g0t[:], 0.0))
                    ph.emit("gpsimd", lambda e: e.memset(g1t[:], 0.0),
                            inc=kb.PLS)
                    ph.emit("gpsimd", lambda e: e.nop(),
                            waits=[(kb.PLS, kb.PLS.n)])
                    for i in range(NT):
                        w = []
                        if i >= 1:
                            w.append((VE, ve0 + nve * (i - 1) + 1))
                        ph.emit("gpsimd", lambda e, i=i: e.indirect_dma_start(
                            g0t[:], None, yall, IOA(idx0i[:, i:i + 1], 0),
                            bounds_check=bcr["r"], oob_is_err=False),
                            waits=w, inc=GP, inck=16)
                        ph.emit("gpsimd", lambda e, i=i: e.indirect_dma_start(
                            g1t[:], None, yall, IOA(idx1i[:, i:i + 1], 0),
                            bounds_check=bcr["r"], oob_is_err=False),
                            inc=GP, inck=16)

                for i in range(NT):
                    p = i % 2
                    w = [(kb.XL[i % 2], xl_loads[i])]
                    if gather:
                        w.append((GP, gp0 + 32 * (i + 1)))
                    if write_ln and i >= 2:
                        w.append((kb.LNS[i % 2], lns_val[i - 2]))
                    if gather:
                        ph.emit("vector", lambda e, i=i: e.tensor_scalar(
                            g1t[:], g1t[:], cw1[:, i:i + 1], None, ALU.mult),
                            waits=w)
                        ph.emit("vector", lambda e, i=i:
                                e.scalar_tensor_tensor(
                                    g0t[:], g0t[:], cw0[:, i:i + 1], g1t[:],
                                    ALU.mult, ALU.add))
                        ph.emit("vector", lambda e, p=p:
                                e.scalar_tensor_tensor(
                                    xt[p][:], g0t[:], gscale, xt[p][:],
                                    ALU.mult, ALU.add), inc=VE)
                    elif attn:
                        ph.emit("vector", lambda e, p=p: e.tensor_tensor(
                            xt[p][:], xt[p][:], ct[p][:], ALU.add),
                            waits=w, inc=VE)
                    else:
                        ph.emit("vector", lambda e: e.nop(), waits=w, inc=VE)
                    if write_ln:
                        ph.emit("vector", lambda e, p=p: e.reduce_sum(
                            mean[:].unsqueeze(1), xt[p][:].unsqueeze(1),
                            axis=AX.X))
                        ph.emit("vector", lambda e: e.tensor_scalar_mul(
                            mean[:], mean[:], 1.0 / D))
                        ph.emit("vector", lambda e: e.tensor_tensor(
                            rstd[:], mean[:], mean[:], ALU.mult),
                            waits=[(SE, se0 + 2 * i + 1)])
                        ph.emit("vector", lambda e: e.scalar_tensor_tensor(
                            rstd[:], ssq[:], 1.0 / D, rstd[:],
                            ALU.mult, ALU.subtract))
                        ph.emit("vector", lambda e: e.tensor_scalar_add(
                            rstd[:], rstd[:], 1e-5), inc=VE)
                        ph.emit("vector", lambda e, p=p: e.tensor_scalar(
                            lnf[:], xt[p][:], mean[:, 0:1], None,
                            ALU.subtract), waits=[(SE, se0 + 2 * i + 2)])
                        ph.emit("vector", lambda e: e.tensor_scalar(
                            lnf[:], lnf[:], rstd[:, 0:1], None, ALU.mult))
                        ph.emit("vector", lambda e, p=p: e.tensor_copy(
                            lhi[p][:], lnf[:]), inc=VE)
                        ph.emit("vector", lambda e, p=p: e.tensor_tensor(
                            llo[p][:], lnf[:], lhi[p][:], ALU.subtract),
                            inc=VE)

                if write_ln:
                    for i in range(NT):
                        p = i % 2
                        ph.emit("scalar", lambda e, p=p: e.activation(
                            sqs[:], xt[p][:], AF.Square, accum_out=ssq[:]),
                            waits=[(VE, ve0 + nve * i + 1)], inc=SE)
                        ph.emit("scalar", lambda e: e.activation(
                            rstd[:], rstd[:], AF.Ln),
                            waits=[(VE, ve0 + nve * i + 2)])
                        ph.emit("scalar", lambda e: e.activation(
                            rstd[:], rstd[:], AF.Exp, scale=-0.5), inc=SE)
                ph.run()

        def phase_gating():
            with ExitStack() as pc:
                def psb(n, s, d):
                    return pc.enter_context(nc.sbuf_tensor(n, s, d))

                ph = Phase(nc, kb)
                lfh = psb("glfh", [128, FC, TOK], BF16)
                lfl = psb("glfl", [128, FC, TOK], BF16)
                plg = [pc.enter_context(nc.psum_tensor(f"gpl{i}", [128, E],
                                                       F32))
                       for i in range(2)]
                te0, ve0 = TE.n, VE.n
                for f in range(FC):
                    ph.emit("sync", lambda e, f=f: e.dma_start(
                        lfh[:, f, :], lntm[:, f * 128:(f + 1) * 128],
                        transpose=True), inc=LD, inck=16)
                    ph.emit("sync", lambda e, f=f: e.dma_start(
                        lfl[:, f, :], lnlo[:, f * 128:(f + 1) * 128],
                        transpose=True), inc=LD, inck=16)
                ldall = LD.n

                def mmtile(e, c, p):
                    sl = slice(c * 128, (c + 1) * 128)
                    for f in range(FC):
                        e.matmul(p[:], lfh[:, f, sl], wghi[:, f, :],
                                 start=(f == 0), stop=False)
                        e.matmul(p[:], lfh[:, f, sl], wglo[:, f, :],
                                 start=False, stop=False)
                        mm = e.matmul(p[:], lfl[:, f, sl], wghi[:, f, :],
                                      start=False, stop=(f == FC - 1))
                    return mm

                for c in range(NT):
                    w = [(LD, ldall)] if c == 0 else []
                    if c >= 2:
                        w.append((VE, ve0 + (c - 1)))
                    ph.emit("tensor", lambda e, c=c: mmtile(e, c, plg[c % 2]),
                            waits=w, inc=TE)
                    ph.emit("vector", lambda e, c=c: e.tensor_copy(
                        lgp[:, c, :], plg[c % 2][:]),
                        waits=[(TE, te0 + c + 1)], inc=VE)
                ph.run()

        def phase_routing(debug=False):
            with ExitStack() as pc:
                def psb(n, s, d):
                    return pc.enter_context(nc.sbuf_tensor(n, s, d))

                def pps(n, s):
                    return pc.enter_context(nc.psum_tensor(n, s, F32))

                ph = Phase(nc, kb)
                S = ph.seq
                m1 = psb("rm1", [128, 32], F32)
                sml = psb("rsml", [128, 32, E], F32)
                sp = psb("rsp", [128, 32, E], F32)
                Z = psb("rZ", [128, 32], F32)
                eq = psb("req", [128, 32, E], F32)
                oh1 = psb("roh1", [128, 32, E], F32)
                oh2 = psb("roh2", [128, 32, E], F32)
                fnd = psb("rfnd", [128, 32], F32)
                gg = psb("rgg", [128, 32], F32)
                e2 = psb("re2", [128, 32], F32)
                den = psb("rden", [128, 32], F32)
                sUT = psb("rsUT", [128, 32, E], F32)
                spre = psb("rspre", [128, 32, E], F32)
                kept0 = psb("rkept0", [128, 32, E], F32)
                kept1 = psb("rkept1", [128, 32, E], F32)
                srow = psb("rsrow", [1, 256], F32)
                sTc = psb("rsTc", [32, 8], F32)
                sTc2 = psb("rsTc2", [32, 8], F32)
                sTk = psb("rsTk", [32, 8], F32)
                spay = psb("rspay", [1, 32], F32)
                stab = psb("rstab", [8, 32], F32)
                sP1A = psb("rsP1A", [8, 8], F32)
                sP1B = psb("rsP1B", [8, 8], F32)
                sP2A = psb("rsP2A", [8, 8], F32)
                sP2B = psb("rsP2B", [8, 8], F32)
                stot = psb("rstot", [1, 24], F32)
                counts = psb("rcounts", [1, 8], F32)
                counts8 = psb("rcounts8", [8, 8], F32)
                kbrow = psb("rkbrow", [1, 8], F32)
                myk0 = psb("rmyk0", [1, 8], F32)
                myrows = psb("rmyrows", [1, 32], F32)
                k88 = psb("rk88", [8, 8], F32)
                ktot = psb("rktot", [8, 8], F32)
                sco = psb("rsco", [32, 8], F32)
                erow = psb("rerow", [1, 8], F32)
                any_ = psb("rany", [128, 32], F32)
                sa = psb("rsa", [128, 32], F32)
                idxf = psb("ridxf", [128, 32], F32)
                pBIG = pps("rpBIG", [128, 256])
                pUT = pps("rpUT", [128, 256])
                pROW = pps("rpROW", [1, 256])
                pSM = pps("rpSM", [32, 32])
                cc = next_cc()

                m1b = m1[:].unsqueeze(2).broadcast_to([128, 32, E])
                e2b = e2[:].unsqueeze(2).broadcast_to([128, 32, E])
                oh1f = oh1[:].rearrange("p a b -> p (a b)")
                oh2f = oh2[:].rearrange("p a b -> p (a b)")

                S("vector", lambda e: e.reduce_max(
                    m1[:].rearrange("p (c o) -> p c o", o=1), lgp[:],
                    axis=AX.X))
                S("vector", lambda e: e.tensor_tensor(
                    sml[:], lgp[:], m1b, ALU.subtract))
                S("scalar", lambda e: e.activation(
                    sp[:].rearrange("p a b -> p (a b)"),
                    sml[:].rearrange("p a b -> p (a b)"), AF.Exp))
                S("vector", lambda e: e.reduce_sum(
                    Z[:].rearrange("p (c o) -> p c o", o=1), sp[:],
                    axis=AX.X))
                S("vector", lambda e: e.tensor_tensor(
                    eq[:], lgp[:], m1b, ALU.is_equal))
                S("vector", lambda e: e.memset(fnd[:], 0.0))
                for ee in range(E):
                    S("vector", lambda e: e.tensor_scalar(
                        gg[:], fnd[:], -1.0, 1.0, ALU.mult, ALU.add))
                    S("vector", lambda e, ee=ee: e.tensor_tensor(
                        oh1[:, :, ee], eq[:, :, ee], gg[:], ALU.mult))
                    S("vector", lambda e, ee=ee: e.tensor_tensor(
                        fnd[:], fnd[:], oh1[:, :, ee], ALU.add))
                S("vector", lambda e: e.scalar_tensor_tensor(
                    sml[:], oh1[:], -2.0, sp[:], ALU.mult, ALU.add))
                S("vector", lambda e: e.reduce_max(
                    e2[:].rearrange("p (c o) -> p c o", o=1), sml[:],
                    axis=AX.X))
                S("vector", lambda e: e.tensor_tensor(
                    eq[:], sml[:], e2b, ALU.is_equal))
                S("vector", lambda e: e.memset(fnd[:], 0.0))
                for ee in range(E):
                    S("vector", lambda e: e.tensor_scalar(
                        gg[:], fnd[:], -1.0, 1.0, ALU.mult, ALU.add))
                    S("vector", lambda e, ee=ee: e.tensor_tensor(
                        oh2[:, :, ee], eq[:, :, ee], gg[:], ALU.mult))
                    S("vector", lambda e, ee=ee: e.tensor_tensor(
                        fnd[:], fnd[:], oh2[:, :, ee], ALU.add))
                S("vector", lambda e: e.tensor_scalar(
                    den[:], Z[:], 1e-9, 1.0, ALU.mult, ALU.add))
                S("vector", lambda e: e.tensor_tensor(
                    den[:], den[:], e2[:], ALU.add))
                S("vector", lambda e: e.reciprocal(cw0[:], den[:]))
                S("vector", lambda e: e.tensor_tensor(
                    cw1[:], e2[:], cw0[:], ALU.mult))

                S("tensor", lambda e: e.matmul(pROW[:], onescol[:], oh1f,
                                               start=True, stop=True))
                S("vector", lambda e: e.tensor_copy(srow[:], pROW[:]))
                S("sync", lambda e: e.dma_start(bcoflat, srow[:]))
                S("sync", lambda e: e.dma_start(sTc[:], bco))
                S("tensor", lambda e: e.matmul(pROW[:], onescol[:], oh2f,
                                               start=True, stop=True))
                S("vector", lambda e: e.tensor_copy(srow[:], pROW[:]))
                S("sync", lambda e: e.dma_start(bcoflat, srow[:]))
                S("sync", lambda e: e.dma_start(sTc2[:], bco))
                S("tensor", lambda e: e.matmul(pROW[0:1, 0:8], indAc[:],
                                               sTc[:], start=True, stop=True))
                S("tensor", lambda e: e.matmul(pROW[0:1, 8:16], indBc[:],
                                               sTc[:], start=True, stop=True))
                S("tensor", lambda e: e.matmul(pROW[0:1, 16:24], indAc[:],
                                               sTc2[:], start=True,
                                               stop=True))
                S("tensor", lambda e: e.matmul(pROW[0:1, 24:32], indBc[:],
                                               sTc2[:], start=True,
                                               stop=True))
                S("vector", lambda e: e.tensor_copy(spay[:],
                                                    pROW[0:1, 0:32]))
                S("sync", lambda e: e.dma_start(aghin, spay[:]))
                ph.emit("gpsimd", lambda e: e.collective_compute(
                    "AllGather", ALU.bypass, replica_groups=[list(range(8))],
                    ins=[aghin.opt()], outs=[aghout.opt()]),
                    waits=[ph.last], inc=cc)
                S("sync", lambda e: e.dma_start(stab[:], aghout))

                S("tensor", lambda e: e.matmul(pSM[0:8, 0:8], sut8[:],
                                               stab[:, 0:8], start=True,
                                               stop=True))
                S("vector", lambda e: e.tensor_copy(sP1A[:], pSM[0:8, 0:8]))
                S("tensor", lambda e: e.matmul(pSM[0:1, 0:8], ones8c[:],
                                               stab[:, 0:8], start=True,
                                               stop=True))
                S("tensor", lambda e: e.matmul(pSM[0:1, 8:16], ones8c[:],
                                               stab[:, 8:16], start=True,
                                               stop=True))
                S("tensor", lambda e: e.matmul(pSM[0:1, 16:24], ones8c[:],
                                               stab[:, 16:24], start=True,
                                               stop=True))
                S("vector", lambda e: e.tensor_copy(stot[:],
                                                    pSM[0:1, 0:24]))
                S("tensor", lambda e: e.matmul(pSM[0:8, 0:8], sut8[:],
                                               stab[:, 8:16], start=True,
                                               stop=False))
                S("tensor", lambda e: e.matmul(pSM[0:8, 0:8], ones8r[:],
                                               stot[0:1, 0:8], start=False,
                                               stop=True))
                S("vector", lambda e: e.tensor_copy(sP1B[:], pSM[0:8, 0:8]))
                S("tensor", lambda e: e.matmul(pSM[0:8, 0:8], sut8[:],
                                               stab[:, 16:24], start=True,
                                               stop=True))
                S("vector", lambda e: e.tensor_copy(sP2A[:], pSM[0:8, 0:8]))
                S("tensor", lambda e: e.matmul(pSM[0:8, 0:8], sut8[:],
                                               stab[:, 24:32], start=True,
                                               stop=False))
                S("tensor", lambda e: e.matmul(pSM[0:8, 0:8], ones8r[:],
                                               stot[0:1, 16:24], start=False,
                                               stop=True))
                S("vector", lambda e: e.tensor_copy(sP2B[:], pSM[0:8, 0:8]))
                S("vector", lambda e: e.tensor_tensor(
                    counts[:], stot[0:1, 0:8], stot[0:1, 8:16], ALU.add))
                S("vector", lambda e: e.tensor_scalar_min(
                    counts[:], counts[:], float(CAP)))
                S("tensor", lambda e: e.matmul(pSM[0:8, 8:16], ones8r[:],
                                               counts[:], start=True,
                                               stop=True))
                S("vector", lambda e: e.tensor_copy(counts8[:],
                                                    pSM[0:8, 8:16]))

                def clampk(Pall, hsl):
                    S("vector", lambda e, P=Pall: e.tensor_scalar(
                        k88[:], P[:], -1.0, float(CAP), ALU.mult, ALU.add))
                    S("vector", lambda e: e.tensor_relu(k88[:], k88[:]))
                    S("vector", lambda e, h=hsl: e.tensor_tensor(
                        k88[:], k88[:], h, ALU.min))

                def clampk2(Pall, hsl):
                    S("vector", lambda e, P=Pall: e.tensor_tensor(
                        k88[:], P[:], counts8[:], ALU.add))
                    S("vector", lambda e: e.tensor_scalar(
                        k88[:], k88[:], -1.0, float(CAP), ALU.mult, ALU.add))
                    S("vector", lambda e: e.tensor_relu(k88[:], k88[:]))
                    S("vector", lambda e, h=hsl: e.tensor_tensor(
                        k88[:], k88[:], h, ALU.min))

                clampk(sP1A, stab[:, 0:8])
                S("vector", lambda e: e.tensor_copy(ktot[:], k88[:]))
                clampk(sP1B, stab[:, 8:16])
                S("vector", lambda e: e.tensor_tensor(ktot[:], ktot[:],
                                                      k88[:], ALU.add))
                S("tensor", lambda e: e.matmul(pSM[0:1, 0:8], myhot[:],
                                               ktot[:], start=True,
                                               stop=True))
                S("vector", lambda e: e.tensor_copy(myk0[:], pSM[0:1, 0:8]))
                clampk2(sP2A, stab[:, 16:24])
                S("vector", lambda e: e.tensor_tensor(ktot[:], ktot[:],
                                                      k88[:], ALU.add))
                clampk2(sP2B, stab[:, 24:32])
                S("vector", lambda e: e.tensor_tensor(ktot[:], ktot[:],
                                                      k88[:], ALU.add))
                S("tensor", lambda e: e.matmul(pSM[0:1, 0:8], ltc[:],
                                               ktot[:], start=True,
                                               stop=True))
                S("vector", lambda e: e.tensor_copy(kbrow[:], pSM[0:1, 0:8]))
                for j, src in enumerate([sP1A, sP1B, sP2A, sP2B]):
                    S("tensor", lambda e, src=src: e.matmul(
                        pSM[0:1, 0:8], myhot[:], src[:], start=True,
                        stop=True))
                    S("vector", lambda e, j=j: e.tensor_copy(
                        myrows[0:1, j * 8:(j + 1) * 8], pSM[0:1, 0:8]))

                def keep_pass(Tcs, ohf, oh, myA, myB, with_counts, keptdst):
                    S("tensor", lambda e: e.matmul(pSM[0:32, 0:8], suts[:],
                                                   Tcs[:], start=True,
                                                   stop=False))
                    S("tensor", lambda e: e.matmul(
                        pSM[0:32, 0:8], indAr[:],
                        myrows[0:1, myA * 8:myA * 8 + 8],
                        start=False, stop=False))
                    S("tensor", lambda e: e.matmul(
                        pSM[0:32, 0:8], indBr[:],
                        myrows[0:1, myB * 8:myB * 8 + 8],
                        start=False, stop=not with_counts))
                    if with_counts:
                        S("tensor", lambda e: e.matmul(
                            pSM[0:32, 0:8], ones32r[:], counts[:],
                            start=False, stop=True))
                    S("vector", lambda e: e.tensor_copy(sco[:],
                                                        pSM[0:32, 0:8]))
                    S("sync", lambda e: e.dma_start(bco, sco[:]))
                    S("sync", lambda e: e.dma_start(srow[:], bcoflat))
                    S("tensor", lambda e: e.matmul(pBIG[:], ones128[:],
                                                   srow[:], start=True,
                                                   stop=True))
                    S("tensor", lambda e, ohf=ohf: e.matmul(
                        pUT[:], ut128[:], ohf, start=True, stop=True))
                    S("vector", lambda e: e.tensor_copy(
                        sUT[:].rearrange("p a b -> p (a b)"), pUT[:]))
                    S("vector", lambda e: e.tensor_tensor(
                        spre[:].rearrange("p a b -> p (a b)"),
                        sUT[:].rearrange("p a b -> p (a b)"), pBIG[:],
                        ALU.add))
                    S("vector", lambda e: e.tensor_single_scalar(
                        spre[:], spre[:], CAP + 0.4, ALU.is_le))
                    S("vector", lambda e, oh=oh, kd=keptdst: e.tensor_tensor(
                        kd[:], spre[:], oh[:], ALU.mult))

                keep_pass(sTc, oh1f, oh1, 0, 1, False, kept0)
                keep_pass(sTc2, oh2f, oh2, 2, 3, True, kept1)

                def rank_pass(kept, extra_row, idxi, cwx, dump=False):
                    kf = kept[:].rearrange("p a b -> p (a b)")
                    S("tensor", lambda e, kf=kf: e.matmul(
                        pROW[:], onescol[:], kf, start=True, stop=True))
                    S("vector", lambda e: e.tensor_copy(srow[:], pROW[:]))
                    S("sync", lambda e: e.dma_start(bcoflat, srow[:]))
                    S("sync", lambda e: e.dma_start(sTk[:], bco))
                    S("vector", lambda e: e.tensor_tensor(
                        erow[:], ebase[:], kbrow[:], ALU.add))
                    if extra_row is not None:
                        S("vector", lambda e: e.tensor_tensor(
                            erow[:], erow[:], extra_row[:], ALU.add))
                    S("vector", lambda e: e.tensor_scalar_add(
                        erow[:], erow[:], -1.0))
                    S("tensor", lambda e: e.matmul(pSM[0:32, 0:8], sutc[:],
                                                   sTk[:], start=True,
                                                   stop=False))
                    S("tensor", lambda e: e.matmul(pSM[0:32, 0:8],
                                                   ones32r[:], erow[:],
                                                   start=False, stop=True))
                    S("vector", lambda e: e.tensor_copy(sco[:],
                                                        pSM[0:32, 0:8]))
                    if dump and DEBUG:
                        S("sync", lambda e: e.dma_start(dbgmap["dsco"],
                                                        sco[:]))
                        S("sync", lambda e: e.dma_start(dbgmap["dstk"],
                                                        sTk[:]))
                    S("sync", lambda e: e.dma_start(bco, sco[:]))
                    S("sync", lambda e: e.dma_start(srow[:], bcoflat))
                    S("tensor", lambda e: e.matmul(pBIG[:], ones128[:],
                                                   srow[:], start=True,
                                                   stop=True))
                    S("tensor", lambda e, kf=kf: e.matmul(
                        pUT[:], ut128[:], kf, start=True, stop=True))
                    S("vector", lambda e: e.tensor_copy(
                        sUT[:].rearrange("p a b -> p (a b)"), pUT[:]))
                    S("vector", lambda e: e.tensor_tensor(
                        spre[:].rearrange("p a b -> p (a b)"),
                        sUT[:].rearrange("p a b -> p (a b)"), pBIG[:],
                        ALU.add))
                    S("vector", lambda e, kept=kept: e.tensor_tensor(
                        spre[:], spre[:], kept[:], ALU.mult))
                    S("vector", lambda e: e.reduce_sum(
                        idxf[:].rearrange("p (c o) -> p c o", o=1), spre[:],
                        axis=AX.X))
                    S("vector", lambda e, kept=kept: e.reduce_sum(
                        any_[:].rearrange("p (c o) -> p c o", o=1), kept[:],
                        axis=AX.X))
                    S("vector", lambda e: e.tensor_scalar(
                        sa[:], any_[:], -SENT, SENT, ALU.mult, ALU.add))
                    S("vector", lambda e: e.tensor_tensor(
                        idxf[:], idxf[:], sa[:], ALU.add))
                    S("vector", lambda e, ii=idxi: e.tensor_copy(ii[:],
                                                                 idxf[:]))
                    S("vector", lambda e, cwx=cwx: e.tensor_tensor(
                        cwx[:], cwx[:], any_[:], ALU.mult))

                rank_pass(kept0, None, idx0i, cw0, dump=True)
                rank_pass(kept1, myk0, idx1i, cw1)

                if debug:
                    S("sync", lambda e: e.dma_start(dbgmap["dtab"], stab[:]))
                    S("vector", lambda e: e.tensor_copy(
                        spay[0:1, 0:8], counts[:]))
                    S("vector", lambda e: e.tensor_copy(
                        spay[0:1, 8:16], kbrow[:]))
                    S("vector", lambda e: e.tensor_copy(
                        spay[0:1, 16:24], myk0[:]))
                    S("sync", lambda e: e.dma_start(
                        dbgmap["drows"][0:1, 0:32], spay[:]))
                    S("sync", lambda e: e.dma_start(
                        dbgmap["drows"][0:1, 32:64], myrows[:]))
                    S("sync", lambda e: e.dma_start(dbgmap["dcw0"], cw0[:]))
                    S("sync", lambda e: e.dma_start(dbgmap["dcw1"], cw1[:]))
                    S("vector", lambda e: e.tensor_copy(sa[:], idx0i[:]))
                    S("sync", lambda e: e.dma_start(dbgmap["didx0"], sa[:]))
                    S("vector", lambda e: e.tensor_copy(sa[:], idx1i[:]))
                    S("sync", lambda e: e.dma_start(dbgmap["didx1"], sa[:]))
                    S("sync", lambda e: e.dma_start(
                        dbgmap["dlg"], lgp[:].rearrange("p a b -> p (a b)")))
                ph.run()

        def phase_scatter():
            with ExitStack() as pc:
                ph = Phase(nc, kb)
                lt = [pc.enter_context(nc.sbuf_tensor(f"sct{i}", [128, D],
                                                      BF16))
                      for i in range(2)]
                lv = lntm.rearrange("(n p) d -> n p d", p=128)
                xl_ld = []
                gp_done = []
                bcr = {}
                ph.emit("gpsimd", lambda e: bcr.__setitem__(
                    "r", e.to_reg(E * CAP - 1)))
                for i in range(NT):
                    w = []
                    if i >= 2:
                        w.append((kb.SC[i % 2], gp_done[i - 2]))
                    ph.emit("sync", lambda e, i=i: e.dma_start(
                        lt[i % 2][:], lv[i]), waits=w,
                        inc=kb.XL[i % 2], inck=16)
                    xl_ld.append(kb.XL[i % 2].n)
                    ph.emit("gpsimd", lambda e, i=i: e.indirect_dma_start(
                        stage, IOA(idx0i[:, i:i + 1], 0), lt[i % 2][:], None,
                        bounds_check=bcr["r"], oob_is_err=False),
                        waits=[(kb.XL[i % 2], xl_ld[i])],
                        inc=kb.SC[i % 2], inck=16)
                    ph.emit("gpsimd", lambda e, i=i: e.indirect_dma_start(
                        stage, IOA(idx1i[:, i:i + 1], 0), lt[i % 2][:], None,
                        bounds_check=bcr["r"], oob_is_err=False),
                        inc=kb.SC[i % 2], inck=16)
                    gp_done.append(kb.SC[i % 2].n)
                ph.run()

        def phase_expert():
            with ExitStack() as pc:
                def psb(n, s, d):
                    return pc.enter_context(nc.sbuf_tensor(n, s, d))

                def pps(n, s):
                    return pc.enter_context(nc.psum_tensor(n, s, F32))

                ph = Phase(nc, kb)
                b1r = psb("b1r", [1, H], BF16)
                b2rb = psb("b2rb", [1, D], BF16)
                w1sb = psb("w1sb", [128, FC, HCN, 128], BF16)
                w2sb = psb("w2sb", [128, HCN, DCN, 128], BF16)
                sfm = [psb(f"sfm{i}", [128, FC, 512], BF16) for i in range(2)]
                hsb = psb("hsb", [128, HCN, 512], BF16)
                u2 = psb("u2", [128, 512], F32)
                t2 = psb("t2", [128, 512], F32)
                sg = psb("sg", [128, 512], F32)
                yt = [psb(f"yt{i}", [128, 512], BF16) for i in range(2)]
                ytt = [psb(f"ytt{i}", [128, D], BF16) for i in range(2)]
                pH = [pps("pH0", [128, 512]), pps("pH1", [128, 512])]
                pW = [pps("pW0", [128, 512]), pps("pW1", [128, 512])]
                ccr = next_cc()
                ccy = next_cc()

                ph.emit("gpsimd", lambda e: e.dma_start(
                    b1r[:], b1.unsqueeze(0)), inc=GP, inck=16)
                ph.emit("gpsimd", lambda e: e.dma_start(
                    b2rb[:], b2.unsqueeze(0)), inc=GP, inck=16)
                w1v = w1.rearrange("(f p) (hc j) -> p f hc j", p=128, j=128)
                for f in range(FC):
                    ph.emit("gpsimd", lambda e, f=f: e.dma_start(
                        w1sb[:, f, :, :], w1v[:, f, :, :]), inc=GP, inck=16)
                w2v = w2.rearrange("(hc p) (dc j) -> p hc dc j", p=128, j=128)
                for q in range(4):
                    ph.emit("gpsimd", lambda e, q=q: e.dma_start(
                        w2sb[:, q * 8:(q + 1) * 8, :, :],
                        w2v[:, q * 8:(q + 1) * 8, :, :]), inc=GP, inck=16)
                gpw = GP.n
                ph.emit("gpsimd", lambda e: e.collective_compute(
                    "ReduceScatter", ALU.add, replica_groups=[list(range(8))],
                    ins=[stage.opt()], outs=[rsout.opt()]), inc=ccr)

                te0, ve0, se0 = TE.n, VE.n, SE.n
                te_h_end = {}
                te_w_end = {}
                ld_tr = {}
                se_evac = {}
                yt_store = {}

                for g in range(4):
                    for f in range(FC):
                        w = []
                        if g == 0 and f == 0:
                            w.append((ccr, 1))
                        if g >= 2 and f == 0:
                            w.append((TE, te_h_end[(g - 2, HCN - 1)]))
                        ph.emit("sync", lambda e, g=g, f=f: e.dma_start(
                            sfm[g % 2][:, f, :],
                            rsout[g * 512:(g + 1) * 512,
                                  f * 128:(f + 1) * 128],
                            transpose=True), waits=w,
                            inc=kb.XS[g % 2], inck=16)
                    ld_tr[g] = kb.XS[g % 2].n

                    for hc in range(HCN):
                        k = g * HCN + hc
                        w = [(kb.XS[g % 2], ld_tr[g])]
                        if k == 0:
                            w.append((GP, gpw))
                        if k >= 2:
                            w.append((VE, ve0 + 3 * (k - 1)))
                        if g >= 1 and hc == 0:
                            pass

                        def hmms(e, g=g, hc=hc):
                            p = pH[hc % 2]
                            e.matmul(p[:],
                                     b1r[0:1, hc * 128:(hc + 1) * 128],
                                     onesrowb[:], start=True, stop=False)
                            for f in range(FC):
                                mm = e.matmul(p[:], w1sb[:, f, hc, :],
                                              sfm[g % 2][:, f, :],
                                              start=False,
                                              stop=(f == FC - 1))
                            return mm
                        ph.emit("tensor", hmms, waits=w, inc=TE)
                        te_h_end[(g, hc)] = TE.n
                        ph.emit("scalar", lambda e, hc=hc: e.activation(
                            u2[:], pH[hc % 2][:], AF.Square),
                            waits=[(TE, TE.n)] +
                            ([(VE, ve0 + 3 * (k - 1) + 1)] if k >= 1 else []),
                            inc=SE)
                        wv = [(SE, SE.n)]
                        if g >= 1 and hc == 0:
                            wv.append((TE, te_w_end[g - 1]))
                        ph.emit("vector", lambda e: e.tensor_scalar(
                            u2[:], u2[:], GELU_A, 1.0, ALU.mult, ALU.add),
                            waits=wv)
                        ph.emit("vector", lambda e, hc=hc: e.tensor_tensor(
                            t2[:], u2[:], pH[hc % 2][:], ALU.mult), inc=VE)
                        ph.emit("scalar", lambda e: e.activation(
                            sg[:], t2[:], AF.Sigmoid, scale=GELU_2C),
                            waits=[(VE, VE.n)], inc=SE)
                        ph.emit("vector", lambda e, hc=hc: e.tensor_tensor(
                            t2[:], sg[:], pH[hc % 2][:], ALU.mult),
                            waits=[(SE, SE.n)], inc=VE)
                        ph.emit("vector", lambda e, hc=hc: e.tensor_copy(
                            hsb[:, hc, :], t2[:]), inc=VE)
                    ve_g_end = VE.n

                    for dc in range(DCN):
                        w = [(VE, ve_g_end)]
                        if (g, dc) in se_evac:
                            pass
                        prev = (g, dc - 2) if dc >= 2 else \
                            ((g - 1, dc + DCN - 2) if g >= 1 else None)
                        if prev is not None and prev in se_evac:
                            w.append((SE, se_evac[prev]))

                        def wmms(e, g=g, dc=dc):
                            p = pW[dc % 2]
                            e.matmul(p[:],
                                     b2rb[0:1, dc * 128:(dc + 1) * 128],
                                     onesrowb[:], start=True, stop=False)
                            for hc in range(HCN):
                                mm = e.matmul(p[:], w2sb[:, hc, dc, :],
                                              hsb[:, hc, :],
                                              start=False,
                                              stop=(hc == HCN - 1))
                            return mm
                        ph.emit("tensor", wmms, waits=w, inc=TE)
                        te_w_end[g] = TE.n
                        wv = [(TE, TE.n)]
                        if (g * DCN + dc) >= 2:
                            pk = (g, dc - 2) if dc >= 2 else \
                                (g - 1, dc + DCN - 2)
                            wv.append((kb.LNS[dc % 2], yt_store[pk]))
                        ph.emit("scalar", lambda e, dc=dc: e.copy(
                            yt[dc % 2][:], pW[dc % 2][:]), waits=wv, inc=SE)
                        se_evac[(g, dc)] = SE.n
                        ph.emit("sync", lambda e, g=g, dc=dc: e.dma_start(
                            yfm[dc * 128:(dc + 1) * 128,
                                g * 512:(g + 1) * 512],
                            yt[dc % 2][:]), waits=[(SE, SE.n)],
                            inc=kb.LNS[dc % 2], inck=16)
                        yt_store[(g, dc)] = kb.LNS[dc % 2].n

                lns_all = [(kb.LNS[0], kb.LNS[0].n), (kb.LNS[1], kb.LNS[1].n)]
                tr_val, st_val = {}, {}
                for kt in range(16):
                    w = list(lns_all)
                    if kt >= 2:
                        w.append((kb.TS[kt % 2], st_val[kt - 2]))
                    ph.emit("sync", lambda e, kt=kt: e.dma_start(
                        ytt[kt % 2][:], yfm[:, kt * 128:(kt + 1) * 128],
                        transpose=True), waits=w, inc=kb.TT[kt % 2], inck=16)
                    tr_val[kt] = kb.TT[kt % 2].n
                    ph.emit("sync", lambda e, kt=kt: e.dma_start(
                        yagin[kt * 128:(kt + 1) * 128, :], ytt[kt % 2][:]),
                        waits=[(kb.TT[kt % 2], tr_val[kt])],
                        inc=kb.TS[kt % 2], inck=16)
                    st_val[kt] = kb.TS[kt % 2].n
                ts_all = [(kb.TS[0], kb.TS[0].n), (kb.TS[1], kb.TS[1].n)]
                if DEBUG:
                    ph.emit("sync", lambda e: e.dma_start(
                        dbgmap["drs"], rsout), waits=[(ccr, 1)],
                        inc=LD, inck=16)
                    ph.emit("sync", lambda e: e.dma_start(
                        dbgmap["dyag"], yagin), waits=list(ts_all),
                        inc=LD, inck=16)
                ph.emit("gpsimd", lambda e: e.collective_compute(
                    "AllGather", ALU.bypass, replica_groups=[list(range(8))],
                    ins=[yagin.opt()], outs=[yall.opt()]),
                    waits=list(ts_all), inc=ccy)
                ph.emit("gpsimd", lambda e: e.nop(), waits=[(ccy, 1)])
                ph.run()

        SCALE = 1.0 / float(np.sqrt(96.0))
        CUSH = 20.0

        def phase_attn():
            with ExitStack() as pco:
                def osb(n, sh, d):
                    return pco.enter_context(nc.sbuf_tensor(n, sh, d))

                def ops(n, sh):
                    return pco.enter_context(nc.psum_tensor(n, sh, F32))

                ph = Phase(nc, kb)
                b = KBx()
                b.qc = osb("aqc", [128, 4, T], BF16)
                b.kvn = osb("akvn", [128, 2, T], BF16)
                b.Kh = osb("aKh", [128, 8, T], BF16)
                b.Vh = osb("aVh", [128, 16, 8, 65], BF16)
                b.ao = osb("aao", [128, 8, T], BF16)
                b.Qh = osb("aQh", [128, T], BF16)
                b.pT = [osb(f"apT{i}", [128, 512], BF16) for i in range(2)]
                b.scr = osb("ascr", [128, 512], F32)
                b.scr2 = osb("ascr2", [128, 512], F32)
                b.scr64 = b.scr[0:64, :]
                b.qsr = osb("aqsr", [1, T], F32)
                b.ksr = osb("aksr", [1, 4], F32)
                b.ksmax = osb("aksmax", [1, 1], F32)
                b.m16 = osb("am16", [1, 16], F32)
                b.mrowb = osb("amrowb", [1, T], BF16)
                b.lrow = osb("alrow", [1, 512], F32)
                b.wqdecsb = osb("awqdec", [128, 4, D], BF16)
                b.wqropesb = osb("awqrope", [128, 4, 512], BF16)
                b.pS = [ops("apS0", [128, 512]), ops("apS1", [128, 512])]
                b.pAO = ops("apAO", [65, 512])
                b.pB = ops("apB", [128, 512])
                b.pR1 = ops("apR1", [1, 512])
                b.pR1b = ops("apR1b", [1, 512])

                ph.seq("gpsimd", lambda e: e.dma_start(
                    b.wqdecsb[:],
                    wqdec.rearrange("(f p) j -> p f j", p=128)))
                ph.seq("gpsimd", lambda e: e.dma_start(
                    b.wqropesb[:],
                    wqropep.rearrange("(f p) j -> p f j", p=128)))

                for s_ in range(2):
                    _attn_seq(ph, s_, b)
                sv = stage.rearrange("(g p) d -> g p d", p=128)
                for g in range(E * CAP // 128):
                    ph.emit("gpsimd", lambda e, g=g:
                            e.dma_start(sv[g], ztile[:]),
                            inc=kb.SC[0], inck=16)
                ph.run()

        def _attn_seq(ph, s, b):
            S = ph.seq
            tok0 = s * T

            with ExitStack() as pa:
                def asb(n, sh, d):
                    return pa.enter_context(nc.sbuf_tensor(n, sh, d))

                af = asb(f"aaf{s}", [128, FC, 512], BF16)
                wkvsb = asb(f"awkv{s}", [128, FC, KVR], BF16)
                wkropesb = asb(f"awkrope{s}", [128, FC, 512], BF16)
                wqsb = asb(f"awq{s}", [128, FC, QR], BF16)
                krraw = asb(f"akrraw{s}", [128, 4, 512], BF16)
                qrraw = asb(f"aqrraw{s}", [128, 4, 512], BF16)
                rstage = asb(f"arstage{s}", [128, 4, 512], BF16)
                cosb = asb(f"acosb{s}", [128, T], BF16)
                sinb = asb(f"asinb{s}", [128, T], BF16)
                t1 = asb(f"at1{s}", [128, 512], F32)
                t2a = asb(f"at2a{s}", [128, 512], F32)
                rstdr = asb(f"arstdr{s}", [1, 512], F32)
                pPJ = pa.enter_context(nc.psum_tensor(f"apPJ{s}",
                                                      [128, 512], F32))

                S("gpsimd", lambda e: e.dma_start(cosb[:], ccosr))
                S("gpsimd", lambda e: e.dma_start(sinb[:], csinr))
                S("gpsimd", lambda e: e.dma_start(
                    wkvsb[:], wkv.rearrange("(f p) j -> p f j", p=128)))
                S("gpsimd", lambda e: e.dma_start(
                    wkropesb[:],
                    wkropep.rearrange("(f p) j -> p f j", p=128)))
                S("gpsimd", lambda e: e.dma_start(
                    wqsb[:], wq.rearrange("(f p) j -> p f j", p=128)))
                gpA = ph.last

                def rope(c0, raw, dstA, dstB, cA, cB):
                    cs = slice(c0, c0 + 512)
                    S("vector", lambda e: e.tensor_tensor(
                        t1[:], raw[:, cA, :], cosb[:, cs], ALU.mult))
                    S("vector", lambda e: e.tensor_tensor(
                        t2a[:], raw[:, cB, :], sinb[:, cs], ALU.mult))
                    S("vector", lambda e: e.tensor_tensor(
                        t1[:], t1[:], t2a[:], ALU.subtract))
                    S("vector", lambda e: e.tensor_tensor(
                        t2a[:], raw[:, cB, :], cosb[:, cs], ALU.mult))
                    S("vector", lambda e: e.tensor_copy(dstA, t1[:]))
                    S("vector", lambda e: e.tensor_tensor(
                        t1[:], raw[:, cA, :], sinb[:, cs], ALU.mult))
                    S("vector", lambda e: e.tensor_tensor(
                        t2a[:], t2a[:], t1[:], ALU.add))
                    S("vector", lambda e: e.tensor_copy(dstB, t2a[:]))

                for tg in range(4):
                    c0 = tg * 512
                    cs = slice(c0, c0 + 512)
                    for f in range(FC):
                        S("sync", lambda e, f=f, c0=c0: e.dma_start(
                            af[:, f, :],
                            lntm[tok0 + c0:tok0 + c0 + 512,
                                 f * 128:(f + 1) * 128], transpose=True))

                    def mmgroup(wsb, oc, wid=128):
                        def g(e, oc=oc):
                            for f in range(FC):
                                mm = e.matmul(
                                    pPJ[0:wid, :],
                                    wsb[:, f, oc * wid:(oc + 1) * wid],
                                    af[:, f, :],
                                    start=(f == 0), stop=(f == FC - 1))
                            return mm
                        return g

                    for oc in range(2):
                        S("tensor", mmgroup(wkvsb, oc),
                          extra_waits=[gpA])
                        S("scalar", lambda e, oc=oc, cs=cs: e.copy(
                            b.kvn[:, oc, cs], pPJ[:]))
                    S("scalar", lambda e, cs=cs: e.activation(
                        b.scr[:], b.kvn[:, 0, cs], AF.Square))
                    S("tensor", lambda e: e.matmul(
                        b.pR1[:], onescol[:], b.scr[:], start=True,
                        stop=False))
                    S("scalar", lambda e, cs=cs: e.activation(
                        b.scr[:], b.kvn[:, 1, cs], AF.Square))
                    S("tensor", lambda e: e.matmul(
                        b.pR1[:], onescol[:], b.scr[:], start=False,
                        stop=True))
                    S("vector", lambda e: e.tensor_scalar(
                        rstdr[:], b.pR1[:], 1.0 / KVR, 1e-6, ALU.mult,
                        ALU.add))
                    S("scalar", lambda e: e.activation(rstdr[:], rstdr[:],
                                                       AF.Ln))
                    S("scalar", lambda e: e.activation(rstdr[:], rstdr[:],
                                                       AF.Exp, scale=-0.5))
                    S("tensor", lambda e: e.matmul(
                        b.pB[:], ones128[:], rstdr[:], start=True,
                        stop=True))
                    for oc in range(2):
                        S("vector", lambda e, oc=oc, cs=cs: e.tensor_tensor(
                            t1[:], b.kvn[:, oc, cs], b.pB[:], ALU.mult))
                        S("scalar", lambda e, oc=oc, cs=cs: e.activation(
                            b.kvn[:, oc, cs], t1[:], AF.Copy,
                            scale=gkvc[:, oc:oc + 1]))
                    for oc in range(4):
                        S("tensor", mmgroup(wqsb, oc))
                        S("scalar", lambda e, oc=oc, cs=cs: e.copy(
                            b.qc[:, oc, cs], pPJ[:]))
                    if DEBUG and s == 0 and tg == 0:
                        S("sync", lambda e: e.dma_start(
                            dbgmap["dqc2"],
                            b.qc[:].rearrange("p a c -> p (a c)")))
                        S("sync", lambda e: e.dma_start(
                            dbgmap["dwq"],
                            wqsb[:].rearrange("p a c -> p (a c)")))
                        S("sync", lambda e: e.dma_start(
                            dbgmap["dwkv"],
                            wkvsb[:].rearrange("p a c -> p (a c)")))
                    for oc in range(4):
                        S("tensor", mmgroup(wkropesb, oc))
                        S("scalar", lambda e, oc=oc: e.copy(
                            krraw[:, oc, :], pPJ[:]))
                    for cp in range(2):
                        rope(c0, krraw, rstage[:, cp, :],
                             rstage[:, cp + 2, :], cp, cp + 2)
                    for ch in range(4):
                        S("sync", lambda e, ch=ch, c0=c0: e.dma_start(
                            kroped[ch * 128:(ch + 1) * 128, c0:c0 + 512],
                            rstage[:, ch, :]))
                    if DEBUG and s == 0 and tg == 0:
                        S("sync", lambda e: e.dma_start(
                            dbgmap["dqcF"],
                            b.qc[:].rearrange("p a c -> p (a c)")))
                    for oc in range(4):
                        def qrmm(e, oc=oc, cs=cs):
                            for f in range(4):
                                mm = e.matmul(
                                    pPJ[:],
                                    b.wqropesb[:, f,
                                               oc * 128:(oc + 1) * 128],
                                    b.qc[:, f, cs],
                                    start=(f == 0), stop=(f == 3))
                            return mm
                        S("tensor", qrmm)
                        S("scalar", lambda e, oc=oc: e.copy(
                            qrraw[:, oc, :], pPJ[:]))
                    for cp in range(2):
                        rope(c0, qrraw, rstage[:, cp, :],
                             rstage[:, cp + 2, :], cp, cp + 2)
                    for ch in range(4):
                        S("sync", lambda e, ch=ch, c0=c0: e.dma_start(
                            qroped[ch * 128:(ch + 1) * 128, c0:c0 + 512],
                            rstage[:, ch, :]))

            if DEBUG and s == 0:
                S("sync", lambda e: e.dma_start(
                    dbgmap["dqc"], b.qc[:].rearrange("p a c -> p (a c)")))
                S("sync", lambda e: e.dma_start(dbgmap["dkrop"], kroped))
                S("sync", lambda e: e.dma_start(dbgmap["dqrop"], qroped))

            for hh in range(2):
                with ExitStack() as pb_:
                    def bsb(n, sh, d):
                        return pb_.enter_context(nc.sbuf_tensor(n, sh, d))

                    wkdecsb = bsb(f"awkdec{s}{hh}", [128, 2, 512], BF16)
                    wvdecsb = bsb(f"awvdec{s}{hh}", [128, 2, 512], BF16)
                    vttp = [bsb(f"avtt{s}{hh}{i}", [128, 512], BF16)
                            for i in range(2)]
                    pPJ2 = [pb_.enter_context(nc.psum_tensor(
                        f"apPJ2{s}{hh}{i}", [64, 512], F32))
                        for i in range(2)]

                    S("gpsimd", lambda e, hh=hh: e.dma_start(
                        wkdecsb[:],
                        wkdec.rearrange("(f p) j -> p f j", p=128)
                        [:, :, hh * 512:(hh + 1) * 512]))
                    S("gpsimd", lambda e, hh=hh: e.dma_start(
                        wvdecsb[:],
                        wvdec.rearrange("(f p) j -> p f j", p=128)
                        [:, :, hh * 512:(hh + 1) * 512]))
                    gpB = ph.last

                    S("vector", lambda e: e.memset(b.Kh[:], 0.0))
                    S("vector", lambda e: e.memset(b.Qh[:], 0.0))
                    S("vector", lambda e: e.memset(b.Vh[:], 1.0))
                    join0 = ph.last
                    se_pa = SE.n
                    for hl in range(8):
                        ph.emit("gpsimd", lambda e, hl=hl: e.dma_start(
                            b.Kh[17:18, hl, :], conesb2048),
                            waits=[join0], inc=GP, inck=16)
                    pjev = {0: None, 1: None}
                    last_vcp = None
                    vst = {0: None, 1: None}
                    pj = 0
                    vv = 0
                    for hl in range(8):
                        h = hh * 8 + hl
                        for tg in range(4):
                            c0 = tg * 512
                            bk = pj % 2
                            w = [] if pjev[bk] is None else \
                                [(SE, pjev[bk])]
                            if pj < 2:
                                w += [gpB, (SE, se_pa)]

                            def kdmm(e, hl=hl, c0=c0, bk=bk):
                                for f in range(2):
                                    mm = e.matmul(
                                        pPJ2[bk][:],
                                        wkdecsb[:, f,
                                                hl * 64:(hl + 1) * 64],
                                        b.kvn[:, f, c0:c0 + 512],
                                        start=(f == 0), stop=(f == 1))
                                return mm
                            ph.emit("tensor", kdmm, waits=w, inc=TE)
                            ph.emit("scalar",
                                    lambda e, hl=hl, c0=c0, bk=bk: e.copy(
                                        b.Kh[64:128, hl, c0:c0 + 512],
                                        pPJ2[bk][:]),
                                    waits=[(TE, TE.n)], inc=SE)
                            pjev[bk] = SE.n
                            pj += 1
                        ph.emit("gpsimd", lambda e, h=h, hl=hl: e.dma_start(
                            b.Kh[0:16, hl, :],
                            kroped[(h // 8) * 128 + (h % 8) * 16:
                                   (h // 8) * 128 + (h % 8) * 16 + 16, :]),
                            waits=[join0], inc=GP, inck=16)
                        ph.emit("gpsimd", lambda e, h=h, hl=hl: e.dma_start(
                            b.Kh[32:48, hl, :],
                            kroped[256 + (h // 8) * 128 + (h % 8) * 16:
                                   256 + (h // 8) * 128 + (h % 8) * 16 + 16,
                                   :]),
                            waits=[join0], inc=GP, inck=16)
                        for tg in range(4):
                            c0 = tg * 512
                            bk = pj % 2
                            vb = vv % 2
                            w = [] if pjev[bk] is None else \
                                [(SE, pjev[bk])]

                            def vdmm(e, hl=hl, c0=c0, bk=bk):
                                for f in range(2):
                                    mm = e.matmul(
                                        pPJ2[bk][:],
                                        wvdecsb[:, f,
                                                hl * 64:(hl + 1) * 64],
                                        b.kvn[:, f, c0:c0 + 512],
                                        start=(f == 0), stop=(f == 1))
                                return mm
                            ph.emit("tensor", vdmm, waits=w, inc=TE)
                            wv = [(TE, TE.n)]
                            if last_vcp is not None:
                                wv.append((VE, last_vcp))
                            ph.emit("scalar", lambda e, bk=bk: e.copy(
                                b.scr64, pPJ2[bk][:]), waits=wv, inc=SE)
                            pjev[bk] = SE.n
                            wc = [(SE, SE.n)]
                            if vst[vb] is not None:
                                wc.append((kb.TS[vb], vst[vb]))
                            ph.emit("vector", lambda e, vb=vb: e.tensor_copy(
                                vttp[vb][0:64, :], b.scr64),
                                waits=wc, inc=VE)
                            last_vcp = VE.n
                            ph.emit("sync",
                                    lambda e, hl=hl, c0=c0, vb=vb:
                                    e.dma_start(
                                        vfmd[hl * 64:(hl + 1) * 64,
                                             c0:c0 + 512],
                                        vttp[vb][0:64, :]),
                                    waits=[(VE, VE.n)],
                                    inc=kb.TS[vb], inck=16)
                            vst[vb] = kb.TS[vb].n
                            pj += 1
                            vv += 1
                    ts_fin = [(kb.TS[0], kb.TS[0].n),
                              (kb.TS[1], kb.TS[1].n)]
                    trv = {}
                    vhc = {0: None, 1: None}
                    for kt in range(16):
                        vb = kt % 2
                        w = list(ts_fin)
                        if vhc[vb] is not None:
                            w.append((VE, vhc[vb]))
                        ph.emit("sync", lambda e, kt=kt, vb=vb: e.dma_start(
                            vttp[vb][:],
                            vfmd[0:512, kt * 128:(kt + 1) * 128],
                            transpose=True), waits=w,
                            inc=kb.TT[vb], inck=16)
                        trv[kt] = kb.TT[vb].n
                        ph.emit("vector", lambda e, kt=kt, vb=vb:
                                e.tensor_copy(
                                    b.Vh[:, kt, :, 0:64],
                                    vttp[vb][:].rearrange(
                                        "p (h d) -> p h d", d=64)),
                                waits=[(kb.TT[vb], trv[kt])], inc=VE)
                        vhc[vb] = VE.n
                    ph.emit("gpsimd", lambda e: e.nop(),
                            waits=[(TE, TE.n), (SE, SE.n), (VE, VE.n),
                                   (GP, GP.n)] + ts_fin +
                            [(kb.TT[0], kb.TT[0].n),
                             (kb.TT[1], kb.TT[1].n)],
                            inc=kb.PLS)

                    for hl in range(8):
                        _attn_head(ph, s, hh * 8 + hl, hl, b)

            with ExitStack() as pw:
                wosb = pw.enter_context(nc.sbuf_tensor(
                    f"awosb{s}", [128, 8, 8, 128], BF16))
                cfm = [pw.enter_context(nc.sbuf_tensor(
                    f"acfm{s}{i}", [128, 512], BF16)) for i in range(2)]
                pWO = [pw.enter_context(nc.psum_tensor(
                    f"apWO{s}{i}", [128, 512], F32)) for i in range(2)]
                S("gpsimd", lambda e: e.dma_start(
                    wosb[:],
                    wo.rearrange("(pr p) (dc j) -> p pr dc j", p=128,
                                 j=128)))
                gpW = ph.last
                se_heads = SE.n
                wo_ev = {0: None, 1: None}
                wo_st = {0: None, 1: None}
                for qg in range(4):
                    for dc in range(DCN):
                        bk = dc % 2
                        w = [gpW, (SE, se_heads)] if wo_ev[bk] is None \
                            else [(SE, wo_ev[bk])]

                        def womm(e, qg=qg, dc=dc, bk=bk):
                            for pr in range(8):
                                mm = e.matmul(
                                    pWO[bk][:], wosb[:, pr, dc, :],
                                    b.ao[:, pr, qg * 512:(qg + 1) * 512],
                                    start=(pr == 0), stop=(pr == 7))
                            return mm
                        ph.emit("tensor", womm, waits=w, inc=TE)
                        wv = [(TE, TE.n)]
                        if wo_st[bk] is not None:
                            wv.append((kb.TS[bk], wo_st[bk]))
                        ph.emit("scalar", lambda e, bk=bk: e.copy(
                            cfm[bk][:], pWO[bk][:]), waits=wv, inc=SE)
                        wo_ev[bk] = SE.n
                        ph.emit("sync", lambda e, qg=qg, dc=dc, bk=bk:
                                e.dma_start(
                                    contribfm[dc * 128:(dc + 1) * 128,
                                              tok0 + qg * 512:
                                              tok0 + qg * 512 + 512],
                                    cfm[bk][:]),
                                waits=[(SE, SE.n)],
                                inc=kb.TS[bk], inck=16)
                        wo_st[bk] = kb.TS[bk].n
                ph.emit("gpsimd", lambda e: e.nop(),
                        waits=[(TE, TE.n), (SE, SE.n),
                               (kb.TS[0], kb.TS[0].n),
                               (kb.TS[1], kb.TS[1].n)],
                        inc=kb.PLS)

        def _attn_head(ph, s, h, hl, b):
            S = ph.seq
            hd0 = ph.last
            qd_ev = {0: None, 1: None}
            for tg in range(4):
                c0 = tg * 512
                bk = tg % 2
                w = [hd0] if qd_ev[bk] is None else [(SE, qd_ev[bk])]

                def qdmm(e, c0=c0, bk=bk):
                    for f in range(4):
                        mm = e.matmul(
                            b.pS[bk][0:64, :],
                            b.wqdecsb[:, f, h * 64:(h + 1) * 64],
                            b.qc[:, f, c0:c0 + 512],
                            start=(f == 0), stop=(f == 3))
                    return mm
                ph.emit("tensor", qdmm, waits=w, inc=TE)
                ph.emit("scalar", lambda e, c0=c0, bk=bk: e.copy(
                    b.Qh[64:128, c0:c0 + 512], b.pS[bk][0:64, :]),
                    waits=[(TE, TE.n)], inc=SE)
                qd_ev[bk] = SE.n
            S("gpsimd", lambda e: e.dma_start(
                b.Qh[0:16, :],
                qroped[(h // 8) * 128 + (h % 8) * 16:
                       (h // 8) * 128 + (h % 8) * 16 + 16, :]))
            S("gpsimd", lambda e: e.dma_start(
                b.Qh[32:48, :],
                qroped[256 + (h // 8) * 128 + (h % 8) * 16:
                       256 + (h // 8) * 128 + (h % 8) * 16 + 16, :]))
            stats0 = ph.last
            mmQv = mmKv = krv = None
            for tg in range(4):
                c0 = tg * 512
                wq_ = [stats0] if tg == 0 else [(TE, mmQv)]
                ph.emit("scalar", lambda e, c0=c0: e.activation(
                    b.scr[:], b.Qh[:, c0:c0 + 512], AF.Square),
                    waits=wq_, inc=SE)
                ph.emit("tensor", lambda e: e.matmul(
                    b.pR1[:], qmaskc[:], b.scr[:], start=True, stop=True),
                    waits=[(SE, SE.n)], inc=TE)
                mmQv = TE.n
                ph.emit("scalar", lambda e, c0=c0: e.copy(
                    b.qsr[0:1, c0:c0 + 512], b.pR1[:]),
                    waits=[(TE, mmQv)], inc=SE)
                wk_ = [] if mmKv is None else [(TE, mmKv)]
                ph.emit("scalar", lambda e, c0=c0, hl=hl: e.activation(
                    b.scr2[:], b.Kh[:, hl, c0:c0 + 512], AF.Square),
                    waits=wk_, inc=SE)
                ph.emit("tensor", lambda e: e.matmul(
                    b.pR1b[:], qmaskc[:], b.scr2[:], start=True, stop=True),
                    waits=[(SE, SE.n)], inc=TE)
                mmKv = TE.n
                wl_ = [(TE, mmKv)] + ([] if krv is None else [(VE, krv)])
                ph.emit("scalar", lambda e: e.copy(b.lrow[:], b.pR1b[:]),
                        waits=wl_, inc=SE)
                ph.emit("vector", lambda e, tg=tg: e.reduce_max(
                    b.ksr[0:1, tg:tg + 1].unsqueeze(2),
                    b.lrow[:].unsqueeze(1), axis=AX.X),
                    waits=[(SE, SE.n)], inc=VE)
                krv = VE.n
            S("vector", lambda e: e.reduce_max(
                b.m16[:].unsqueeze(2),
                b.qsr[:].rearrange("o (a c) -> o a c", a=16), axis=AX.X),
              extra_waits=[(SE, SE.n)])
            S("vector", lambda e: e.reduce_max(
                b.ksmax[:].unsqueeze(2), b.ksr[:].unsqueeze(1), axis=AX.X))
            S("vector", lambda e: e.tensor_scalar(
                b.m16[:], b.m16[:], b.ksmax[0:1, 0:1], None, ALU.mult))
            S("scalar", lambda e: e.activation(b.m16[:], b.m16[:], AF.Ln))
            S("scalar", lambda e: e.activation(b.m16[:], b.m16[:], AF.Exp,
                                               scale=0.5))
            S("vector", lambda e: e.tensor_scalar(
                b.m16[:], b.m16[:], 1.0, CUSH, ALU.mult, ALU.add))
            S("vector", lambda e: e.tensor_scalar(
                b.mrowb[:].rearrange("o (a c) -> o a c", c=128),
                b.m16[:].unsqueeze(2).broadcast_to([1, 16, 128]),
                -1.0, None, ALU.mult))
            S("gpsimd", lambda e: e.dma_start(b.Qh[17:18, :], b.mrowb[:]))

            pao_free = None
            ao_mult = None
            hd0 = ph.last
            for qg in range(4):
                kts = list(range(4 * qg + 3, 4 * qg - 1, -1)) + \
                    list(range(0, 4 * qg))
                n = len(kts)
                exp_done = {}
                av_done = {}

                def emit_smm(j):
                    kt = kts[j]
                    wdt = 512 if kt < 4 * qg else \
                        512 - (kt - 4 * qg) * 128
                    q0 = qg * 512 + (512 - wdt)
                    wts = []
                    if j >= 2:
                        wts.append((SE, exp_done[j - 2]))

                    def smm(e, kt=kt, wdt=wdt, q0=q0, j=j):
                        return e.matmul(
                            b.pS[j % 2][:, 512 - wdt:],
                            b.Kh[:, hl, kt * 128:(kt + 1) * 128],
                            b.Qh[:, q0:q0 + wdt], start=True, stop=True)
                    ph.emit("tensor", smm, waits=wts, inc=TE)
                    smm_val = TE.n
                    if kt >= 4 * qg:
                        ph.emit("vector", lambda e, wdt=wdt, j=j:
                                e.tensor_tensor(
                                    b.pS[j % 2][:, 512 - wdt:512 - wdt + 128],
                                    b.pS[j % 2][:, 512 - wdt:512 - wdt + 128],
                                    trimaskb[:], ALU.add),
                                waits=[(TE, smm_val)], inc=VE)
                        wexp = [(VE, VE.n)]
                    else:
                        wexp = [(TE, smm_val)]
                    if j >= 2:
                        wexp.append((TE, av_done[j - 2]))

                    def expf(e, wdt=wdt, j=j):
                        return e.activation(
                            b.pT[j % 2][:, 512 - wdt:],
                            b.pS[j % 2][:, 512 - wdt:],
                            AF.Exp, scale=SCALE)
                    ph.emit("scalar", expf, waits=wexp, inc=SE)
                    exp_done[j] = SE.n

                def emit_av(j):
                    kt = kts[j]
                    wdt = 512 if kt < 4 * qg else \
                        512 - (kt - 4 * qg) * 128

                    def avmm(e, kt=kt, wdt=wdt, j=j):
                        return e.matmul(
                            b.pAO[:, 512 - wdt:], b.Vh[:, kt, hl, :],
                            b.pT[j % 2][:, 512 - wdt:],
                            start=False, stop=False)
                    ph.emit("tensor", avmm,
                            waits=[(SE, exp_done[j])], inc=TE)
                    av_done[j] = TE.n

                ph.emit("tensor", lambda e: e.matmul(
                    b.pAO[:, :], ztile[0:1, 0:65], ztile[0:1, 0:512],
                    start=True, stop=False),
                    waits=[pao_free if pao_free is not None else hd0],
                    inc=TE)
                emit_smm(0)
                for j in range(1, n):
                    emit_smm(j)
                    emit_av(j - 1)
                emit_av(n - 1)
                ph.emit("tensor", lambda e: e.matmul(
                    b.pAO[:, :], ztile[0:1, 0:65], ztile[0:1, 0:512],
                    start=False, stop=True), inc=TE)
                te_stop = TE.n
                S("scalar", lambda e: e.copy(b.lrow[:], b.pAO[64:65, :]),
                  extra_waits=[(TE, te_stop)])
                S("vector", lambda e: e.reciprocal(b.lrow[:], b.lrow[:]))
                S("tensor", lambda e: e.matmul(
                    b.pB[:], ones128[:], b.lrow[:], start=True, stop=True))
                S("scalar", lambda e: e.copy(b.scr64, b.pAO[0:64, :]))
                S("vector", lambda e: e.tensor_tensor(
                    b.scr64, b.scr64, b.pB[0:64, :], ALU.mult))
                S("scalar", lambda e, qg=qg: e.copy(
                    b.ao[(h % 2) * 64:(h % 2) * 64 + 64, h // 2,
                         qg * 512:(qg + 1) * 512], b.scr64))
                pao_free = (SE, SE.n)
            if DEBUG and s == 0 and h == 0:
                S("sync", lambda e: e.dma_start(dbgmap["dm16"], b.m16[:]))
                S("sync", lambda e: e.dma_start(dbgmap["dqsr"], b.qsr[:]))
                S("sync", lambda e: e.dma_start(dbgmap["dksr"], b.ksr[:]))
                S("sync", lambda e: e.dma_start(dbgmap["dqh"], b.Qh[:]))
                S("sync", lambda e: e.dma_start(dbgmap["dkh"],
                                                b.Kh[:, 0, :]))
                S("sync", lambda e: e.dma_start(
                    dbgmap["dao0"], b.ao[:, 0, :]))
                S("sync", lambda e: e.dma_start(
                    dbgmap["dkvn"],
                    b.kvn[:].rearrange("p a c -> p (a c)")))

        plist = [
            phase_init,
            lambda: phase_boundary("1", x_in, None, write_ln=True),
            phase_gating,
            lambda: phase_routing(debug=DEBUG),
            phase_scatter,
            phase_expert,
        ]
        if STAGE == 1:
            plist.append(lambda: phase_boundary("2", x_in, out,
                                                write_ln=False, gather=True))
        elif STAGE == 2:
            plist += [
                lambda: phase_boundary("2", x_in, xdram, write_ln=True,
                                       gather=True),
                phase_attn,
                lambda: phase_boundary("3", xdram, out, write_ln=False,
                                       attn=True),
            ]
        else:
            plist += [
                lambda: phase_boundary("2", x_in, xdram, write_ln=True,
                                       gather=True),
                phase_attn,
                lambda: phase_boundary("3", xdram, xdram, write_ln=True,
                                       attn=True),
                phase_gating,
                lambda: phase_routing(),
                phase_scatter,
                phase_expert,
                lambda: phase_boundary("4", xdram, out, write_ln=False,
                                       gather=True),
            ]
        for i, p in enumerate(plist):
            if i < nph:
                p()

    return nc


def _consts(cid):
    ut = np.triu(np.ones((128, 128), np.float32))
    sut_split = np.zeros((32, 32), np.float32)
    for i in range(32):
        for k in range(32):
            if (k < i) and ((k < 16) == (i < 16)):
                sut_split[k, i] = 1.0
    sut_cont = np.triu(np.ones((32, 32), np.float32), 1)
    sut8 = np.triu(np.ones((8, 8), np.float32), 1)
    indA = np.zeros((1, 32), np.float32)
    indA[0, :16] = 1
    indB = np.zeros((1, 32), np.float32)
    indB[0, 16:] = 1
    ltc = (np.arange(8) < cid).astype(np.float32).reshape(8, 1)
    myh = np.zeros((8, 1), np.float32)
    myh[cid, 0] = 1
    tri = np.where(np.arange(128)[:, None] <= np.arange(128)[None, :],
                   0.0, -1.0e30).astype(np.float32)
    inv_freq = 1.0 / 10000.0 ** (np.arange(0, RD, 2, dtype=np.float32) / RD)
    t = np.arange(T, dtype=np.float32)
    freqs = np.outer(t, inv_freq)
    cosr = np.tile(np.cos(freqs).T, (8, 1)).astype(np.float32)
    sinr = np.tile(np.sin(freqs).T, (8, 1)).astype(np.float32)
    qmask = np.ones((128, 1), np.float32)
    qmask[16:32] = 0
    qmask[48:64] = 0
    return dict(
        cut128=ut, cident=np.eye(128, dtype=np.float32), csuts=sut_split, csutc=sut_cont, csut8=sut8,
        cones128=np.ones((1, 128), np.float32),
        conescol=np.ones((128, 1), np.float32),
        cindAr=indA, cindBr=indB,
        cindAc=np.ascontiguousarray(indA.reshape(32, 1)),
        cindBc=np.ascontiguousarray(indB.reshape(32, 1)),
        cones32r=np.ones((1, 32), np.float32),
        cones8c=np.ones((8, 1), np.float32),
        cones8r=np.ones((1, 8), np.float32),
        cltc=ltc, cmyhot=myh,
        cebase=(np.arange(8, dtype=np.float32) * CAP).reshape(1, 8),
        conesrowb=np.ones((1, 512), ml_dtypes.bfloat16),
        ctrimask=tri, ccosr=cosr, csinr=sinr, cqmask=qmask,
        conesb2048=np.ones((1, T), ml_dtypes.bfloat16),
    )


def _rope_perm():
    idx = np.zeros(NH * RD, np.int64)
    for h in range(NH):
        for r in range(RD):
            old = h * RD + r
            if r < 16:
                new = (h // 8) * 128 + (h % 8) * 16 + r
            else:
                new = 256 + (h // 8) * 128 + (h % 8) * 16 + (r - 16)
            idx[new] = old
    return idx


def kernel(**inputs):
    f32 = np.float32
    x = np.ascontiguousarray(np.asarray(inputs["L0_attn_x"], f32))
    perm = _rope_perm()
    shared = dict(
        wg=np.ascontiguousarray(np.asarray(inputs["L0_ffn_Wg"], f32)),
        wkv=np.ascontiguousarray(np.asarray(inputs["L0_attn_W_kv"], f32)),
        gkv=np.ascontiguousarray(np.asarray(inputs["L0_attn_g_kv"], f32)),
        wkdec=np.ascontiguousarray(np.asarray(inputs["L0_attn_W_kdec"], f32)),
        wvdec=np.ascontiguousarray(np.asarray(inputs["L0_attn_W_vdec"], f32)),
        wq=np.ascontiguousarray(np.asarray(inputs["L0_attn_W_q"], f32)),
        wqdec=np.ascontiguousarray(np.asarray(inputs["L0_attn_W_qdec"], f32)),
        wkropep=np.ascontiguousarray(
            np.asarray(inputs["L0_attn_W_krope"], f32)[:, perm]),
        wqropep=np.ascontiguousarray(
            np.asarray(inputs["L0_attn_W_qrope"], f32)[:, perm]),
        wo=np.ascontiguousarray(np.asarray(inputs["L0_attn_W_o"], f32)),
    )
    W1 = np.asarray(inputs["L0_ffn_W1"], f32)
    B1 = np.asarray(inputs["L0_ffn_b1"], f32)
    W2 = np.asarray(inputs["L0_ffn_W2"], f32)
    B2 = np.asarray(inputs["L0_ffn_b2"], f32)

    import os
    nc = build(int(os.environ.get("KNP", "99")))
    in_maps = []
    for c in range(8):
        shard = np.ascontiguousarray(np.concatenate([x[c], x[c + 8]], axis=0))
        m = dict(shared)
        m.update(_consts(c))
        m.update(x=shard,
                 w1=np.ascontiguousarray(W1[c]),
                 b1=np.ascontiguousarray(B1[c]),
                 w2=np.ascontiguousarray(W2[c]),
                 b2=np.ascontiguousarray(B2[c]))
        in_maps.append(m)
    res = run_bass_kernel_spmd(nc, in_maps, core_ids=list(range(8)))
    o = np.empty((B, T, D), f32)
    for c in range(8):
        y = np.asarray(res.results[c]["out"], f32)
        o[c] = y[:T]
        o[c + 8] = y[T:]
    if DEBUG:
        kernel.dbg = [{k: np.asarray(v) for k, v in r.items()
                       if k.startswith("d")} for r in res.results]
    return o


# version 21
# speedup vs baseline: 1.2764x; 1.0111x over previous
import numpy as np
import ml_dtypes
from contextlib import ExitStack
import concourse.bass as bass
import concourse.mybir as mybir
from concourse.bass_utils import run_bass_kernel_spmd

F32 = mybir.dt.float32
BF16 = mybir.dt.bfloat16
I32 = mybir.dt.int32
AF = mybir.ActivationFunctionType
ALU = mybir.AluOpType
AX = mybir.AxisListType
IOA = bass.IndirectOffsetOnAxis

STAGE = 3
DEBUG = False

B, T, D = 16, 2048, 1024
TOK, NT = 2 * T, 32
E, H, CAP = 8, 4096, 2048
NH, HD, RD, QR, KVR = 16, 64, 32, 512, 256
FC, HCN, DCN = 8, 32, 8
SENT = 100000.0
GELU_A = 0.044715
GELU_2C = 2.0 * 0.7978845608028654
ENGS = ("sync", "tensor", "vector", "scalar", "gpsimd")


class Sem:
    def __init__(self, ctx, nc, name):
        self.h = ctx.enter_context(nc.semaphore(name))
        self.n = 0


class Phase:

    def __init__(self, nc, kb):
        self.nc = nc
        self.kb = kb
        self.items = {g: [] for g in ENGS}
        self.last = None

    def emit(self, eng, fn, waits=(), inc=None, inck=1):
        val = None
        if inc is not None:
            inc.n += inck
            val = inc.n
            self.last = (inc, val)
        self.items[eng].append((list(waits), fn, inc, inck, val))
        return self.last

    def seq(self, eng, fn, extra_waits=()):
        w = list(extra_waits)
        if self.last is not None:
            w.append(self.last)
        sem = {"sync": self.kb.LD, "gpsimd": self.kb.GP,
               "tensor": self.kb.TE, "vector": self.kb.VE,
               "scalar": self.kb.SE}[eng]
        inck = 16 if eng in ("sync", "gpsimd") else 1
        return self.emit(eng, fn, waits=w, inc=sem, inck=inck)

    def run(self):
        nc, kb = self.nc, self.kb
        drains = [(kb.LD, kb.LD.n)] + [(s, s.n) for s in
                  getattr(kb, "DRAINS", ()) if s.n]
        self.items["sync"].append((drains, None, None, 0, None))
        gdrains = [(kb.GP, kb.GP.n)] + [(s, s.n) for s in
                   getattr(kb, "SDRAINS", ()) if s.n]
        self.items["gpsimd"].append((gdrains, None, None, 0, None))
        with nc.Block() as bk:
            def mk(eng):
                auto = kb.AUTO[eng] if eng in ("vector", "scalar") else None

                def body(e):
                    prev = None
                    for waits, fn, sem, inck, val in self.items[eng]:
                        for (ws, wv) in waits:
                            e.wait_ge(ws.h, wv)
                        if fn is not None:
                            if auto is not None and prev is not None:
                                e.wait_ge(prev[0].h, prev[1])
                            inst = fn(e)
                            if inst is None:
                                if sem is not None and inck > 0:
                                    e.nop().then_inc(sem.h, inck)
                                continue
                            if sem is not None and inck > 0:
                                inst.then_inc(sem.h, inck)
                                if auto is not None:
                                    prev = (sem, val)
                            elif auto is not None:
                                auto.n += 1
                                inst.then_inc(auto.h, 1)
                                prev = (auto, auto.n)
                        elif sem is not None and inck > 0:
                            e.nop().then_inc(sem.h, inck)
                return body
            for eng in ENGS:
                if self.items[eng]:
                    getattr(bk, eng)(mk(eng))


class KBx:
    pass


def build(nph=99):
    nc = bass.Bass("TRN2", target_bir_lowering=False)
    dt_ = nc.dram_tensor

    _names = {}
    _osb, _ops = nc.sbuf_tensor, nc.psum_tensor

    def _uniq(name):
        i = _names.get(name, 0)
        _names[name] = i + 1
        return f"{name}_r{i}" if i else name

    nc.sbuf_tensor = lambda name, *a, **k: _osb(_uniq(name), *a, **k)
    nc.psum_tensor = lambda name, *a, **k: _ops(_uniq(name), *a, **k)

    x_in = dt_("x", [TOK, D], F32, kind="ExternalInput").ap()
    out = dt_("out", [TOK, D], F32, kind="ExternalOutput").ap()
    wg = dt_("wg", [D, E], F32, kind="ExternalInput").ap()
    w1 = dt_("w1", [D, H], F32, kind="ExternalInput").ap()
    b1 = dt_("b1", [H], F32, kind="ExternalInput").ap()
    w2 = dt_("w2", [H, D], F32, kind="ExternalInput").ap()
    b2 = dt_("b2", [D], F32, kind="ExternalInput").ap()
    wkv = dt_("wkv", [D, KVR], F32, kind="ExternalInput").ap()
    gkv = dt_("gkv", [KVR], F32, kind="ExternalInput").ap()
    wkdec = dt_("wkdec", [KVR, D], F32, kind="ExternalInput").ap()
    wvdec = dt_("wvdec", [KVR, D], F32, kind="ExternalInput").ap()
    wq = dt_("wq", [D, QR], F32, kind="ExternalInput").ap()
    wqdec = dt_("wqdec", [QR, D], F32, kind="ExternalInput").ap()
    wkropep = dt_("wkropep", [D, NH * RD], F32, kind="ExternalInput").ap()
    wqropep = dt_("wqropep", [QR, NH * RD], F32, kind="ExternalInput").ap()
    wo = dt_("wo", [D, D], F32, kind="ExternalInput").ap()
    cut128 = dt_("cut128", [128, 128], F32, kind="ExternalInput").ap()
    cident = dt_("cident", [128, 128], F32, kind="ExternalInput").ap()
    csuts = dt_("csuts", [32, 32], F32, kind="ExternalInput").ap()
    csutc = dt_("csutc", [32, 32], F32, kind="ExternalInput").ap()
    csut8 = dt_("csut8", [8, 8], F32, kind="ExternalInput").ap()
    cones128 = dt_("cones128", [1, 128], F32, kind="ExternalInput").ap()
    conescol = dt_("conescol", [128, 1], F32, kind="ExternalInput").ap()
    cindAr = dt_("cindAr", [1, 32], F32, kind="ExternalInput").ap()
    cindBr = dt_("cindBr", [1, 32], F32, kind="ExternalInput").ap()
    cindAc = dt_("cindAc", [32, 1], F32, kind="ExternalInput").ap()
    cindBc = dt_("cindBc", [32, 1], F32, kind="ExternalInput").ap()
    cones32r = dt_("cones32r", [1, 32], F32, kind="ExternalInput").ap()
    cones8c = dt_("cones8c", [8, 1], F32, kind="ExternalInput").ap()
    cones8r = dt_("cones8r", [1, 8], F32, kind="ExternalInput").ap()
    cltc = dt_("cltc", [8, 1], F32, kind="ExternalInput").ap()
    cmyhot = dt_("cmyhot", [8, 1], F32, kind="ExternalInput").ap()
    cebase = dt_("cebase", [1, 8], F32, kind="ExternalInput").ap()
    conesrowb = dt_("conesrowb", [1, 512], BF16, kind="ExternalInput").ap()
    ctrimask = dt_("ctrimask", [128, 128], F32, kind="ExternalInput").ap()
    ccosr = dt_("ccosr", [128, T], F32, kind="ExternalInput").ap()
    csinr = dt_("csinr", [128, T], F32, kind="ExternalInput").ap()
    cqmask = dt_("cqmask", [128, 1], F32, kind="ExternalInput").ap()
    conesb2048 = dt_("conesb2048", [1, T], BF16, kind="ExternalInput").ap()

    xdram = dt_("xdram", [TOK, D], F32).ap()
    lntm = dt_("lntm", [TOK, D], BF16).ap()
    lnlo = dt_("lnlo", [TOK, D], BF16).ap()
    stage = dt_("stg", [E * CAP, D], BF16).ap()
    rsout = dt_("rsout", [CAP, D], BF16).ap()
    yfm = dt_("yfm", [D, CAP], BF16).ap()
    yagin = dt_("yagin", [CAP, D], BF16).ap()
    yall = dt_("yall", [E * CAP, D], BF16, addr_space="Shared").ap()
    aghin = dt_("aghin", [1, 32], F32).ap()
    aghout = dt_("aghout", [8, 32], F32, addr_space="Shared").ap()
    bco = dt_("bco", [32, 8], F32).ap()
    bcoflat = bco.rearrange("c e -> (c e)").unsqueeze(0)
    contribfm = dt_("contribfm", [D, TOK], BF16).ap()
    kroped = dt_("kroped", [512, T], BF16).ap()
    qroped = dt_("qroped", [512, T], BF16).ap()
    vfmd = dt_("vfmd", [D, T], BF16).ap()
    mrowd = dt_("mrowd", [1, T], BF16).ap()

    dbgmap = {}
    if DEBUG:
        for nm in ["didx0", "didx1", "dcw0", "dcw1"]:
            dbgmap[nm] = dt_(nm, [128, 32], F32, kind="ExternalOutput").ap()
        dbgmap["dlg"] = dt_("dlg", [128, 256], F32, kind="ExternalOutput").ap()
        dbgmap["dtab"] = dt_("dtab", [8, 32], F32, kind="ExternalOutput").ap()
        dbgmap["drows"] = dt_("drows", [1, 64], F32, kind="ExternalOutput").ap()
        dbgmap["dsco"] = dt_("dsco", [32, 8], F32, kind="ExternalOutput").ap()
        dbgmap["dstk"] = dt_("dstk", [32, 8], F32, kind="ExternalOutput").ap()
        dbgmap["drs"] = dt_("drs", [CAP, D], BF16, kind="ExternalOutput").ap()
        dbgmap["dm16"] = dt_("dm16", [1, 16], F32, kind="ExternalOutput").ap()
        dbgmap["dqsr"] = dt_("dqsr", [1, T], F32, kind="ExternalOutput").ap()
        dbgmap["dksr"] = dt_("dksr", [1, 4], F32, kind="ExternalOutput").ap()
        dbgmap["dqh"] = dt_("dqh", [128, T], BF16, kind="ExternalOutput").ap()
        dbgmap["dkh"] = dt_("dkh", [128, T], BF16, kind="ExternalOutput").ap()
        dbgmap["dao0"] = dt_("dao0", [128, T], BF16, kind="ExternalOutput").ap()
        dbgmap["dpt"] = dt_("dpt", [128, 512], BF16, kind="ExternalOutput").ap()
        dbgmap["dkvn"] = dt_("dkvn", [128, 2 * T], BF16, kind="ExternalOutput").ap()
        dbgmap["dqc"] = dt_("dqc", [128, 4 * T], BF16, kind="ExternalOutput").ap()
        dbgmap["dqc2"] = dt_("dqc2", [128, 4 * T], BF16, kind="ExternalOutput").ap()
        dbgmap["dqcF"] = dt_("dqcF", [128, 4 * T], BF16, kind="ExternalOutput").ap()
        dbgmap["dwq"] = dt_("dwq", [128, FC * QR], BF16, kind="ExternalOutput").ap()
        dbgmap["dwkv"] = dt_("dwkv", [128, FC * KVR], BF16, kind="ExternalOutput").ap()
        dbgmap["dkrop"] = dt_("dkrop", [512, T], BF16, kind="ExternalOutput").ap()
        dbgmap["dqrop"] = dt_("dqrop", [512, T], BF16, kind="ExternalOutput").ap()
        dbgmap["dyag"] = dt_("dyag", [CAP, D], BF16, kind="ExternalOutput").ap()

    ctx = ExitStack()
    with ctx:
        def sb(name, shape, dtype):
            return ctx.enter_context(nc.sbuf_tensor(name, shape, dtype))

        kb = KBx()
        kb.LD = Sem(ctx, nc, "LD")
        kb.GP = Sem(ctx, nc, "GP")
        kb.TE = Sem(ctx, nc, "TE")
        kb.VE = Sem(ctx, nc, "VE")
        kb.SE = Sem(ctx, nc, "SE")
        kb.CC = [Sem(ctx, nc, f"CC{i}") for i in range(8)]
        kb.AUTO = {"vector": Sem(ctx, nc, "AUV"), "scalar": Sem(ctx, nc, "AUS")}
        kb.PLS = Sem(ctx, nc, "PLS")
        kb.WGH = Sem(ctx, nc, "WGH")
        kb.XS = [Sem(ctx, nc, "XS0"), Sem(ctx, nc, "XS1")]
        kb.LNS = [Sem(ctx, nc, "LNS0"), Sem(ctx, nc, "LNS1")]
        kb.XL = [Sem(ctx, nc, "XL0"), Sem(ctx, nc, "XL1")]
        kb.TT = [Sem(ctx, nc, "TT0"), Sem(ctx, nc, "TT1")]
        kb.TS = [Sem(ctx, nc, "TS0"), Sem(ctx, nc, "TS1")]
        kb.SC = [Sem(ctx, nc, "SC0"), Sem(ctx, nc, "SC1")]
        kb.DRAINS = kb.XS + kb.LNS + kb.XL + kb.TT + kb.TS + [kb.WGH]
        kb.SDRAINS = kb.SC
        kb.icc = 0
        LD, GP, TE, VE, SE = kb.LD, kb.GP, kb.TE, kb.VE, kb.SE

        def next_cc():
            s = kb.CC[kb.icc]
            kb.icc += 1
            return s

        ut128 = sb("ut128", [128, 128], F32)
        ident = sb("ident", [128, 128], F32)
        suts = sb("suts", [32, 32], F32)
        sutc = sb("sutc", [32, 32], F32)
        sut8 = sb("sut8", [8, 8], F32)
        ones128 = sb("ones128", [1, 128], F32)
        onescol = sb("onescol", [128, 1], F32)
        indAr = sb("indAr", [1, 32], F32)
        indBr = sb("indBr", [1, 32], F32)
        indAc = sb("indAc", [32, 1], F32)
        indBc = sb("indBc", [32, 1], F32)
        ones32r = sb("ones32r", [1, 32], F32)
        ones8c = sb("ones8c", [8, 1], F32)
        ones8r = sb("ones8r", [1, 8], F32)
        ltc = sb("ltc", [8, 1], F32)
        myhot = sb("myhot", [8, 1], F32)
        ebase = sb("ebase", [1, 8], F32)
        onesrowb = sb("onesrowb", [1, 512], BF16)
        wghi = sb("wghi", [128, FC, E], BF16)
        wglo = sb("wglo", [128, FC, E], BF16)
        gkvc = sb("gkvc", [128, 2], F32)
        trimaskb = sb("trimaskb", [128, 128], F32)
        ztile = sb("ztile", [128, D], BF16)
        qmaskc = sb("qmaskc", [128, 1], F32)
        idx0i = sb("idx0i", [128, 32], I32)
        idx1i = sb("idx1i", [128, 32], I32)
        cw0 = sb("cw0", [128, 32], F32)
        cw1 = sb("cw1", [128, 32], F32)
        lgp = sb("lgp", [128, 32, E], F32)

        def phase_init():
            with ExitStack() as pc:
                def psb(n, s, d):
                    return pc.enter_context(nc.sbuf_tensor(n, s, d))

                ph = Phase(nc, kb)
                wgf = psb("wgf", [128, FC, E], F32)
                gkvr = psb("gkvr", [2, 128], F32)
                gps = pc.enter_context(nc.psum_tensor("gps", [128, 2], F32))
                consts = [(ut128, cut128), (ident, cident),
                          (suts, csuts), (sutc, csutc),
                          (sut8, csut8), (ones128, cones128),
                          (onescol, conescol), (indAr, cindAr),
                          (indBr, cindBr), (indAc, cindAc), (indBc, cindBc),
                          (ones32r, cones32r), (ones8c, cones8c),
                          (ones8r, cones8r), (ltc, cltc), (myhot, cmyhot),
                          (ebase, cebase), (onesrowb, conesrowb)]
                for dst, src in consts:
                    ph.emit("sync", lambda e, d=dst, s=src:
                            e.dma_start(d[:], s), inc=LD, inck=16)
                ph.emit("sync", lambda e: e.dma_start(
                    wgf[:], wg.rearrange("(f p) e -> p f e", p=128)),
                    inc=LD, inck=16)
                ph.emit("sync", lambda e: e.dma_start(
                    trimaskb[:], ctrimask), inc=LD, inck=16)
                ph.emit("sync", lambda e: e.dma_start(
                    qmaskc[:], cqmask), inc=LD, inck=16)
                ph.emit("sync", lambda e: e.dma_start(
                    gkvr[:], gkv.rearrange("(c p) -> c p", p=128)),
                    inc=LD, inck=16)
                ldA = LD.n
                ph.emit("gpsimd", lambda e: e.dma_start(
                    wghi[:], wg.rearrange("(f p) e -> p f e", p=128)),
                    inc=kb.WGH, inck=16)
                wgA = kb.WGH.n

                ph.emit("gpsimd", lambda e: e.memset(ztile[:], 0.0),
                        inc=kb.PLS)
                ph.emit("gpsimd", lambda e: e.nop(),
                        waits=[(kb.PLS, kb.PLS.n)])
                sv = stage.rearrange("(g p) d -> g p d", p=128)
                for g in range(E * CAP // 128):
                    ph.emit("gpsimd", lambda e, g=g:
                            e.dma_start(sv[g], ztile[:]), inc=GP, inck=16)
                ph.emit("vector", lambda e: e.tensor_tensor(
                    wglo[:], wgf[:], wghi[:], ALU.subtract),
                    waits=[(LD, ldA), (kb.WGH, wgA)], inc=VE)
                ph.emit("tensor", lambda e: e.transpose(
                    gps[:], gkvr[:], ident[0:2, 0:2]),
                    waits=[(LD, ldA)], inc=TE)
                ph.emit("scalar", lambda e: e.copy(gkvc[:], gps[:]),
                        waits=[(TE, TE.n)], inc=SE)
                ph.run()

        def phase_boundary(tag, src_x, dst_x, write_ln, gather=False,
                           attn=False, gscale=0.5):
            with ExitStack() as pc:
                def psb(n, s, d):
                    return pc.enter_context(nc.sbuf_tensor(n, s, d))

                ph = Phase(nc, kb)
                xt = [psb(f"x{tag}{i}", [128, D], F32) for i in range(2)]
                ct = [psb(f"c{tag}{i}", [128, D], BF16) for i in range(2)] \
                    if attn else None
                lnf = psb(f"lf{tag}", [128, D], F32)
                lhi = [psb(f"lh{tag}{i}", [128, D], BF16) for i in range(2)]
                llo = [psb(f"ll{tag}{i}", [128, D], BF16) for i in range(2)]
                mean = psb(f"mn{tag}", [128, 1], F32)
                ssq = psb(f"sq{tag}", [128, 1], F32)
                rstd = psb(f"rs{tag}", [128, 1], F32)
                sqs = psb(f"sqs{tag}", [128, D], F32)
                if gather:
                    g0t = psb(f"g0{tag}", [128, D], F32)
                    g1t = psb(f"g1{tag}", [128, D], F32)
                xv = src_x.rearrange("(n p) d -> n p d", p=128)
                dv = dst_x.rearrange("(n p) d -> n p d", p=128) \
                    if dst_x is not None else None
                lv = lntm.rearrange("(n p) d -> n p d", p=128)
                lov = lnlo.rearrange("(n p) d -> n p d", p=128)

                nve = 4 if write_ln else 1
                ve0, se0, gp0 = VE.n, SE.n, GP.n
                xl_loads, xs_storex, lns_val = [], [], []

                for i in range(NT):
                    w = []
                    if i >= 2:
                        w.append((VE, ve0 + nve * (i - 1)))
                        if dst_x is not None:
                            w.append((kb.XS[i % 2], xs_storex[i - 2]))
                    ph.emit("sync", lambda e, i=i: e.dma_start(
                        xt[i % 2][:], xv[i]), waits=w,
                        inc=kb.XL[i % 2], inck=16)
                    if attn:
                        ph.emit("sync", lambda e, i=i: e.dma_start(
                            ct[i % 2][:], contribfm[:, i * 128:(i + 1) * 128],
                            transpose=True), inc=kb.XL[i % 2], inck=16)
                    xl_loads.append(kb.XL[i % 2].n)
                    if dst_x is not None:
                        ph.emit("sync", lambda e, i=i: e.dma_start(
                            dv[i], xt[i % 2][:]),
                            waits=[(VE, ve0 + nve * i + 1)],
                            inc=kb.XS[i % 2], inck=16)
                    xs_storex.append(kb.XS[i % 2].n)
                    if write_ln:
                        ph.emit("sync", lambda e, i=i: e.dma_start(
                            lv[i], lhi[i % 2][:]),
                            waits=[(VE, ve0 + nve * i + 3)],
                            inc=kb.LNS[i % 2], inck=16)
                        ph.emit("sync", lambda e, i=i: e.dma_start(
                            lov[i], llo[i % 2][:]),
                            waits=[(VE, ve0 + nve * i + 4)],
                            inc=kb.LNS[i % 2], inck=16)
                    lns_val.append(kb.LNS[i % 2].n)

                if gather:
                    bcr = {}
                    ph.emit("gpsimd", lambda e: bcr.__setitem__(
                        "r", e.to_reg(E * CAP - 1)))
                    ph.emit("gpsimd", lambda e: e.memset(g0t[:], 0.0))
                    ph.emit("gpsimd", lambda e: e.memset(g1t[:], 0.0),
                            inc=kb.PLS)
                    ph.emit("gpsimd", lambda e: e.nop(),
                            waits=[(kb.PLS, kb.PLS.n)])
                    for i in range(NT):
                        w = []
                        if i >= 1:
                            w.append((VE, ve0 + nve * (i - 1) + 1))
                        ph.emit("gpsimd", lambda e, i=i: e.indirect_dma_start(
                            g0t[:], None, yall, IOA(idx0i[:, i:i + 1], 0),
                            bounds_check=bcr["r"], oob_is_err=False),
                            waits=w, inc=GP, inck=16)
                        ph.emit("gpsimd", lambda e, i=i: e.indirect_dma_start(
                            g1t[:], None, yall, IOA(idx1i[:, i:i + 1], 0),
                            bounds_check=bcr["r"], oob_is_err=False),
                            inc=GP, inck=16)

                for i in range(NT):
                    p = i % 2
                    w = [(kb.XL[i % 2], xl_loads[i])]
                    if gather:
                        w.append((GP, gp0 + 32 * (i + 1)))
                    if write_ln and i >= 2:
                        w.append((kb.LNS[i % 2], lns_val[i - 2]))
                    if gather:
                        ph.emit("vector", lambda e, i=i: e.tensor_scalar(
                            g1t[:], g1t[:], cw1[:, i:i + 1], None, ALU.mult),
                            waits=w)
                        ph.emit("vector", lambda e, i=i:
                                e.scalar_tensor_tensor(
                                    g0t[:], g0t[:], cw0[:, i:i + 1], g1t[:],
                                    ALU.mult, ALU.add))
                        ph.emit("vector", lambda e, p=p:
                                e.scalar_tensor_tensor(
                                    xt[p][:], g0t[:], gscale, xt[p][:],
                                    ALU.mult, ALU.add), inc=VE)
                    elif attn:
                        ph.emit("vector", lambda e, p=p: e.tensor_tensor(
                            xt[p][:], xt[p][:], ct[p][:], ALU.add),
                            waits=w, inc=VE)
                    else:
                        ph.emit("vector", lambda e: e.nop(), waits=w, inc=VE)
                    if write_ln:
                        ph.emit("vector", lambda e, p=p: e.reduce_sum(
                            mean[:].unsqueeze(1), xt[p][:].unsqueeze(1),
                            axis=AX.X))
                        ph.emit("vector", lambda e: e.tensor_scalar_mul(
                            mean[:], mean[:], 1.0 / D))
                        ph.emit("vector", lambda e: e.tensor_tensor(
                            rstd[:], mean[:], mean[:], ALU.mult),
                            waits=[(SE, se0 + 2 * i + 1)])
                        ph.emit("vector", lambda e: e.scalar_tensor_tensor(
                            rstd[:], ssq[:], 1.0 / D, rstd[:],
                            ALU.mult, ALU.subtract))
                        ph.emit("vector", lambda e: e.tensor_scalar_add(
                            rstd[:], rstd[:], 1e-5), inc=VE)
                        ph.emit("vector", lambda e, p=p: e.tensor_scalar(
                            lnf[:], xt[p][:], mean[:, 0:1], None,
                            ALU.subtract), waits=[(SE, se0 + 2 * i + 2)])
                        ph.emit("vector", lambda e: e.tensor_scalar(
                            lnf[:], lnf[:], rstd[:, 0:1], None, ALU.mult))
                        ph.emit("vector", lambda e, p=p: e.tensor_copy(
                            lhi[p][:], lnf[:]), inc=VE)
                        ph.emit("vector", lambda e, p=p: e.tensor_tensor(
                            llo[p][:], lnf[:], lhi[p][:], ALU.subtract),
                            inc=VE)

                if write_ln:
                    for i in range(NT):
                        p = i % 2
                        ph.emit("scalar", lambda e, p=p: e.activation(
                            sqs[:], xt[p][:], AF.Square, accum_out=ssq[:]),
                            waits=[(VE, ve0 + nve * i + 1)], inc=SE)
                        ph.emit("scalar", lambda e: e.activation(
                            rstd[:], rstd[:], AF.Ln),
                            waits=[(VE, ve0 + nve * i + 2)])
                        ph.emit("scalar", lambda e: e.activation(
                            rstd[:], rstd[:], AF.Exp, scale=-0.5), inc=SE)
                ph.run()

        def phase_gating():
            with ExitStack() as pc:
                def psb(n, s, d):
                    return pc.enter_context(nc.sbuf_tensor(n, s, d))

                ph = Phase(nc, kb)
                lfh = psb("glfh", [128, FC, TOK], BF16)
                lfl = psb("glfl", [128, FC, TOK], BF16)
                plg = [pc.enter_context(nc.psum_tensor(f"gpl{i}", [128, E],
                                                       F32))
                       for i in range(2)]
                te0, ve0 = TE.n, VE.n
                for f in range(FC):
                    ph.emit("sync", lambda e, f=f: e.dma_start(
                        lfh[:, f, :], lntm[:, f * 128:(f + 1) * 128],
                        transpose=True), inc=LD, inck=16)
                    ph.emit("sync", lambda e, f=f: e.dma_start(
                        lfl[:, f, :], lnlo[:, f * 128:(f + 1) * 128],
                        transpose=True), inc=LD, inck=16)
                ldall = LD.n

                def mmtile(e, c, p):
                    sl = slice(c * 128, (c + 1) * 128)
                    for f in range(FC):
                        e.matmul(p[:], lfh[:, f, sl], wghi[:, f, :],
                                 start=(f == 0), stop=False)
                        e.matmul(p[:], lfh[:, f, sl], wglo[:, f, :],
                                 start=False, stop=False)
                        mm = e.matmul(p[:], lfl[:, f, sl], wghi[:, f, :],
                                      start=False, stop=(f == FC - 1))
                    return mm

                for c in range(NT):
                    w = [(LD, ldall)] if c == 0 else []
                    if c >= 2:
                        w.append((VE, ve0 + (c - 1)))
                    ph.emit("tensor", lambda e, c=c: mmtile(e, c, plg[c % 2]),
                            waits=w, inc=TE)
                    ph.emit("vector", lambda e, c=c: e.tensor_copy(
                        lgp[:, c, :], plg[c % 2][:]),
                        waits=[(TE, te0 + c + 1)], inc=VE)
                ph.run()

        def phase_routing(debug=False):
            with ExitStack() as pc:
                def psb(n, s, d):
                    return pc.enter_context(nc.sbuf_tensor(n, s, d))

                def pps(n, s):
                    return pc.enter_context(nc.psum_tensor(n, s, F32))

                ph = Phase(nc, kb)
                S = ph.seq
                m1 = psb("rm1", [128, 32], F32)
                sml = psb("rsml", [128, 32, E], F32)
                sp = psb("rsp", [128, 32, E], F32)
                Z = psb("rZ", [128, 32], F32)
                eq = psb("req", [128, 32, E], F32)
                oh1 = psb("roh1", [128, 32, E], F32)
                oh2 = psb("roh2", [128, 32, E], F32)
                fnd = psb("rfnd", [128, 32], F32)
                gg = psb("rgg", [128, 32], F32)
                e2 = psb("re2", [128, 32], F32)
                den = psb("rden", [128, 32], F32)
                sUT = psb("rsUT", [128, 32, E], F32)
                spre = psb("rspre", [128, 32, E], F32)
                kept0 = psb("rkept0", [128, 32, E], F32)
                kept1 = psb("rkept1", [128, 32, E], F32)
                srow = psb("rsrow", [1, 256], F32)
                sTc = psb("rsTc", [32, 8], F32)
                sTc2 = psb("rsTc2", [32, 8], F32)
                sTk = psb("rsTk", [32, 8], F32)
                spay = psb("rspay", [1, 32], F32)
                stab = psb("rstab", [8, 32], F32)
                sP1A = psb("rsP1A", [8, 8], F32)
                sP1B = psb("rsP1B", [8, 8], F32)
                sP2A = psb("rsP2A", [8, 8], F32)
                sP2B = psb("rsP2B", [8, 8], F32)
                stot = psb("rstot", [1, 24], F32)
                counts = psb("rcounts", [1, 8], F32)
                counts8 = psb("rcounts8", [8, 8], F32)
                kbrow = psb("rkbrow", [1, 8], F32)
                myk0 = psb("rmyk0", [1, 8], F32)
                myrows = psb("rmyrows", [1, 32], F32)
                k88 = psb("rk88", [8, 8], F32)
                ktot = psb("rktot", [8, 8], F32)
                sco = psb("rsco", [32, 8], F32)
                erow = psb("rerow", [1, 8], F32)
                any_ = psb("rany", [128, 32], F32)
                sa = psb("rsa", [128, 32], F32)
                idxf = psb("ridxf", [128, 32], F32)
                pBIG = pps("rpBIG", [128, 256])
                pUT = pps("rpUT", [128, 256])
                pROW = pps("rpROW", [1, 256])
                pSM = pps("rpSM", [32, 32])
                cc = next_cc()

                m1b = m1[:].unsqueeze(2).broadcast_to([128, 32, E])
                e2b = e2[:].unsqueeze(2).broadcast_to([128, 32, E])
                oh1f = oh1[:].rearrange("p a b -> p (a b)")
                oh2f = oh2[:].rearrange("p a b -> p (a b)")

                S("vector", lambda e: e.reduce_max(
                    m1[:].rearrange("p (c o) -> p c o", o=1), lgp[:],
                    axis=AX.X))
                S("vector", lambda e: e.tensor_tensor(
                    sml[:], lgp[:], m1b, ALU.subtract))
                S("scalar", lambda e: e.activation(
                    sp[:].rearrange("p a b -> p (a b)"),
                    sml[:].rearrange("p a b -> p (a b)"), AF.Exp))
                S("vector", lambda e: e.reduce_sum(
                    Z[:].rearrange("p (c o) -> p c o", o=1), sp[:],
                    axis=AX.X))
                S("vector", lambda e: e.tensor_tensor(
                    eq[:], lgp[:], m1b, ALU.is_equal))
                S("vector", lambda e: e.memset(fnd[:], 0.0))
                for ee in range(E):
                    S("vector", lambda e: e.tensor_scalar(
                        gg[:], fnd[:], -1.0, 1.0, ALU.mult, ALU.add))
                    S("vector", lambda e, ee=ee: e.tensor_tensor(
                        oh1[:, :, ee], eq[:, :, ee], gg[:], ALU.mult))
                    S("vector", lambda e, ee=ee: e.tensor_tensor(
                        fnd[:], fnd[:], oh1[:, :, ee], ALU.add))
                S("vector", lambda e: e.scalar_tensor_tensor(
                    sml[:], oh1[:], -2.0, sp[:], ALU.mult, ALU.add))
                S("vector", lambda e: e.reduce_max(
                    e2[:].rearrange("p (c o) -> p c o", o=1), sml[:],
                    axis=AX.X))
                S("vector", lambda e: e.tensor_tensor(
                    eq[:], sml[:], e2b, ALU.is_equal))
                S("vector", lambda e: e.memset(fnd[:], 0.0))
                for ee in range(E):
                    S("vector", lambda e: e.tensor_scalar(
                        gg[:], fnd[:], -1.0, 1.0, ALU.mult, ALU.add))
                    S("vector", lambda e, ee=ee: e.tensor_tensor(
                        oh2[:, :, ee], eq[:, :, ee], gg[:], ALU.mult))
                    S("vector", lambda e, ee=ee: e.tensor_tensor(
                        fnd[:], fnd[:], oh2[:, :, ee], ALU.add))
                S("vector", lambda e: e.tensor_scalar(
                    den[:], Z[:], 1e-9, 1.0, ALU.mult, ALU.add))
                S("vector", lambda e: e.tensor_tensor(
                    den[:], den[:], e2[:], ALU.add))
                S("vector", lambda e: e.reciprocal(cw0[:], den[:]))
                S("vector", lambda e: e.tensor_tensor(
                    cw1[:], e2[:], cw0[:], ALU.mult))

                S("tensor", lambda e: e.matmul(pROW[:], onescol[:], oh1f,
                                               start=True, stop=True))
                S("vector", lambda e: e.tensor_copy(srow[:], pROW[:]))
                S("sync", lambda e: e.dma_start(bcoflat, srow[:]))
                S("sync", lambda e: e.dma_start(sTc[:], bco))
                S("tensor", lambda e: e.matmul(pROW[:], onescol[:], oh2f,
                                               start=True, stop=True))
                S("vector", lambda e: e.tensor_copy(srow[:], pROW[:]))
                S("sync", lambda e: e.dma_start(bcoflat, srow[:]))
                S("sync", lambda e: e.dma_start(sTc2[:], bco))
                S("tensor", lambda e: e.matmul(pROW[0:1, 0:8], indAc[:],
                                               sTc[:], start=True, stop=True))
                S("tensor", lambda e: e.matmul(pROW[0:1, 8:16], indBc[:],
                                               sTc[:], start=True, stop=True))
                S("tensor", lambda e: e.matmul(pROW[0:1, 16:24], indAc[:],
                                               sTc2[:], start=True,
                                               stop=True))
                S("tensor", lambda e: e.matmul(pROW[0:1, 24:32], indBc[:],
                                               sTc2[:], start=True,
                                               stop=True))
                S("vector", lambda e: e.tensor_copy(spay[:],
                                                    pROW[0:1, 0:32]))
                S("sync", lambda e: e.dma_start(aghin, spay[:]))
                ph.emit("gpsimd", lambda e: e.collective_compute(
                    "AllGather", ALU.bypass, replica_groups=[list(range(8))],
                    ins=[aghin.opt()], outs=[aghout.opt()]),
                    waits=[ph.last], inc=cc)
                S("sync", lambda e: e.dma_start(stab[:], aghout))

                S("tensor", lambda e: e.matmul(pSM[0:8, 0:8], sut8[:],
                                               stab[:, 0:8], start=True,
                                               stop=True))
                S("vector", lambda e: e.tensor_copy(sP1A[:], pSM[0:8, 0:8]))
                S("tensor", lambda e: e.matmul(pSM[0:1, 0:8], ones8c[:],
                                               stab[:, 0:8], start=True,
                                               stop=True))
                S("tensor", lambda e: e.matmul(pSM[0:1, 8:16], ones8c[:],
                                               stab[:, 8:16], start=True,
                                               stop=True))
                S("tensor", lambda e: e.matmul(pSM[0:1, 16:24], ones8c[:],
                                               stab[:, 16:24], start=True,
                                               stop=True))
                S("vector", lambda e: e.tensor_copy(stot[:],
                                                    pSM[0:1, 0:24]))
                S("tensor", lambda e: e.matmul(pSM[0:8, 0:8], sut8[:],
                                               stab[:, 8:16], start=True,
                                               stop=False))
                S("tensor", lambda e: e.matmul(pSM[0:8, 0:8], ones8r[:],
                                               stot[0:1, 0:8], start=False,
                                               stop=True))
                S("vector", lambda e: e.tensor_copy(sP1B[:], pSM[0:8, 0:8]))
                S("tensor", lambda e: e.matmul(pSM[0:8, 0:8], sut8[:],
                                               stab[:, 16:24], start=True,
                                               stop=True))
                S("vector", lambda e: e.tensor_copy(sP2A[:], pSM[0:8, 0:8]))
                S("tensor", lambda e: e.matmul(pSM[0:8, 0:8], sut8[:],
                                               stab[:, 24:32], start=True,
                                               stop=False))
                S("tensor", lambda e: e.matmul(pSM[0:8, 0:8], ones8r[:],
                                               stot[0:1, 16:24], start=False,
                                               stop=True))
                S("vector", lambda e: e.tensor_copy(sP2B[:], pSM[0:8, 0:8]))
                S("vector", lambda e: e.tensor_tensor(
                    counts[:], stot[0:1, 0:8], stot[0:1, 8:16], ALU.add))
                S("vector", lambda e: e.tensor_scalar_min(
                    counts[:], counts[:], float(CAP)))
                S("tensor", lambda e: e.matmul(pSM[0:8, 8:16], ones8r[:],
                                               counts[:], start=True,
                                               stop=True))
                S("vector", lambda e: e.tensor_copy(counts8[:],
                                                    pSM[0:8, 8:16]))

                def clampk(Pall, hsl):
                    S("vector", lambda e, P=Pall: e.tensor_scalar(
                        k88[:], P[:], -1.0, float(CAP), ALU.mult, ALU.add))
                    S("vector", lambda e: e.tensor_relu(k88[:], k88[:]))
                    S("vector", lambda e, h=hsl: e.tensor_tensor(
                        k88[:], k88[:], h, ALU.min))

                def clampk2(Pall, hsl):
                    S("vector", lambda e, P=Pall: e.tensor_tensor(
                        k88[:], P[:], counts8[:], ALU.add))
                    S("vector", lambda e: e.tensor_scalar(
                        k88[:], k88[:], -1.0, float(CAP), ALU.mult, ALU.add))
                    S("vector", lambda e: e.tensor_relu(k88[:], k88[:]))
                    S("vector", lambda e, h=hsl: e.tensor_tensor(
                        k88[:], k88[:], h, ALU.min))

                clampk(sP1A, stab[:, 0:8])
                S("vector", lambda e: e.tensor_copy(ktot[:], k88[:]))
                clampk(sP1B, stab[:, 8:16])
                S("vector", lambda e: e.tensor_tensor(ktot[:], ktot[:],
                                                      k88[:], ALU.add))
                S("tensor", lambda e: e.matmul(pSM[0:1, 0:8], myhot[:],
                                               ktot[:], start=True,
                                               stop=True))
                S("vector", lambda e: e.tensor_copy(myk0[:], pSM[0:1, 0:8]))
                clampk2(sP2A, stab[:, 16:24])
                S("vector", lambda e: e.tensor_tensor(ktot[:], ktot[:],
                                                      k88[:], ALU.add))
                clampk2(sP2B, stab[:, 24:32])
                S("vector", lambda e: e.tensor_tensor(ktot[:], ktot[:],
                                                      k88[:], ALU.add))
                S("tensor", lambda e: e.matmul(pSM[0:1, 0:8], ltc[:],
                                               ktot[:], start=True,
                                               stop=True))
                S("vector", lambda e: e.tensor_copy(kbrow[:], pSM[0:1, 0:8]))
                for j, src in enumerate([sP1A, sP1B, sP2A, sP2B]):
                    S("tensor", lambda e, src=src: e.matmul(
                        pSM[0:1, 0:8], myhot[:], src[:], start=True,
                        stop=True))
                    S("vector", lambda e, j=j: e.tensor_copy(
                        myrows[0:1, j * 8:(j + 1) * 8], pSM[0:1, 0:8]))

                def keep_pass(Tcs, ohf, oh, myA, myB, with_counts, keptdst):
                    S("tensor", lambda e: e.matmul(pSM[0:32, 0:8], suts[:],
                                                   Tcs[:], start=True,
                                                   stop=False))
                    S("tensor", lambda e: e.matmul(
                        pSM[0:32, 0:8], indAr[:],
                        myrows[0:1, myA * 8:myA * 8 + 8],
                        start=False, stop=False))
                    S("tensor", lambda e: e.matmul(
                        pSM[0:32, 0:8], indBr[:],
                        myrows[0:1, myB * 8:myB * 8 + 8],
                        start=False, stop=not with_counts))
                    if with_counts:
                        S("tensor", lambda e: e.matmul(
                            pSM[0:32, 0:8], ones32r[:], counts[:],
                            start=False, stop=True))
                    S("vector", lambda e: e.tensor_copy(sco[:],
                                                        pSM[0:32, 0:8]))
                    S("sync", lambda e: e.dma_start(bco, sco[:]))
                    S("sync", lambda e: e.dma_start(srow[:], bcoflat))
                    S("tensor", lambda e: e.matmul(pBIG[:], ones128[:],
                                                   srow[:], start=True,
                                                   stop=True))
                    S("tensor", lambda e, ohf=ohf: e.matmul(
                        pUT[:], ut128[:], ohf, start=True, stop=True))
                    S("vector", lambda e: e.tensor_copy(
                        sUT[:].rearrange("p a b -> p (a b)"), pUT[:]))
                    S("vector", lambda e: e.tensor_tensor(
                        spre[:].rearrange("p a b -> p (a b)"),
                        sUT[:].rearrange("p a b -> p (a b)"), pBIG[:],
                        ALU.add))
                    S("vector", lambda e: e.tensor_single_scalar(
                        spre[:], spre[:], CAP + 0.4, ALU.is_le))
                    S("vector", lambda e, oh=oh, kd=keptdst: e.tensor_tensor(
                        kd[:], spre[:], oh[:], ALU.mult))

                keep_pass(sTc, oh1f, oh1, 0, 1, False, kept0)
                keep_pass(sTc2, oh2f, oh2, 2, 3, True, kept1)

                def rank_pass(kept, extra_row, idxi, cwx, dump=False):
                    kf = kept[:].rearrange("p a b -> p (a b)")
                    S("tensor", lambda e, kf=kf: e.matmul(
                        pROW[:], onescol[:], kf, start=True, stop=True))
                    S("vector", lambda e: e.tensor_copy(srow[:], pROW[:]))
                    S("sync", lambda e: e.dma_start(bcoflat, srow[:]))
                    S("sync", lambda e: e.dma_start(sTk[:], bco))
                    S("vector", lambda e: e.tensor_tensor(
                        erow[:], ebase[:], kbrow[:], ALU.add))
                    if extra_row is not None:
                        S("vector", lambda e: e.tensor_tensor(
                            erow[:], erow[:], extra_row[:], ALU.add))
                    S("vector", lambda e: e.tensor_scalar_add(
                        erow[:], erow[:], -1.0))
                    S("tensor", lambda e: e.matmul(pSM[0:32, 0:8], sutc[:],
                                                   sTk[:], start=True,
                                                   stop=False))
                    S("tensor", lambda e: e.matmul(pSM[0:32, 0:8],
                                                   ones32r[:], erow[:],
                                                   start=False, stop=True))
                    S("vector", lambda e: e.tensor_copy(sco[:],
                                                        pSM[0:32, 0:8]))
                    if dump and DEBUG:
                        S("sync", lambda e: e.dma_start(dbgmap["dsco"],
                                                        sco[:]))
                        S("sync", lambda e: e.dma_start(dbgmap["dstk"],
                                                        sTk[:]))
                    S("sync", lambda e: e.dma_start(bco, sco[:]))
                    S("sync", lambda e: e.dma_start(srow[:], bcoflat))
                    S("tensor", lambda e: e.matmul(pBIG[:], ones128[:],
                                                   srow[:], start=True,
                                                   stop=True))
                    S("tensor", lambda e, kf=kf: e.matmul(
                        pUT[:], ut128[:], kf, start=True, stop=True))
                    S("vector", lambda e: e.tensor_copy(
                        sUT[:].rearrange("p a b -> p (a b)"), pUT[:]))
                    S("vector", lambda e: e.tensor_tensor(
                        spre[:].rearrange("p a b -> p (a b)"),
                        sUT[:].rearrange("p a b -> p (a b)"), pBIG[:],
                        ALU.add))
                    S("vector", lambda e, kept=kept: e.tensor_tensor(
                        spre[:], spre[:], kept[:], ALU.mult))
                    S("vector", lambda e: e.reduce_sum(
                        idxf[:].rearrange("p (c o) -> p c o", o=1), spre[:],
                        axis=AX.X))
                    S("vector", lambda e, kept=kept: e.reduce_sum(
                        any_[:].rearrange("p (c o) -> p c o", o=1), kept[:],
                        axis=AX.X))
                    S("vector", lambda e: e.tensor_scalar(
                        sa[:], any_[:], -SENT, SENT, ALU.mult, ALU.add))
                    S("vector", lambda e: e.tensor_tensor(
                        idxf[:], idxf[:], sa[:], ALU.add))
                    S("vector", lambda e, ii=idxi: e.tensor_copy(ii[:],
                                                                 idxf[:]))
                    S("vector", lambda e, cwx=cwx: e.tensor_tensor(
                        cwx[:], cwx[:], any_[:], ALU.mult))

                rank_pass(kept0, None, idx0i, cw0, dump=True)
                rank_pass(kept1, myk0, idx1i, cw1)

                if debug:
                    S("sync", lambda e: e.dma_start(dbgmap["dtab"], stab[:]))
                    S("vector", lambda e: e.tensor_copy(
                        spay[0:1, 0:8], counts[:]))
                    S("vector", lambda e: e.tensor_copy(
                        spay[0:1, 8:16], kbrow[:]))
                    S("vector", lambda e: e.tensor_copy(
                        spay[0:1, 16:24], myk0[:]))
                    S("sync", lambda e: e.dma_start(
                        dbgmap["drows"][0:1, 0:32], spay[:]))
                    S("sync", lambda e: e.dma_start(
                        dbgmap["drows"][0:1, 32:64], myrows[:]))
                    S("sync", lambda e: e.dma_start(dbgmap["dcw0"], cw0[:]))
                    S("sync", lambda e: e.dma_start(dbgmap["dcw1"], cw1[:]))
                    S("vector", lambda e: e.tensor_copy(sa[:], idx0i[:]))
                    S("sync", lambda e: e.dma_start(dbgmap["didx0"], sa[:]))
                    S("vector", lambda e: e.tensor_copy(sa[:], idx1i[:]))
                    S("sync", lambda e: e.dma_start(dbgmap["didx1"], sa[:]))
                    S("sync", lambda e: e.dma_start(
                        dbgmap["dlg"], lgp[:].rearrange("p a b -> p (a b)")))
                ph.run()

        def phase_scatter():
            with ExitStack() as pc:
                ph = Phase(nc, kb)
                lt = [pc.enter_context(nc.sbuf_tensor(f"sct{i}", [128, D],
                                                      BF16))
                      for i in range(2)]
                lv = lntm.rearrange("(n p) d -> n p d", p=128)
                xl_ld = []
                gp_done = []
                bcr = {}
                ph.emit("gpsimd", lambda e: bcr.__setitem__(
                    "r", e.to_reg(E * CAP - 1)))
                for i in range(NT):
                    w = []
                    if i >= 2:
                        w.append((kb.SC[i % 2], gp_done[i - 2]))
                    ph.emit("sync", lambda e, i=i: e.dma_start(
                        lt[i % 2][:], lv[i]), waits=w,
                        inc=kb.XL[i % 2], inck=16)
                    xl_ld.append(kb.XL[i % 2].n)
                    ph.emit("gpsimd", lambda e, i=i: e.indirect_dma_start(
                        stage, IOA(idx0i[:, i:i + 1], 0), lt[i % 2][:], None,
                        bounds_check=bcr["r"], oob_is_err=False),
                        waits=[(kb.XL[i % 2], xl_ld[i])],
                        inc=kb.SC[i % 2], inck=16)
                    ph.emit("gpsimd", lambda e, i=i: e.indirect_dma_start(
                        stage, IOA(idx1i[:, i:i + 1], 0), lt[i % 2][:], None,
                        bounds_check=bcr["r"], oob_is_err=False),
                        inc=kb.SC[i % 2], inck=16)
                    gp_done.append(kb.SC[i % 2].n)
                ph.run()

        def phase_expert():
            with ExitStack() as pc:
                def psb(n, s, d):
                    return pc.enter_context(nc.sbuf_tensor(n, s, d))

                def pps(n, s):
                    return pc.enter_context(nc.psum_tensor(n, s, F32))

                ph = Phase(nc, kb)
                b1r = psb("b1r", [1, H], BF16)
                b2rb = psb("b2rb", [1, D], BF16)
                w1sb = psb("w1sb", [128, FC, HCN, 128], BF16)
                w2sb = psb("w2sb", [128, HCN, DCN, 128], BF16)
                sfm = [psb(f"sfm{i}", [128, FC, 512], BF16) for i in range(2)]
                hsb = psb("hsb", [128, HCN, 512], BF16)
                u2 = psb("u2", [128, 512], F32)
                t2 = psb("t2", [128, 512], F32)
                sg = psb("sg", [128, 512], F32)
                yt = [psb(f"yt{i}", [128, 512], BF16) for i in range(2)]
                ytt = [psb(f"ytt{i}", [128, D], BF16) for i in range(2)]
                pH = [pps("pH0", [128, 512]), pps("pH1", [128, 512])]
                pW = [pps("pW0", [128, 512]), pps("pW1", [128, 512])]
                ccr = next_cc()
                ccy = next_cc()

                ph.emit("gpsimd", lambda e: e.dma_start(
                    b1r[:], b1.unsqueeze(0)), inc=GP, inck=16)
                ph.emit("gpsimd", lambda e: e.dma_start(
                    b2rb[:], b2.unsqueeze(0)), inc=GP, inck=16)
                w1v = w1.rearrange("(f p) (hc j) -> p f hc j", p=128, j=128)
                for f in range(FC):
                    ph.emit("gpsimd", lambda e, f=f: e.dma_start(
                        w1sb[:, f, :, :], w1v[:, f, :, :]), inc=GP, inck=16)
                w2v = w2.rearrange("(hc p) (dc j) -> p hc dc j", p=128, j=128)
                for q in range(4):
                    ph.emit("gpsimd", lambda e, q=q: e.dma_start(
                        w2sb[:, q * 8:(q + 1) * 8, :, :],
                        w2v[:, q * 8:(q + 1) * 8, :, :]), inc=GP, inck=16)
                gpw = GP.n
                ph.emit("gpsimd", lambda e: e.collective_compute(
                    "ReduceScatter", ALU.add, replica_groups=[list(range(8))],
                    ins=[stage.opt()], outs=[rsout.opt()]), inc=ccr)

                te0, ve0, se0 = TE.n, VE.n, SE.n
                te_h_end = {}
                te_w_end = {}
                ld_tr = {}
                se_evac = {}
                yt_store = {}

                for g in range(4):
                    for f in range(FC):
                        w = []
                        if g == 0 and f == 0:
                            w.append((ccr, 1))
                        if g >= 2 and f == 0:
                            w.append((TE, te_h_end[(g - 2, HCN - 1)]))
                        ph.emit("sync", lambda e, g=g, f=f: e.dma_start(
                            sfm[g % 2][:, f, :],
                            rsout[g * 512:(g + 1) * 512,
                                  f * 128:(f + 1) * 128],
                            transpose=True), waits=w,
                            inc=kb.XS[g % 2], inck=16)
                    ld_tr[g] = kb.XS[g % 2].n

                    for hc in range(HCN):
                        k = g * HCN + hc
                        w = [(kb.XS[g % 2], ld_tr[g])]
                        if k == 0:
                            w.append((GP, gpw))
                        if k >= 2:
                            w.append((VE, ve0 + 3 * (k - 1)))
                        if g >= 1 and hc == 0:
                            pass

                        def hmms(e, g=g, hc=hc):
                            p = pH[hc % 2]
                            e.matmul(p[:],
                                     b1r[0:1, hc * 128:(hc + 1) * 128],
                                     onesrowb[:], start=True, stop=False)
                            for f in range(FC):
                                mm = e.matmul(p[:], w1sb[:, f, hc, :],
                                              sfm[g % 2][:, f, :],
                                              start=False,
                                              stop=(f == FC - 1))
                            return mm
                        ph.emit("tensor", hmms, waits=w, inc=TE)
                        te_h_end[(g, hc)] = TE.n
                        ph.emit("scalar", lambda e, hc=hc: e.activation(
                            u2[:], pH[hc % 2][:], AF.Square),
                            waits=[(TE, TE.n)] +
                            ([(VE, ve0 + 3 * (k - 1) + 1)] if k >= 1 else []),
                            inc=SE)
                        wv = [(SE, SE.n)]
                        if g >= 1 and hc == 0:
                            wv.append((TE, te_w_end[g - 1]))
                        ph.emit("vector", lambda e: e.tensor_scalar(
                            u2[:], u2[:], GELU_A, 1.0, ALU.mult, ALU.add),
                            waits=wv)
                        ph.emit("vector", lambda e, hc=hc: e.tensor_tensor(
                            t2[:], u2[:], pH[hc % 2][:], ALU.mult), inc=VE)
                        ph.emit("scalar", lambda e: e.activation(
                            sg[:], t2[:], AF.Sigmoid, scale=GELU_2C),
                            waits=[(VE, VE.n)], inc=SE)
                        ph.emit("vector", lambda e, hc=hc: e.tensor_tensor(
                            t2[:], sg[:], pH[hc % 2][:], ALU.mult),
                            waits=[(SE, SE.n)], inc=VE)
                        ph.emit("vector", lambda e, hc=hc: e.tensor_copy(
                            hsb[:, hc, :], t2[:]), inc=VE)
                    ve_g_end = VE.n

                    for dc in range(DCN):
                        w = [(VE, ve_g_end)]
                        if (g, dc) in se_evac:
                            pass
                        prev = (g, dc - 2) if dc >= 2 else \
                            ((g - 1, dc + DCN - 2) if g >= 1 else None)
                        if prev is not None and prev in se_evac:
                            w.append((SE, se_evac[prev]))

                        def wmms(e, g=g, dc=dc):
                            p = pW[dc % 2]
                            e.matmul(p[:],
                                     b2rb[0:1, dc * 128:(dc + 1) * 128],
                                     onesrowb[:], start=True, stop=False)
                            for hc in range(HCN):
                                mm = e.matmul(p[:], w2sb[:, hc, dc, :],
                                              hsb[:, hc, :],
                                              start=False,
                                              stop=(hc == HCN - 1))
                            return mm
                        ph.emit("tensor", wmms, waits=w, inc=TE)
                        te_w_end[g] = TE.n
                        wv = [(TE, TE.n)]
                        if (g * DCN + dc) >= 2:
                            pk = (g, dc - 2) if dc >= 2 else \
                                (g - 1, dc + DCN - 2)
                            wv.append((kb.LNS[dc % 2], yt_store[pk]))
                        ph.emit("scalar", lambda e, dc=dc: e.copy(
                            yt[dc % 2][:], pW[dc % 2][:]), waits=wv, inc=SE)
                        se_evac[(g, dc)] = SE.n
                        ph.emit("sync", lambda e, g=g, dc=dc: e.dma_start(
                            yfm[dc * 128:(dc + 1) * 128,
                                g * 512:(g + 1) * 512],
                            yt[dc % 2][:]), waits=[(SE, SE.n)],
                            inc=kb.LNS[dc % 2], inck=16)
                        yt_store[(g, dc)] = kb.LNS[dc % 2].n

                lns_all = [(kb.LNS[0], kb.LNS[0].n), (kb.LNS[1], kb.LNS[1].n)]
                tr_val, st_val = {}, {}
                for kt in range(16):
                    w = list(lns_all)
                    if kt >= 2:
                        w.append((kb.TS[kt % 2], st_val[kt - 2]))
                    ph.emit("sync", lambda e, kt=kt: e.dma_start(
                        ytt[kt % 2][:], yfm[:, kt * 128:(kt + 1) * 128],
                        transpose=True), waits=w, inc=kb.TT[kt % 2], inck=16)
                    tr_val[kt] = kb.TT[kt % 2].n
                    ph.emit("sync", lambda e, kt=kt: e.dma_start(
                        yagin[kt * 128:(kt + 1) * 128, :], ytt[kt % 2][:]),
                        waits=[(kb.TT[kt % 2], tr_val[kt])],
                        inc=kb.TS[kt % 2], inck=16)
                    st_val[kt] = kb.TS[kt % 2].n
                ts_all = [(kb.TS[0], kb.TS[0].n), (kb.TS[1], kb.TS[1].n)]
                if DEBUG:
                    ph.emit("sync", lambda e: e.dma_start(
                        dbgmap["drs"], rsout), waits=[(ccr, 1)],
                        inc=LD, inck=16)
                    ph.emit("sync", lambda e: e.dma_start(
                        dbgmap["dyag"], yagin), waits=list(ts_all),
                        inc=LD, inck=16)
                ph.emit("gpsimd", lambda e: e.collective_compute(
                    "AllGather", ALU.bypass, replica_groups=[list(range(8))],
                    ins=[yagin.opt()], outs=[yall.opt()]),
                    waits=list(ts_all), inc=ccy)
                ph.emit("gpsimd", lambda e: e.nop(), waits=[(ccy, 1)])
                ph.run()

        SCALE = 1.0 / float(np.sqrt(96.0))
        CUSH = 20.0

        def phase_attn():
            with ExitStack() as pco:
                def osb(n, sh, d):
                    return pco.enter_context(nc.sbuf_tensor(n, sh, d))

                def ops(n, sh):
                    return pco.enter_context(nc.psum_tensor(n, sh, F32))

                ph = Phase(nc, kb)
                b = KBx()
                b.qc = osb("aqc", [128, 4, T], BF16)
                b.kvn = osb("akvn", [128, 2, T], BF16)
                b.Kh = osb("aKh", [128, 8, T], BF16)
                b.Vh = osb("aVh", [128, 16, 8, 65], BF16)
                b.ao = osb("aao", [128, 8, T], BF16)
                b.Qh = osb("aQh", [128, T], BF16)
                b.pT = [osb(f"apT{i}", [128, 512], BF16) for i in range(2)]
                b.scr = osb("ascr", [128, 512], F32)
                b.scr2 = osb("ascr2", [128, 512], F32)
                b.scr64 = b.scr[0:64, :]
                b.qsr = osb("aqsr", [1, T], F32)
                b.ksr = osb("aksr", [1, 4], F32)
                b.ksmax = osb("aksmax", [1, 1], F32)
                b.m16 = osb("am16", [1, 16], F32)
                b.mrowb = osb("amrowb", [1, T], BF16)
                b.lrow = osb("alrow", [1, 512], F32)
                b.wqdecsb = osb("awqdec", [128, 4, D], BF16)
                b.wqropesb = osb("awqrope", [128, 4, 512], BF16)
                b.pS = [ops("apS0", [128, 512]), ops("apS1", [128, 512])]
                b.pAO = ops("apAO", [65, 512])
                b.pB = ops("apB", [128, 512])
                b.pR1 = ops("apR1", [1, 512])
                b.pR1b = ops("apR1b", [1, 512])

                ph.seq("gpsimd", lambda e: e.dma_start(
                    b.wqdecsb[:],
                    wqdec.rearrange("(f p) j -> p f j", p=128)))
                ph.seq("gpsimd", lambda e: e.dma_start(
                    b.wqropesb[:],
                    wqropep.rearrange("(f p) j -> p f j", p=128)))

                for s_ in range(2):
                    _attn_seq(ph, s_, b)
                sv = stage.rearrange("(g p) d -> g p d", p=128)
                for g in range(E * CAP // 128):
                    ph.emit("gpsimd", lambda e, g=g:
                            e.dma_start(sv[g], ztile[:]),
                            inc=kb.SC[0], inck=16)
                ph.run()

        def _attn_seq(ph, s, b):
            S = ph.seq
            tok0 = s * T

            with ExitStack() as pa:
                def asb(n, sh, d):
                    return pa.enter_context(nc.sbuf_tensor(n, sh, d))

                af = asb(f"aaf{s}", [128, FC, 512], BF16)
                wkvsb = asb(f"awkv{s}", [128, FC, KVR], BF16)
                wkropesb = asb(f"awkrope{s}", [128, FC, 512], BF16)
                wqsb = asb(f"awq{s}", [128, FC, QR], BF16)
                krraw = asb(f"akrraw{s}", [128, 4, 512], BF16)
                qrraw = asb(f"aqrraw{s}", [128, 4, 512], BF16)
                rstage = asb(f"arstage{s}", [128, 4, 512], BF16)
                cosb = asb(f"acosb{s}", [128, T], BF16)
                sinb = asb(f"asinb{s}", [128, T], BF16)
                t1 = asb(f"at1{s}", [128, 512], F32)
                t2a = asb(f"at2a{s}", [128, 512], F32)
                rstdr = asb(f"arstdr{s}", [1, 512], F32)
                pPJ = pa.enter_context(nc.psum_tensor(f"apPJ{s}",
                                                      [128, 512], F32))

                S("gpsimd", lambda e: e.dma_start(cosb[:], ccosr))
                S("gpsimd", lambda e: e.dma_start(sinb[:], csinr))
                S("gpsimd", lambda e: e.dma_start(
                    wkvsb[:], wkv.rearrange("(f p) j -> p f j", p=128)))
                S("gpsimd", lambda e: e.dma_start(
                    wkropesb[:],
                    wkropep.rearrange("(f p) j -> p f j", p=128)))
                S("gpsimd", lambda e: e.dma_start(
                    wqsb[:], wq.rearrange("(f p) j -> p f j", p=128)))
                gpA = ph.last

                def rope(c0, raw, dstA, dstB, cA, cB):
                    cs = slice(c0, c0 + 512)
                    S("vector", lambda e: e.tensor_tensor(
                        t1[:], raw[:, cA, :], cosb[:, cs], ALU.mult))
                    S("vector", lambda e: e.tensor_tensor(
                        t2a[:], raw[:, cB, :], sinb[:, cs], ALU.mult))
                    S("vector", lambda e: e.tensor_tensor(
                        t1[:], t1[:], t2a[:], ALU.subtract))
                    S("vector", lambda e: e.tensor_tensor(
                        t2a[:], raw[:, cB, :], cosb[:, cs], ALU.mult))
                    S("vector", lambda e: e.tensor_copy(dstA, t1[:]))
                    S("vector", lambda e: e.tensor_tensor(
                        t1[:], raw[:, cA, :], sinb[:, cs], ALU.mult))
                    S("vector", lambda e: e.tensor_tensor(
                        t2a[:], t2a[:], t1[:], ALU.add))
                    S("vector", lambda e: e.tensor_copy(dstB, t2a[:]))

                for tg in range(4):
                    c0 = tg * 512
                    cs = slice(c0, c0 + 512)
                    for f in range(FC):
                        S("sync", lambda e, f=f, c0=c0: e.dma_start(
                            af[:, f, :],
                            lntm[tok0 + c0:tok0 + c0 + 512,
                                 f * 128:(f + 1) * 128], transpose=True))

                    def mmgroup(wsb, oc, wid=128):
                        def g(e, oc=oc):
                            for f in range(FC):
                                mm = e.matmul(
                                    pPJ[0:wid, :],
                                    wsb[:, f, oc * wid:(oc + 1) * wid],
                                    af[:, f, :],
                                    start=(f == 0), stop=(f == FC - 1))
                            return mm
                        return g

                    for oc in range(2):
                        S("tensor", mmgroup(wkvsb, oc),
                          extra_waits=[gpA])
                        S("scalar", lambda e, oc=oc, cs=cs: e.copy(
                            b.kvn[:, oc, cs], pPJ[:]))
                    S("scalar", lambda e, cs=cs: e.activation(
                        b.scr[:], b.kvn[:, 0, cs], AF.Square))
                    S("tensor", lambda e: e.matmul(
                        b.pR1[:], onescol[:], b.scr[:], start=True,
                        stop=False))
                    S("scalar", lambda e, cs=cs: e.activation(
                        b.scr[:], b.kvn[:, 1, cs], AF.Square))
                    S("tensor", lambda e: e.matmul(
                        b.pR1[:], onescol[:], b.scr[:], start=False,
                        stop=True))
                    S("vector", lambda e: e.tensor_scalar(
                        rstdr[:], b.pR1[:], 1.0 / KVR, 1e-6, ALU.mult,
                        ALU.add))
                    S("scalar", lambda e: e.activation(rstdr[:], rstdr[:],
                                                       AF.Ln))
                    S("scalar", lambda e: e.activation(rstdr[:], rstdr[:],
                                                       AF.Exp, scale=-0.5))
                    S("tensor", lambda e: e.matmul(
                        b.pB[:], ones128[:], rstdr[:], start=True,
                        stop=True))
                    for oc in range(2):
                        S("vector", lambda e, oc=oc, cs=cs: e.tensor_tensor(
                            t1[:], b.kvn[:, oc, cs], b.pB[:], ALU.mult))
                        S("scalar", lambda e, oc=oc, cs=cs: e.activation(
                            b.kvn[:, oc, cs], t1[:], AF.Copy,
                            scale=gkvc[:, oc:oc + 1]))
                    for oc in range(4):
                        S("tensor", mmgroup(wqsb, oc))
                        S("scalar", lambda e, oc=oc, cs=cs: e.copy(
                            b.qc[:, oc, cs], pPJ[:]))
                    if DEBUG and s == 0 and tg == 0:
                        S("sync", lambda e: e.dma_start(
                            dbgmap["dqc2"],
                            b.qc[:].rearrange("p a c -> p (a c)")))
                        S("sync", lambda e: e.dma_start(
                            dbgmap["dwq"],
                            wqsb[:].rearrange("p a c -> p (a c)")))
                        S("sync", lambda e: e.dma_start(
                            dbgmap["dwkv"],
                            wkvsb[:].rearrange("p a c -> p (a c)")))
                    for oc in range(4):
                        S("tensor", mmgroup(wkropesb, oc))
                        S("scalar", lambda e, oc=oc: e.copy(
                            krraw[:, oc, :], pPJ[:]))
                    for cp in range(2):
                        rope(c0, krraw, rstage[:, cp, :],
                             rstage[:, cp + 2, :], cp, cp + 2)
                    for ch in range(4):
                        S("sync", lambda e, ch=ch, c0=c0: e.dma_start(
                            kroped[ch * 128:(ch + 1) * 128, c0:c0 + 512],
                            rstage[:, ch, :]))
                    if DEBUG and s == 0 and tg == 0:
                        S("sync", lambda e: e.dma_start(
                            dbgmap["dqcF"],
                            b.qc[:].rearrange("p a c -> p (a c)")))
                    for oc in range(4):
                        def qrmm(e, oc=oc, cs=cs):
                            for f in range(4):
                                mm = e.matmul(
                                    pPJ[:],
                                    b.wqropesb[:, f,
                                               oc * 128:(oc + 1) * 128],
                                    b.qc[:, f, cs],
                                    start=(f == 0), stop=(f == 3))
                            return mm
                        S("tensor", qrmm)
                        S("scalar", lambda e, oc=oc: e.copy(
                            qrraw[:, oc, :], pPJ[:]))
                    for cp in range(2):
                        rope(c0, qrraw, rstage[:, cp, :],
                             rstage[:, cp + 2, :], cp, cp + 2)
                    for ch in range(4):
                        S("sync", lambda e, ch=ch, c0=c0: e.dma_start(
                            qroped[ch * 128:(ch + 1) * 128, c0:c0 + 512],
                            rstage[:, ch, :]))

            if DEBUG and s == 0:
                S("sync", lambda e: e.dma_start(
                    dbgmap["dqc"], b.qc[:].rearrange("p a c -> p (a c)")))
                S("sync", lambda e: e.dma_start(dbgmap["dkrop"], kroped))
                S("sync", lambda e: e.dma_start(dbgmap["dqrop"], qroped))

            for hh in range(2):
                with ExitStack() as pb_:
                    def bsb(n, sh, d):
                        return pb_.enter_context(nc.sbuf_tensor(n, sh, d))

                    wkdecsb = bsb(f"awkdec{s}{hh}", [128, 2, 512], BF16)
                    wvdecsb = bsb(f"awvdec{s}{hh}", [128, 2, 512], BF16)
                    vttp = [bsb(f"avtt{s}{hh}{i}", [128, 512], BF16)
                            for i in range(2)]
                    pPJ2 = [pb_.enter_context(nc.psum_tensor(
                        f"apPJ2{s}{hh}{i}", [64, 512], F32))
                        for i in range(2)]

                    S("gpsimd", lambda e, hh=hh: e.dma_start(
                        wkdecsb[:],
                        wkdec.rearrange("(f p) j -> p f j", p=128)
                        [:, :, hh * 512:(hh + 1) * 512]))
                    S("gpsimd", lambda e, hh=hh: e.dma_start(
                        wvdecsb[:],
                        wvdec.rearrange("(f p) j -> p f j", p=128)
                        [:, :, hh * 512:(hh + 1) * 512]))
                    gpB = ph.last

                    S("vector", lambda e: e.memset(b.Kh[:], 0.0))
                    S("vector", lambda e: e.memset(b.Qh[:], 0.0))
                    S("vector", lambda e: e.memset(b.Vh[:], 1.0))
                    join0 = ph.last
                    se_pa = SE.n
                    for hl in range(8):
                        ph.emit("gpsimd", lambda e, hl=hl: e.dma_start(
                            b.Kh[17:18, hl, :], conesb2048),
                            waits=[join0], inc=GP, inck=16)
                    pjev = {0: None, 1: None}
                    last_vcp = None
                    vst = {0: None, 1: None}
                    pj = 0
                    vv = 0
                    for hl in range(8):
                        h = hh * 8 + hl
                        for tg in range(4):
                            c0 = tg * 512
                            bk = pj % 2
                            w = [] if pjev[bk] is None else \
                                [(SE, pjev[bk])]
                            if pj < 2:
                                w += [gpB, (SE, se_pa)]

                            def kdmm(e, hl=hl, c0=c0, bk=bk):
                                for f in range(2):
                                    mm = e.matmul(
                                        pPJ2[bk][:],
                                        wkdecsb[:, f,
                                                hl * 64:(hl + 1) * 64],
                                        b.kvn[:, f, c0:c0 + 512],
                                        start=(f == 0), stop=(f == 1))
                                return mm
                            ph.emit("tensor", kdmm, waits=w, inc=TE)
                            ph.emit("scalar",
                                    lambda e, hl=hl, c0=c0, bk=bk: e.copy(
                                        b.Kh[64:128, hl, c0:c0 + 512],
                                        pPJ2[bk][:]),
                                    waits=[(TE, TE.n)], inc=SE)
                            pjev[bk] = SE.n
                            pj += 1
                        ph.emit("gpsimd", lambda e, h=h, hl=hl: e.dma_start(
                            b.Kh[0:16, hl, :],
                            kroped[(h // 8) * 128 + (h % 8) * 16:
                                   (h // 8) * 128 + (h % 8) * 16 + 16, :]),
                            waits=[join0], inc=GP, inck=16)
                        ph.emit("gpsimd", lambda e, h=h, hl=hl: e.dma_start(
                            b.Kh[32:48, hl, :],
                            kroped[256 + (h // 8) * 128 + (h % 8) * 16:
                                   256 + (h // 8) * 128 + (h % 8) * 16 + 16,
                                   :]),
                            waits=[join0], inc=GP, inck=16)
                        for tg in range(4):
                            c0 = tg * 512
                            bk = pj % 2
                            vb = vv % 2
                            w = [] if pjev[bk] is None else \
                                [(SE, pjev[bk])]

                            def vdmm(e, hl=hl, c0=c0, bk=bk):
                                for f in range(2):
                                    mm = e.matmul(
                                        pPJ2[bk][:],
                                        wvdecsb[:, f,
                                                hl * 64:(hl + 1) * 64],
                                        b.kvn[:, f, c0:c0 + 512],
                                        start=(f == 0), stop=(f == 1))
                                return mm
                            ph.emit("tensor", vdmm, waits=w, inc=TE)
                            wv = [(TE, TE.n)]
                            if last_vcp is not None:
                                wv.append((VE, last_vcp))
                            ph.emit("scalar", lambda e, bk=bk: e.copy(
                                b.scr64, pPJ2[bk][:]), waits=wv, inc=SE)
                            pjev[bk] = SE.n
                            wc = [(SE, SE.n)]
                            if vst[vb] is not None:
                                wc.append((kb.TS[vb], vst[vb]))
                            ph.emit("vector", lambda e, vb=vb: e.tensor_copy(
                                vttp[vb][0:64, :], b.scr64),
                                waits=wc, inc=VE)
                            last_vcp = VE.n
                            ph.emit("sync",
                                    lambda e, hl=hl, c0=c0, vb=vb:
                                    e.dma_start(
                                        vfmd[hl * 64:(hl + 1) * 64,
                                             c0:c0 + 512],
                                        vttp[vb][0:64, :]),
                                    waits=[(VE, VE.n)],
                                    inc=kb.TS[vb], inck=16)
                            vst[vb] = kb.TS[vb].n
                            pj += 1
                            vv += 1
                    ts_fin = [(kb.TS[0], kb.TS[0].n),
                              (kb.TS[1], kb.TS[1].n)]
                    trv = {}
                    vhc = {0: None, 1: None}
                    for kt in range(16):
                        vb = kt % 2
                        w = list(ts_fin)
                        if vhc[vb] is not None:
                            w.append((VE, vhc[vb]))
                        ph.emit("sync", lambda e, kt=kt, vb=vb: e.dma_start(
                            vttp[vb][:],
                            vfmd[0:512, kt * 128:(kt + 1) * 128],
                            transpose=True), waits=w,
                            inc=kb.TT[vb], inck=16)
                        trv[kt] = kb.TT[vb].n
                        ph.emit("vector", lambda e, kt=kt, vb=vb:
                                e.tensor_copy(
                                    b.Vh[:, kt, :, 0:64],
                                    vttp[vb][:].rearrange(
                                        "p (h d) -> p h d", d=64)),
                                waits=[(kb.TT[vb], trv[kt])], inc=VE)
                        vhc[vb] = VE.n
                    ph.emit("gpsimd", lambda e: e.nop(),
                            waits=[(TE, TE.n), (SE, SE.n), (VE, VE.n),
                                   (GP, GP.n)] + ts_fin +
                            [(kb.TT[0], kb.TT[0].n),
                             (kb.TT[1], kb.TT[1].n)],
                            inc=kb.PLS)

                    for hl in range(8):
                        _attn_head(ph, s, hh * 8 + hl, hl, b)

            with ExitStack() as pw:
                wosb = pw.enter_context(nc.sbuf_tensor(
                    f"awosb{s}", [128, 8, 8, 128], BF16))
                cfm = [pw.enter_context(nc.sbuf_tensor(
                    f"acfm{s}{i}", [128, 512], BF16)) for i in range(2)]
                pWO = [pw.enter_context(nc.psum_tensor(
                    f"apWO{s}{i}", [128, 512], F32)) for i in range(2)]
                S("gpsimd", lambda e: e.dma_start(
                    wosb[:],
                    wo.rearrange("(pr p) (dc j) -> p pr dc j", p=128,
                                 j=128)))
                gpW = ph.last
                se_heads = SE.n
                wo_ev = {0: None, 1: None}
                wo_st = {0: None, 1: None}
                for qg in range(4):
                    for dc in range(DCN):
                        bk = dc % 2
                        w = [gpW, (SE, se_heads)] if wo_ev[bk] is None \
                            else [(SE, wo_ev[bk])]

                        def womm(e, qg=qg, dc=dc, bk=bk):
                            for pr in range(8):
                                mm = e.matmul(
                                    pWO[bk][:], wosb[:, pr, dc, :],
                                    b.ao[:, pr, qg * 512:(qg + 1) * 512],
                                    start=(pr == 0), stop=(pr == 7))
                            return mm
                        ph.emit("tensor", womm, waits=w, inc=TE)
                        wv = [(TE, TE.n)]
                        if wo_st[bk] is not None:
                            wv.append((kb.TS[bk], wo_st[bk]))
                        ph.emit("scalar", lambda e, bk=bk: e.copy(
                            cfm[bk][:], pWO[bk][:]), waits=wv, inc=SE)
                        wo_ev[bk] = SE.n
                        ph.emit("sync", lambda e, qg=qg, dc=dc, bk=bk:
                                e.dma_start(
                                    contribfm[dc * 128:(dc + 1) * 128,
                                              tok0 + qg * 512:
                                              tok0 + qg * 512 + 512],
                                    cfm[bk][:]),
                                waits=[(SE, SE.n)],
                                inc=kb.TS[bk], inck=16)
                        wo_st[bk] = kb.TS[bk].n
                ph.emit("gpsimd", lambda e: e.nop(),
                        waits=[(TE, TE.n), (SE, SE.n),
                               (kb.TS[0], kb.TS[0].n),
                               (kb.TS[1], kb.TS[1].n)],
                        inc=kb.PLS)

        def _attn_head(ph, s, h, hl, b):
            S = ph.seq
            hd0 = ph.last
            qd_ev = {0: None, 1: None}
            for tg in range(4):
                c0 = tg * 512
                bk = tg % 2
                w = [hd0] if qd_ev[bk] is None else [(SE, qd_ev[bk])]

                def qdmm(e, c0=c0, bk=bk):
                    for f in range(4):
                        mm = e.matmul(
                            b.pS[bk][0:64, :],
                            b.wqdecsb[:, f, h * 64:(h + 1) * 64],
                            b.qc[:, f, c0:c0 + 512],
                            start=(f == 0), stop=(f == 3))
                    return mm
                ph.emit("tensor", qdmm, waits=w, inc=TE)
                ph.emit("scalar", lambda e, c0=c0, bk=bk: e.copy(
                    b.Qh[64:128, c0:c0 + 512], b.pS[bk][0:64, :]),
                    waits=[(TE, TE.n)], inc=SE)
                qd_ev[bk] = SE.n
            S("gpsimd", lambda e: e.dma_start(
                b.Qh[0:16, :],
                qroped[(h // 8) * 128 + (h % 8) * 16:
                       (h // 8) * 128 + (h % 8) * 16 + 16, :]))
            S("gpsimd", lambda e: e.dma_start(
                b.Qh[32:48, :],
                qroped[256 + (h // 8) * 128 + (h % 8) * 16:
                       256 + (h // 8) * 128 + (h % 8) * 16 + 16, :]))
            stats0 = ph.last
            mmQv = mmKv = krv = None
            for tg in range(4):
                c0 = tg * 512
                wq_ = [stats0] if tg == 0 else [(TE, mmQv)]
                ph.emit("scalar", lambda e, c0=c0: e.activation(
                    b.scr[:], b.Qh[:, c0:c0 + 512], AF.Square),
                    waits=wq_, inc=SE)
                ph.emit("tensor", lambda e: e.matmul(
                    b.pR1[:], qmaskc[:], b.scr[:], start=True, stop=True),
                    waits=[(SE, SE.n)], inc=TE)
                mmQv = TE.n
                ph.emit("scalar", lambda e, c0=c0: e.copy(
                    b.qsr[0:1, c0:c0 + 512], b.pR1[:]),
                    waits=[(TE, mmQv)], inc=SE)
                wk_ = [] if mmKv is None else [(TE, mmKv)]
                ph.emit("scalar", lambda e, c0=c0, hl=hl: e.activation(
                    b.scr2[:], b.Kh[:, hl, c0:c0 + 512], AF.Square),
                    waits=wk_, inc=SE)
                ph.emit("tensor", lambda e: e.matmul(
                    b.pR1b[:], qmaskc[:], b.scr2[:], start=True, stop=True),
                    waits=[(SE, SE.n)], inc=TE)
                mmKv = TE.n
                wl_ = [(TE, mmKv)] + ([] if krv is None else [(VE, krv)])
                ph.emit("scalar", lambda e: e.copy(b.lrow[:], b.pR1b[:]),
                        waits=wl_, inc=SE)
                ph.emit("vector", lambda e, tg=tg: e.reduce_max(
                    b.ksr[0:1, tg:tg + 1].unsqueeze(2),
                    b.lrow[:].unsqueeze(1), axis=AX.X),
                    waits=[(SE, SE.n)], inc=VE)
                krv = VE.n
            S("vector", lambda e: e.reduce_max(
                b.m16[:].unsqueeze(2),
                b.qsr[:].rearrange("o (a c) -> o a c", a=16), axis=AX.X),
              extra_waits=[(SE, SE.n)])
            S("vector", lambda e: e.reduce_max(
                b.ksmax[:].unsqueeze(2), b.ksr[:].unsqueeze(1), axis=AX.X))
            S("vector", lambda e: e.tensor_scalar(
                b.m16[:], b.m16[:], b.ksmax[0:1, 0:1], None, ALU.mult))
            S("scalar", lambda e: e.activation(b.m16[:], b.m16[:], AF.Ln))
            S("scalar", lambda e: e.activation(b.m16[:], b.m16[:], AF.Exp,
                                               scale=0.5))
            S("vector", lambda e: e.tensor_scalar(
                b.m16[:], b.m16[:], 1.0, CUSH, ALU.mult, ALU.add))
            S("vector", lambda e: e.tensor_scalar(
                b.mrowb[:].rearrange("o (a c) -> o a c", c=128),
                b.m16[:].unsqueeze(2).broadcast_to([1, 16, 128]),
                -1.0, None, ALU.mult))
            S("gpsimd", lambda e: e.dma_start(b.Qh[17:18, :], b.mrowb[:]))

            pao_free = None
            ao_mult = None
            hd0 = ph.last
            for qg in range(4):
                kts = list(range(4 * qg + 3, 4 * qg - 1, -1)) + \
                    list(range(0, 4 * qg))
                n = len(kts)
                exp_done = {}
                av_done = {}

                def emit_smm(j):
                    kt = kts[j]
                    wdt = 512 if kt < 4 * qg else \
                        512 - (kt - 4 * qg) * 128
                    q0 = qg * 512 + (512 - wdt)
                    wts = []
                    if j >= 2:
                        wts.append((SE, exp_done[j - 2]))

                    def smm(e, kt=kt, wdt=wdt, q0=q0, j=j):
                        return e.matmul(
                            b.pS[j % 2][:, 512 - wdt:],
                            b.Kh[:, hl, kt * 128:(kt + 1) * 128],
                            b.Qh[:, q0:q0 + wdt], start=True, stop=True)
                    ph.emit("tensor", smm, waits=wts, inc=TE)
                    smm_val = TE.n
                    if kt >= 4 * qg:
                        ph.emit("vector", lambda e, wdt=wdt, j=j:
                                e.tensor_tensor(
                                    b.pS[j % 2][:, 512 - wdt:512 - wdt + 128],
                                    b.pS[j % 2][:, 512 - wdt:512 - wdt + 128],
                                    trimaskb[:], ALU.add),
                                waits=[(TE, smm_val)], inc=VE)
                        wexp = [(VE, VE.n)]
                    else:
                        wexp = [(TE, smm_val)]
                    if j >= 2:
                        wexp.append((TE, av_done[j - 2]))

                    def expf(e, wdt=wdt, j=j):
                        return e.activation(
                            b.pT[j % 2][:, 512 - wdt:],
                            b.pS[j % 2][:, 512 - wdt:],
                            AF.Exp, scale=SCALE)
                    ph.emit("scalar", expf, waits=wexp, inc=SE)
                    exp_done[j] = SE.n

                def emit_av(j):
                    kt = kts[j]
                    wdt = 512 if kt < 4 * qg else \
                        512 - (kt - 4 * qg) * 128

                    def avmm(e, kt=kt, wdt=wdt, j=j):
                        return e.matmul(
                            b.pAO[:, 512 - wdt:], b.Vh[:, kt, hl, :],
                            b.pT[j % 2][:, 512 - wdt:],
                            start=False, stop=False)
                    ph.emit("tensor", avmm,
                            waits=[(SE, exp_done[j])], inc=TE)
                    av_done[j] = TE.n

                ph.emit("tensor", lambda e: e.matmul(
                    b.pAO[:, :], ztile[0:1, 0:65], ztile[0:1, 0:512],
                    start=True, stop=False),
                    waits=[pao_free if pao_free is not None else hd0],
                    inc=TE)
                emit_smm(0)
                for j in range(1, n):
                    emit_smm(j)
                    emit_av(j - 1)
                emit_av(n - 1)
                ph.emit("tensor", lambda e: e.matmul(
                    b.pAO[:, :], ztile[0:1, 0:65], ztile[0:1, 0:512],
                    start=False, stop=True), inc=TE)
                te_stop = TE.n
                S("scalar", lambda e: e.copy(b.lrow[:], b.pAO[64:65, :]),
                  extra_waits=[(TE, te_stop)])
                S("vector", lambda e: e.reciprocal(b.lrow[:], b.lrow[:]))
                S("tensor", lambda e: e.matmul(
                    b.pB[:], ones128[:], b.lrow[:], start=True, stop=True))
                S("scalar", lambda e: e.copy(b.scr64, b.pAO[0:64, :]))
                S("vector", lambda e: e.tensor_tensor(
                    b.scr64, b.scr64, b.pB[0:64, :], ALU.mult))
                S("scalar", lambda e, qg=qg: e.copy(
                    b.ao[(h % 2) * 64:(h % 2) * 64 + 64, h // 2,
                         qg * 512:(qg + 1) * 512], b.scr64))
                pao_free = (SE, SE.n)
            if DEBUG and s == 0 and h == 0:
                S("sync", lambda e: e.dma_start(dbgmap["dm16"], b.m16[:]))
                S("sync", lambda e: e.dma_start(dbgmap["dqsr"], b.qsr[:]))
                S("sync", lambda e: e.dma_start(dbgmap["dksr"], b.ksr[:]))
                S("sync", lambda e: e.dma_start(dbgmap["dqh"], b.Qh[:]))
                S("sync", lambda e: e.dma_start(dbgmap["dkh"],
                                                b.Kh[:, 0, :]))
                S("sync", lambda e: e.dma_start(
                    dbgmap["dao0"], b.ao[:, 0, :]))
                S("sync", lambda e: e.dma_start(
                    dbgmap["dkvn"],
                    b.kvn[:].rearrange("p a c -> p (a c)")))

        plist = [
            phase_init,
            lambda: phase_boundary("1", x_in, None, write_ln=True),
            phase_gating,
            lambda: phase_routing(debug=DEBUG),
            phase_scatter,
            phase_expert,
        ]
        if STAGE == 1:
            plist.append(lambda: phase_boundary("2", x_in, out,
                                                write_ln=False, gather=True))
        elif STAGE == 2:
            plist += [
                lambda: phase_boundary("2", x_in, xdram, write_ln=True,
                                       gather=True),
                phase_attn,
                lambda: phase_boundary("3", xdram, out, write_ln=False,
                                       attn=True),
            ]
        else:
            plist += [
                lambda: phase_boundary("2", x_in, xdram, write_ln=True,
                                       gather=True),
                phase_attn,
                lambda: phase_boundary("3", xdram, xdram, write_ln=True,
                                       attn=True),
                phase_gating,
                lambda: phase_routing(),
                phase_scatter,
                phase_expert,
                lambda: phase_boundary("4", xdram, out, write_ln=False,
                                       gather=True),
            ]
        for i, p in enumerate(plist):
            if i < nph:
                p()

    return nc


def _consts(cid):
    ut = np.triu(np.ones((128, 128), np.float32))
    sut_split = np.zeros((32, 32), np.float32)
    for i in range(32):
        for k in range(32):
            if (k < i) and ((k < 16) == (i < 16)):
                sut_split[k, i] = 1.0
    sut_cont = np.triu(np.ones((32, 32), np.float32), 1)
    sut8 = np.triu(np.ones((8, 8), np.float32), 1)
    indA = np.zeros((1, 32), np.float32)
    indA[0, :16] = 1
    indB = np.zeros((1, 32), np.float32)
    indB[0, 16:] = 1
    ltc = (np.arange(8) < cid).astype(np.float32).reshape(8, 1)
    myh = np.zeros((8, 1), np.float32)
    myh[cid, 0] = 1
    tri = np.where(np.arange(128)[:, None] <= np.arange(128)[None, :],
                   0.0, -1.0e30).astype(np.float32)
    inv_freq = 1.0 / 10000.0 ** (np.arange(0, RD, 2, dtype=np.float32) / RD)
    t = np.arange(T, dtype=np.float32)
    freqs = np.outer(t, inv_freq)
    cosr = np.tile(np.cos(freqs).T, (8, 1)).astype(np.float32)
    sinr = np.tile(np.sin(freqs).T, (8, 1)).astype(np.float32)
    qmask = np.ones((128, 1), np.float32)
    qmask[16:32] = 0
    qmask[48:64] = 0
    return dict(
        cut128=ut, cident=np.eye(128, dtype=np.float32), csuts=sut_split, csutc=sut_cont, csut8=sut8,
        cones128=np.ones((1, 128), np.float32),
        conescol=np.ones((128, 1), np.float32),
        cindAr=indA, cindBr=indB,
        cindAc=np.ascontiguousarray(indA.reshape(32, 1)),
        cindBc=np.ascontiguousarray(indB.reshape(32, 1)),
        cones32r=np.ones((1, 32), np.float32),
        cones8c=np.ones((8, 1), np.float32),
        cones8r=np.ones((1, 8), np.float32),
        cltc=ltc, cmyhot=myh,
        cebase=(np.arange(8, dtype=np.float32) * CAP).reshape(1, 8),
        conesrowb=np.ones((1, 512), ml_dtypes.bfloat16),
        ctrimask=tri, ccosr=cosr, csinr=sinr, cqmask=qmask,
        conesb2048=np.ones((1, T), ml_dtypes.bfloat16),
    )


def _rope_perm():
    idx = np.zeros(NH * RD, np.int64)
    for h in range(NH):
        for r in range(RD):
            old = h * RD + r
            if r < 16:
                new = (h // 8) * 128 + (h % 8) * 16 + r
            else:
                new = 256 + (h // 8) * 128 + (h % 8) * 16 + (r - 16)
            idx[new] = old
    return idx


def kernel(**inputs):
    f32 = np.float32
    x = np.ascontiguousarray(np.asarray(inputs["L0_attn_x"], f32))
    perm = _rope_perm()
    shared = dict(
        wg=np.ascontiguousarray(np.asarray(inputs["L0_ffn_Wg"], f32)),
        wkv=np.ascontiguousarray(np.asarray(inputs["L0_attn_W_kv"], f32)),
        gkv=np.ascontiguousarray(np.asarray(inputs["L0_attn_g_kv"], f32)),
        wkdec=np.ascontiguousarray(np.asarray(inputs["L0_attn_W_kdec"], f32)),
        wvdec=np.ascontiguousarray(np.asarray(inputs["L0_attn_W_vdec"], f32)),
        wq=np.ascontiguousarray(np.asarray(inputs["L0_attn_W_q"], f32)),
        wqdec=np.ascontiguousarray(np.asarray(inputs["L0_attn_W_qdec"], f32)),
        wkropep=np.ascontiguousarray(
            np.asarray(inputs["L0_attn_W_krope"], f32)[:, perm]),
        wqropep=np.ascontiguousarray(
            np.asarray(inputs["L0_attn_W_qrope"], f32)[:, perm]),
        wo=np.ascontiguousarray(np.asarray(inputs["L0_attn_W_o"], f32)),
    )
    W1 = np.asarray(inputs["L0_ffn_W1"], f32)
    B1 = np.asarray(inputs["L0_ffn_b1"], f32)
    W2 = np.asarray(inputs["L0_ffn_W2"], f32)
    B2 = np.asarray(inputs["L0_ffn_b2"], f32)

    import os
    nc = build(int(os.environ.get("KNP", "99")))
    in_maps = []
    for c in range(8):
        shard = np.ascontiguousarray(np.concatenate([x[c], x[c + 8]], axis=0))
        m = dict(shared)
        m.update(_consts(c))
        m.update(x=shard,
                 w1=np.ascontiguousarray(W1[c]),
                 b1=np.ascontiguousarray(B1[c]),
                 w2=np.ascontiguousarray(W2[c]),
                 b2=np.ascontiguousarray(B2[c]))
        in_maps.append(m)
    res = run_bass_kernel_spmd(nc, in_maps, core_ids=list(range(8)))
    o = np.empty((B, T, D), f32)
    for c in range(8):
        y = np.asarray(res.results[c]["out"], f32)
        o[c] = y[:T]
        o[c + 8] = y[T:]
    if DEBUG:
        kernel.dbg = [{k: np.asarray(v) for k, v in r.items()
                       if k.startswith("d")} for r in res.results]
    return o


# version 23 (final)
# speedup vs baseline: 1.2780x; 1.0013x over previous
import numpy as np
import ml_dtypes
from contextlib import ExitStack
import concourse.bass as bass
import concourse.mybir as mybir
from concourse.bass_utils import run_bass_kernel_spmd

F32 = mybir.dt.float32
BF16 = mybir.dt.bfloat16
I32 = mybir.dt.int32
AF = mybir.ActivationFunctionType
ALU = mybir.AluOpType
AX = mybir.AxisListType
IOA = bass.IndirectOffsetOnAxis

STAGE = 3
DEBUG = False

B, T, D = 16, 2048, 1024
TOK, NT = 2 * T, 32
E, H, CAP = 8, 4096, 2048
NH, HD, RD, QR, KVR = 16, 64, 32, 512, 256
FC, HCN, DCN = 8, 32, 8
SENT = 100000.0
GELU_A = 0.044715
GELU_2C = 2.0 * 0.7978845608028654
ENGS = ("sync", "tensor", "vector", "scalar", "gpsimd")


class Sem:
    def __init__(self, ctx, nc, name):
        self.h = ctx.enter_context(nc.semaphore(name))
        self.n = 0


class Phase:

    def __init__(self, nc, kb):
        self.nc = nc
        self.kb = kb
        self.items = {g: [] for g in ENGS}
        self.last = None

    def emit(self, eng, fn, waits=(), inc=None, inck=1):
        val = None
        if inc is not None:
            inc.n += inck
            val = inc.n
            self.last = (inc, val)
        self.items[eng].append((list(waits), fn, inc, inck, val))
        return self.last

    def seq(self, eng, fn, extra_waits=()):
        w = list(extra_waits)
        if self.last is not None:
            w.append(self.last)
        sem = {"sync": self.kb.LD, "gpsimd": self.kb.GP,
               "tensor": self.kb.TE, "vector": self.kb.VE,
               "scalar": self.kb.SE}[eng]
        inck = 16 if eng in ("sync", "gpsimd") else 1
        return self.emit(eng, fn, waits=w, inc=sem, inck=inck)

    def run(self):
        nc, kb = self.nc, self.kb
        drains = [(kb.LD, kb.LD.n)] + [(s, s.n) for s in
                  getattr(kb, "DRAINS", ()) if s.n]
        self.items["sync"].append((drains, None, None, 0, None))
        gdrains = [(kb.GP, kb.GP.n)] + [(s, s.n) for s in
                   getattr(kb, "SDRAINS", ()) if s.n]
        self.items["gpsimd"].append((gdrains, None, None, 0, None))
        with nc.Block() as bk:
            def mk(eng):
                auto = kb.AUTO[eng] if eng in ("vector", "scalar") else None

                def body(e):
                    prev = None
                    for waits, fn, sem, inck, val in self.items[eng]:
                        for (ws, wv) in waits:
                            e.wait_ge(ws.h, wv)
                        if fn is not None:
                            if auto is not None and prev is not None:
                                e.wait_ge(prev[0].h, prev[1])
                            inst = fn(e)
                            if inst is None:
                                if sem is not None and inck > 0:
                                    e.nop().then_inc(sem.h, inck)
                                continue
                            if sem is not None and inck > 0:
                                inst.then_inc(sem.h, inck)
                                if auto is not None:
                                    prev = (sem, val)
                            elif auto is not None:
                                auto.n += 1
                                inst.then_inc(auto.h, 1)
                                prev = (auto, auto.n)
                        elif sem is not None and inck > 0:
                            e.nop().then_inc(sem.h, inck)
                return body
            for eng in ENGS:
                if self.items[eng]:
                    getattr(bk, eng)(mk(eng))


class KBx:
    pass


def build(nph=99):
    nc = bass.Bass("TRN2", target_bir_lowering=False)
    dt_ = nc.dram_tensor

    _names = {}
    _osb, _ops = nc.sbuf_tensor, nc.psum_tensor

    def _uniq(name):
        i = _names.get(name, 0)
        _names[name] = i + 1
        return f"{name}_r{i}" if i else name

    nc.sbuf_tensor = lambda name, *a, **k: _osb(_uniq(name), *a, **k)
    nc.psum_tensor = lambda name, *a, **k: _ops(_uniq(name), *a, **k)

    x_in = dt_("x", [TOK, D], F32, kind="ExternalInput").ap()
    out = dt_("out", [TOK, D], F32, kind="ExternalOutput").ap()
    wg = dt_("wg", [D, E], F32, kind="ExternalInput").ap()
    w1 = dt_("w1", [D, H], F32, kind="ExternalInput").ap()
    b1 = dt_("b1", [H], F32, kind="ExternalInput").ap()
    w2 = dt_("w2", [H, D], F32, kind="ExternalInput").ap()
    b2 = dt_("b2", [D], F32, kind="ExternalInput").ap()
    wkv = dt_("wkv", [D, KVR], F32, kind="ExternalInput").ap()
    gkv = dt_("gkv", [KVR], F32, kind="ExternalInput").ap()
    wkdec = dt_("wkdec", [KVR, D], F32, kind="ExternalInput").ap()
    wvdec = dt_("wvdec", [KVR, D], F32, kind="ExternalInput").ap()
    wq = dt_("wq", [D, QR], F32, kind="ExternalInput").ap()
    wqdec = dt_("wqdec", [QR, D], F32, kind="ExternalInput").ap()
    wkropep = dt_("wkropep", [D, NH * RD], F32, kind="ExternalInput").ap()
    wqropep = dt_("wqropep", [QR, NH * RD], F32, kind="ExternalInput").ap()
    wo = dt_("wo", [D, D], F32, kind="ExternalInput").ap()
    cut128 = dt_("cut128", [128, 128], F32, kind="ExternalInput").ap()
    cident = dt_("cident", [128, 128], F32, kind="ExternalInput").ap()
    csuts = dt_("csuts", [32, 32], F32, kind="ExternalInput").ap()
    csutc = dt_("csutc", [32, 32], F32, kind="ExternalInput").ap()
    csut8 = dt_("csut8", [8, 8], F32, kind="ExternalInput").ap()
    cones128 = dt_("cones128", [1, 128], F32, kind="ExternalInput").ap()
    conescol = dt_("conescol", [128, 1], F32, kind="ExternalInput").ap()
    cindAr = dt_("cindAr", [1, 32], F32, kind="ExternalInput").ap()
    cindBr = dt_("cindBr", [1, 32], F32, kind="ExternalInput").ap()
    cindAc = dt_("cindAc", [32, 1], F32, kind="ExternalInput").ap()
    cindBc = dt_("cindBc", [32, 1], F32, kind="ExternalInput").ap()
    cones32r = dt_("cones32r", [1, 32], F32, kind="ExternalInput").ap()
    cones8c = dt_("cones8c", [8, 1], F32, kind="ExternalInput").ap()
    cones8r = dt_("cones8r", [1, 8], F32, kind="ExternalInput").ap()
    cltc = dt_("cltc", [8, 1], F32, kind="ExternalInput").ap()
    cmyhot = dt_("cmyhot", [8, 1], F32, kind="ExternalInput").ap()
    cebase = dt_("cebase", [1, 8], F32, kind="ExternalInput").ap()
    conesrowb = dt_("conesrowb", [1, 512], BF16, kind="ExternalInput").ap()
    ctrimask = dt_("ctrimask", [128, 128], F32, kind="ExternalInput").ap()
    ccosr = dt_("ccosr", [128, T], F32, kind="ExternalInput").ap()
    csinr = dt_("csinr", [128, T], F32, kind="ExternalInput").ap()
    cqmask = dt_("cqmask", [128, 1], F32, kind="ExternalInput").ap()
    conesb2048 = dt_("conesb2048", [1, T], BF16, kind="ExternalInput").ap()

    xdram = dt_("xdram", [TOK, D], F32).ap()
    lntm = dt_("lntm", [TOK, D], BF16).ap()
    lnlo = dt_("lnlo", [TOK, D], BF16).ap()
    stage = dt_("stg", [E * CAP, D], BF16).ap()
    rsout = dt_("rsout", [CAP, D], BF16).ap()
    yfm = dt_("yfm", [D, CAP], BF16).ap()
    yagin = dt_("yagin", [CAP, D], BF16).ap()
    yall = dt_("yall", [E * CAP, D], BF16, addr_space="Shared").ap()
    aghin = dt_("aghin", [1, 32], F32).ap()
    aghout = dt_("aghout", [8, 32], F32, addr_space="Shared").ap()
    bco = dt_("bco", [32, 8], F32).ap()
    bcoflat = bco.rearrange("c e -> (c e)").unsqueeze(0)
    contribfm = dt_("contribfm", [D, TOK], BF16).ap()
    kroped = dt_("kroped", [512, T], BF16).ap()
    qroped = dt_("qroped", [512, T], BF16).ap()
    vfmd = dt_("vfmd", [D, T], BF16).ap()
    mrowd = dt_("mrowd", [1, T], BF16).ap()

    dbgmap = {}
    if DEBUG:
        for nm in ["didx0", "didx1", "dcw0", "dcw1"]:
            dbgmap[nm] = dt_(nm, [128, 32], F32, kind="ExternalOutput").ap()
        dbgmap["dlg"] = dt_("dlg", [128, 256], F32, kind="ExternalOutput").ap()
        dbgmap["dtab"] = dt_("dtab", [8, 32], F32, kind="ExternalOutput").ap()
        dbgmap["drows"] = dt_("drows", [1, 64], F32, kind="ExternalOutput").ap()
        dbgmap["dsco"] = dt_("dsco", [32, 8], F32, kind="ExternalOutput").ap()
        dbgmap["dstk"] = dt_("dstk", [32, 8], F32, kind="ExternalOutput").ap()
        dbgmap["drs"] = dt_("drs", [CAP, D], BF16, kind="ExternalOutput").ap()
        dbgmap["dm16"] = dt_("dm16", [1, 16], F32, kind="ExternalOutput").ap()
        dbgmap["dqsr"] = dt_("dqsr", [1, T], F32, kind="ExternalOutput").ap()
        dbgmap["dksr"] = dt_("dksr", [1, 4], F32, kind="ExternalOutput").ap()
        dbgmap["dqh"] = dt_("dqh", [128, T], BF16, kind="ExternalOutput").ap()
        dbgmap["dkh"] = dt_("dkh", [128, T], BF16, kind="ExternalOutput").ap()
        dbgmap["dao0"] = dt_("dao0", [128, T], BF16, kind="ExternalOutput").ap()
        dbgmap["dpt"] = dt_("dpt", [128, 512], BF16, kind="ExternalOutput").ap()
        dbgmap["dkvn"] = dt_("dkvn", [128, 2 * T], BF16, kind="ExternalOutput").ap()
        dbgmap["dqc"] = dt_("dqc", [128, 4 * T], BF16, kind="ExternalOutput").ap()
        dbgmap["dqc2"] = dt_("dqc2", [128, 4 * T], BF16, kind="ExternalOutput").ap()
        dbgmap["dqcF"] = dt_("dqcF", [128, 4 * T], BF16, kind="ExternalOutput").ap()
        dbgmap["dwq"] = dt_("dwq", [128, FC * QR], BF16, kind="ExternalOutput").ap()
        dbgmap["dwkv"] = dt_("dwkv", [128, FC * KVR], BF16, kind="ExternalOutput").ap()
        dbgmap["dkrop"] = dt_("dkrop", [512, T], BF16, kind="ExternalOutput").ap()
        dbgmap["dqrop"] = dt_("dqrop", [512, T], BF16, kind="ExternalOutput").ap()
        dbgmap["dyag"] = dt_("dyag", [CAP, D], BF16, kind="ExternalOutput").ap()

    ctx = ExitStack()
    with ctx:
        def sb(name, shape, dtype):
            return ctx.enter_context(nc.sbuf_tensor(name, shape, dtype))

        kb = KBx()
        kb.LD = Sem(ctx, nc, "LD")
        kb.GP = Sem(ctx, nc, "GP")
        kb.TE = Sem(ctx, nc, "TE")
        kb.VE = Sem(ctx, nc, "VE")
        kb.SE = Sem(ctx, nc, "SE")
        kb.CC = [Sem(ctx, nc, f"CC{i}") for i in range(8)]
        kb.AUTO = {"vector": Sem(ctx, nc, "AUV"), "scalar": Sem(ctx, nc, "AUS")}
        kb.PLS = Sem(ctx, nc, "PLS")
        kb.WGH = Sem(ctx, nc, "WGH")
        kb.XS = [Sem(ctx, nc, "XS0"), Sem(ctx, nc, "XS1")]
        kb.LNS = [Sem(ctx, nc, "LNS0"), Sem(ctx, nc, "LNS1")]
        kb.XL = [Sem(ctx, nc, "XL0"), Sem(ctx, nc, "XL1")]
        kb.TT = [Sem(ctx, nc, "TT0"), Sem(ctx, nc, "TT1")]
        kb.TS = [Sem(ctx, nc, "TS0"), Sem(ctx, nc, "TS1")]
        kb.SC = [Sem(ctx, nc, "SC0"), Sem(ctx, nc, "SC1")]
        kb.DRAINS = kb.XS + kb.LNS + kb.XL + kb.TT + kb.TS + [kb.WGH]
        kb.SDRAINS = kb.SC
        kb.icc = 0
        LD, GP, TE, VE, SE = kb.LD, kb.GP, kb.TE, kb.VE, kb.SE

        def next_cc():
            s = kb.CC[kb.icc]
            kb.icc += 1
            return s

        ut128 = sb("ut128", [128, 128], F32)
        ident = sb("ident", [128, 128], F32)
        suts = sb("suts", [32, 32], F32)
        sutc = sb("sutc", [32, 32], F32)
        sut8 = sb("sut8", [8, 8], F32)
        ones128 = sb("ones128", [1, 128], F32)
        onescol = sb("onescol", [128, 1], F32)
        indAr = sb("indAr", [1, 32], F32)
        indBr = sb("indBr", [1, 32], F32)
        indAc = sb("indAc", [32, 1], F32)
        indBc = sb("indBc", [32, 1], F32)
        ones32r = sb("ones32r", [1, 32], F32)
        ones8c = sb("ones8c", [8, 1], F32)
        ones8r = sb("ones8r", [1, 8], F32)
        ltc = sb("ltc", [8, 1], F32)
        myhot = sb("myhot", [8, 1], F32)
        ebase = sb("ebase", [1, 8], F32)
        onesrowb = sb("onesrowb", [1, 512], BF16)
        wghi = sb("wghi", [128, FC, E], BF16)
        wglo = sb("wglo", [128, FC, E], BF16)
        gkvc = sb("gkvc", [128, 2], F32)
        trimaskb = sb("trimaskb", [128, 128], F32)
        ztile = sb("ztile", [128, D], BF16)
        qmaskc = sb("qmaskc", [128, 1], F32)
        idx0i = sb("idx0i", [128, 32], I32)
        idx1i = sb("idx1i", [128, 32], I32)
        cw0 = sb("cw0", [128, 32], F32)
        cw1 = sb("cw1", [128, 32], F32)
        lgp = sb("lgp", [128, 32, E], F32)

        def phase_init():
            with ExitStack() as pc:
                def psb(n, s, d):
                    return pc.enter_context(nc.sbuf_tensor(n, s, d))

                ph = Phase(nc, kb)
                wgf = psb("wgf", [128, FC, E], F32)
                gkvr = psb("gkvr", [2, 128], F32)
                gps = pc.enter_context(nc.psum_tensor("gps", [128, 2], F32))
                consts = [(ut128, cut128), (ident, cident),
                          (suts, csuts), (sutc, csutc),
                          (sut8, csut8), (ones128, cones128),
                          (onescol, conescol), (indAr, cindAr),
                          (indBr, cindBr), (indAc, cindAc), (indBc, cindBc),
                          (ones32r, cones32r), (ones8c, cones8c),
                          (ones8r, cones8r), (ltc, cltc), (myhot, cmyhot),
                          (ebase, cebase), (onesrowb, conesrowb)]
                for dst, src in consts:
                    ph.emit("sync", lambda e, d=dst, s=src:
                            e.dma_start(d[:], s), inc=LD, inck=16)
                ph.emit("sync", lambda e: e.dma_start(
                    wgf[:], wg.rearrange("(f p) e -> p f e", p=128)),
                    inc=LD, inck=16)
                ph.emit("sync", lambda e: e.dma_start(
                    trimaskb[:], ctrimask), inc=LD, inck=16)
                ph.emit("sync", lambda e: e.dma_start(
                    qmaskc[:], cqmask), inc=LD, inck=16)
                ph.emit("sync", lambda e: e.dma_start(
                    gkvr[:], gkv.rearrange("(c p) -> c p", p=128)),
                    inc=LD, inck=16)
                ldA = LD.n
                ph.emit("gpsimd", lambda e: e.dma_start(
                    wghi[:], wg.rearrange("(f p) e -> p f e", p=128)),
                    inc=kb.WGH, inck=16)
                wgA = kb.WGH.n

                ph.emit("gpsimd", lambda e: e.memset(ztile[:], 0.0),
                        inc=kb.PLS)
                ph.emit("gpsimd", lambda e: e.nop(),
                        waits=[(kb.PLS, kb.PLS.n)])
                sv = stage.rearrange("(g p) d -> g p d", p=128)
                for g in range(E * CAP // 128):
                    ph.emit("gpsimd", lambda e, g=g:
                            e.dma_start(sv[g], ztile[:]), inc=GP, inck=16)
                ph.emit("vector", lambda e: e.tensor_tensor(
                    wglo[:], wgf[:], wghi[:], ALU.subtract),
                    waits=[(LD, ldA), (kb.WGH, wgA)], inc=VE)
                ph.emit("tensor", lambda e: e.transpose(
                    gps[:], gkvr[:], ident[0:2, 0:2]),
                    waits=[(LD, ldA)], inc=TE)
                ph.emit("scalar", lambda e: e.copy(gkvc[:], gps[:]),
                        waits=[(TE, TE.n)], inc=SE)
                ph.run()

        def phase_boundary(tag, src_x, dst_x, write_ln, gather=False,
                           attn=False, gscale=0.5):
            with ExitStack() as pc:
                def psb(n, s, d):
                    return pc.enter_context(nc.sbuf_tensor(n, s, d))

                ph = Phase(nc, kb)
                xt = [psb(f"x{tag}{i}", [128, D], F32) for i in range(2)]
                ct = [psb(f"c{tag}{i}", [128, D], BF16) for i in range(2)] \
                    if attn else None
                lnf = psb(f"lf{tag}", [128, D], F32)
                lhi = [psb(f"lh{tag}{i}", [128, D], BF16) for i in range(2)]
                llo = [psb(f"ll{tag}{i}", [128, D], BF16) for i in range(2)]
                mean = psb(f"mn{tag}", [128, 1], F32)
                ssq = psb(f"sq{tag}", [128, 1], F32)
                rstd = psb(f"rs{tag}", [128, 1], F32)
                sqs = psb(f"sqs{tag}", [128, D], F32)
                if gather:
                    g0t = psb(f"g0{tag}", [128, D], F32)
                    g1t = psb(f"g1{tag}", [128, D], F32)
                xv = src_x.rearrange("(n p) d -> n p d", p=128)
                dv = dst_x.rearrange("(n p) d -> n p d", p=128) \
                    if dst_x is not None else None
                lv = lntm.rearrange("(n p) d -> n p d", p=128)
                lov = lnlo.rearrange("(n p) d -> n p d", p=128)

                nve = 4 if write_ln else 1
                ve0, se0, gp0 = VE.n, SE.n, GP.n
                xl_loads, xs_storex, lns_val = [], [], []

                for i in range(NT):
                    w = []
                    if i >= 2:
                        w.append((VE, ve0 + nve * (i - 1)))
                        if dst_x is not None:
                            w.append((kb.XS[i % 2], xs_storex[i - 2]))
                    ph.emit("sync", lambda e, i=i: e.dma_start(
                        xt[i % 2][:], xv[i]), waits=w,
                        inc=kb.XL[i % 2], inck=16)
                    if attn:
                        ph.emit("sync", lambda e, i=i: e.dma_start(
                            ct[i % 2][:], contribfm[:, i * 128:(i + 1) * 128],
                            transpose=True), inc=kb.XL[i % 2], inck=16)
                    xl_loads.append(kb.XL[i % 2].n)
                    if dst_x is not None:
                        ph.emit("sync", lambda e, i=i: e.dma_start(
                            dv[i], xt[i % 2][:]),
                            waits=[(VE, ve0 + nve * i + 1)],
                            inc=kb.XS[i % 2], inck=16)
                    xs_storex.append(kb.XS[i % 2].n)
                    if write_ln:
                        ph.emit("sync", lambda e, i=i: e.dma_start(
                            lv[i], lhi[i % 2][:]),
                            waits=[(VE, ve0 + nve * i + 3)],
                            inc=kb.LNS[i % 2], inck=16)
                        ph.emit("sync", lambda e, i=i: e.dma_start(
                            lov[i], llo[i % 2][:]),
                            waits=[(VE, ve0 + nve * i + 4)],
                            inc=kb.LNS[i % 2], inck=16)
                    lns_val.append(kb.LNS[i % 2].n)

                if gather:
                    bcr = {}
                    ph.emit("gpsimd", lambda e: bcr.__setitem__(
                        "r", e.to_reg(E * CAP - 1)))
                    ph.emit("gpsimd", lambda e: e.memset(g0t[:], 0.0))
                    ph.emit("gpsimd", lambda e: e.memset(g1t[:], 0.0),
                            inc=kb.PLS)
                    ph.emit("gpsimd", lambda e: e.nop(),
                            waits=[(kb.PLS, kb.PLS.n)])
                    for i in range(NT):
                        w = []
                        if i >= 1:
                            w.append((VE, ve0 + nve * (i - 1) + 1))
                        ph.emit("gpsimd", lambda e, i=i: e.indirect_dma_start(
                            g0t[:], None, yall, IOA(idx0i[:, i:i + 1], 0),
                            bounds_check=bcr["r"], oob_is_err=False),
                            waits=w, inc=GP, inck=16)
                        ph.emit("gpsimd", lambda e, i=i: e.indirect_dma_start(
                            g1t[:], None, yall, IOA(idx1i[:, i:i + 1], 0),
                            bounds_check=bcr["r"], oob_is_err=False),
                            inc=GP, inck=16)

                for i in range(NT):
                    p = i % 2
                    w = [(kb.XL[i % 2], xl_loads[i])]
                    if gather:
                        w.append((GP, gp0 + 32 * (i + 1)))
                    if write_ln and i >= 2:
                        w.append((kb.LNS[i % 2], lns_val[i - 2]))
                    if gather:
                        ph.emit("vector", lambda e, i=i: e.tensor_scalar(
                            g1t[:], g1t[:], cw1[:, i:i + 1], None, ALU.mult),
                            waits=w)
                        ph.emit("vector", lambda e, i=i:
                                e.scalar_tensor_tensor(
                                    g0t[:], g0t[:], cw0[:, i:i + 1], g1t[:],
                                    ALU.mult, ALU.add))
                        ph.emit("vector", lambda e, p=p:
                                e.scalar_tensor_tensor(
                                    xt[p][:], g0t[:], gscale, xt[p][:],
                                    ALU.mult, ALU.add), inc=VE)
                    elif attn:
                        ph.emit("vector", lambda e, p=p: e.tensor_tensor(
                            xt[p][:], xt[p][:], ct[p][:], ALU.add),
                            waits=w, inc=VE)
                    else:
                        ph.emit("vector", lambda e: e.nop(), waits=w, inc=VE)
                    if write_ln:
                        ph.emit("vector", lambda e, p=p: e.reduce_sum(
                            mean[:].unsqueeze(1), xt[p][:].unsqueeze(1),
                            axis=AX.X))
                        ph.emit("vector", lambda e: e.tensor_scalar_mul(
                            mean[:], mean[:], 1.0 / D))
                        ph.emit("vector", lambda e: e.tensor_tensor(
                            rstd[:], mean[:], mean[:], ALU.mult),
                            waits=[(SE, se0 + 2 * i + 1)])
                        ph.emit("vector", lambda e: e.scalar_tensor_tensor(
                            rstd[:], ssq[:], 1.0 / D, rstd[:],
                            ALU.mult, ALU.subtract))
                        ph.emit("vector", lambda e: e.tensor_scalar_add(
                            rstd[:], rstd[:], 1e-5), inc=VE)
                        ph.emit("vector", lambda e, p=p: e.tensor_scalar(
                            lnf[:], xt[p][:], mean[:, 0:1], None,
                            ALU.subtract), waits=[(SE, se0 + 2 * i + 2)])
                        ph.emit("vector", lambda e: e.tensor_scalar(
                            lnf[:], lnf[:], rstd[:, 0:1], None, ALU.mult))
                        ph.emit("vector", lambda e, p=p: e.tensor_copy(
                            lhi[p][:], lnf[:]), inc=VE)
                        ph.emit("vector", lambda e, p=p: e.tensor_tensor(
                            llo[p][:], lnf[:], lhi[p][:], ALU.subtract),
                            inc=VE)

                if write_ln:
                    for i in range(NT):
                        p = i % 2
                        ph.emit("scalar", lambda e, p=p: e.activation(
                            sqs[:], xt[p][:], AF.Square, accum_out=ssq[:]),
                            waits=[(VE, ve0 + nve * i + 1)], inc=SE)
                        ph.emit("scalar", lambda e: e.activation(
                            rstd[:], rstd[:], AF.Ln),
                            waits=[(VE, ve0 + nve * i + 2)])
                        ph.emit("scalar", lambda e: e.activation(
                            rstd[:], rstd[:], AF.Exp, scale=-0.5), inc=SE)
                ph.run()

        def phase_gating():
            with ExitStack() as pc:
                def psb(n, s, d):
                    return pc.enter_context(nc.sbuf_tensor(n, s, d))

                ph = Phase(nc, kb)
                lfh = psb("glfh", [128, FC, TOK], BF16)
                lfl = psb("glfl", [128, FC, TOK], BF16)
                plg = [pc.enter_context(nc.psum_tensor(f"gpl{i}", [128, E],
                                                       F32))
                       for i in range(2)]
                te0, ve0 = TE.n, VE.n
                for f in range(FC):
                    ph.emit("sync", lambda e, f=f: e.dma_start(
                        lfh[:, f, :], lntm[:, f * 128:(f + 1) * 128],
                        transpose=True), inc=LD, inck=16)
                    ph.emit("sync", lambda e, f=f: e.dma_start(
                        lfl[:, f, :], lnlo[:, f * 128:(f + 1) * 128],
                        transpose=True), inc=LD, inck=16)
                ldall = LD.n

                def mmtile(e, c, p):
                    sl = slice(c * 128, (c + 1) * 128)
                    for f in range(FC):
                        e.matmul(p[:], lfh[:, f, sl], wghi[:, f, :],
                                 start=(f == 0), stop=False)
                        e.matmul(p[:], lfh[:, f, sl], wglo[:, f, :],
                                 start=False, stop=False)
                        mm = e.matmul(p[:], lfl[:, f, sl], wghi[:, f, :],
                                      start=False, stop=(f == FC - 1))
                    return mm

                for c in range(NT):
                    w = [(LD, ldall)] if c == 0 else []
                    if c >= 2:
                        w.append((VE, ve0 + (c - 1)))
                    ph.emit("tensor", lambda e, c=c: mmtile(e, c, plg[c % 2]),
                            waits=w, inc=TE)
                    ph.emit("vector", lambda e, c=c: e.tensor_copy(
                        lgp[:, c, :], plg[c % 2][:]),
                        waits=[(TE, te0 + c + 1)], inc=VE)
                ph.run()

        def phase_routing(debug=False):
            with ExitStack() as pc:
                def psb(n, s, d):
                    return pc.enter_context(nc.sbuf_tensor(n, s, d))

                def pps(n, s):
                    return pc.enter_context(nc.psum_tensor(n, s, F32))

                ph = Phase(nc, kb)
                S = ph.seq
                m1 = psb("rm1", [128, 32], F32)
                sml = psb("rsml", [128, 32, E], F32)
                sp = psb("rsp", [128, 32, E], F32)
                Z = psb("rZ", [128, 32], F32)
                eq = psb("req", [128, 32, E], F32)
                oh1 = psb("roh1", [128, 32, E], F32)
                oh2 = psb("roh2", [128, 32, E], F32)
                fnd = psb("rfnd", [128, 32], F32)
                gg = psb("rgg", [128, 32], F32)
                e2 = psb("re2", [128, 32], F32)
                den = psb("rden", [128, 32], F32)
                sUT = psb("rsUT", [128, 32, E], F32)
                spre = psb("rspre", [128, 32, E], F32)
                kept0 = psb("rkept0", [128, 32, E], F32)
                kept1 = psb("rkept1", [128, 32, E], F32)
                srow = psb("rsrow", [1, 256], F32)
                sTc = psb("rsTc", [32, 8], F32)
                sTc2 = psb("rsTc2", [32, 8], F32)
                sTk = psb("rsTk", [32, 8], F32)
                spay = psb("rspay", [1, 32], F32)
                stab = psb("rstab", [8, 32], F32)
                sP1A = psb("rsP1A", [8, 8], F32)
                sP1B = psb("rsP1B", [8, 8], F32)
                sP2A = psb("rsP2A", [8, 8], F32)
                sP2B = psb("rsP2B", [8, 8], F32)
                stot = psb("rstot", [1, 24], F32)
                counts = psb("rcounts", [1, 8], F32)
                counts8 = psb("rcounts8", [8, 8], F32)
                kbrow = psb("rkbrow", [1, 8], F32)
                myk0 = psb("rmyk0", [1, 8], F32)
                myrows = psb("rmyrows", [1, 32], F32)
                k88 = psb("rk88", [8, 8], F32)
                ktot = psb("rktot", [8, 8], F32)
                sco = psb("rsco", [32, 8], F32)
                erow = psb("rerow", [1, 8], F32)
                any_ = psb("rany", [128, 32], F32)
                sa = psb("rsa", [128, 32], F32)
                idxf = psb("ridxf", [128, 32], F32)
                pBIG = pps("rpBIG", [128, 256])
                pUT = pps("rpUT", [128, 256])
                pROW = pps("rpROW", [1, 256])
                pSM = pps("rpSM", [32, 32])
                cc = next_cc()

                m1b = m1[:].unsqueeze(2).broadcast_to([128, 32, E])
                e2b = e2[:].unsqueeze(2).broadcast_to([128, 32, E])
                oh1f = oh1[:].rearrange("p a b -> p (a b)")
                oh2f = oh2[:].rearrange("p a b -> p (a b)")

                S("vector", lambda e: e.reduce_max(
                    m1[:].rearrange("p (c o) -> p c o", o=1), lgp[:],
                    axis=AX.X))
                S("vector", lambda e: e.tensor_tensor(
                    sml[:], lgp[:], m1b, ALU.subtract))
                S("scalar", lambda e: e.activation(
                    sp[:].rearrange("p a b -> p (a b)"),
                    sml[:].rearrange("p a b -> p (a b)"), AF.Exp))
                S("vector", lambda e: e.reduce_sum(
                    Z[:].rearrange("p (c o) -> p c o", o=1), sp[:],
                    axis=AX.X))
                S("vector", lambda e: e.tensor_tensor(
                    eq[:], lgp[:], m1b, ALU.is_equal))
                S("vector", lambda e: e.memset(fnd[:], 0.0))
                for ee in range(E):
                    S("vector", lambda e: e.tensor_scalar(
                        gg[:], fnd[:], -1.0, 1.0, ALU.mult, ALU.add))
                    S("vector", lambda e, ee=ee: e.tensor_tensor(
                        oh1[:, :, ee], eq[:, :, ee], gg[:], ALU.mult))
                    S("vector", lambda e, ee=ee: e.tensor_tensor(
                        fnd[:], fnd[:], oh1[:, :, ee], ALU.add))
                S("vector", lambda e: e.scalar_tensor_tensor(
                    sml[:], oh1[:], -2.0, sp[:], ALU.mult, ALU.add))
                S("vector", lambda e: e.reduce_max(
                    e2[:].rearrange("p (c o) -> p c o", o=1), sml[:],
                    axis=AX.X))
                S("vector", lambda e: e.tensor_tensor(
                    eq[:], sml[:], e2b, ALU.is_equal))
                S("vector", lambda e: e.memset(fnd[:], 0.0))
                for ee in range(E):
                    S("vector", lambda e: e.tensor_scalar(
                        gg[:], fnd[:], -1.0, 1.0, ALU.mult, ALU.add))
                    S("vector", lambda e, ee=ee: e.tensor_tensor(
                        oh2[:, :, ee], eq[:, :, ee], gg[:], ALU.mult))
                    S("vector", lambda e, ee=ee: e.tensor_tensor(
                        fnd[:], fnd[:], oh2[:, :, ee], ALU.add))
                S("vector", lambda e: e.tensor_scalar(
                    den[:], Z[:], 1e-9, 1.0, ALU.mult, ALU.add))
                S("vector", lambda e: e.tensor_tensor(
                    den[:], den[:], e2[:], ALU.add))
                S("vector", lambda e: e.reciprocal(cw0[:], den[:]))
                S("vector", lambda e: e.tensor_tensor(
                    cw1[:], e2[:], cw0[:], ALU.mult))

                S("tensor", lambda e: e.matmul(pROW[:], onescol[:], oh1f,
                                               start=True, stop=True))
                S("vector", lambda e: e.tensor_copy(srow[:], pROW[:]))
                S("sync", lambda e: e.dma_start(bcoflat, srow[:]))
                S("sync", lambda e: e.dma_start(sTc[:], bco))
                S("tensor", lambda e: e.matmul(pROW[:], onescol[:], oh2f,
                                               start=True, stop=True))
                S("vector", lambda e: e.tensor_copy(srow[:], pROW[:]))
                S("sync", lambda e: e.dma_start(bcoflat, srow[:]))
                S("sync", lambda e: e.dma_start(sTc2[:], bco))
                S("tensor", lambda e: e.matmul(pROW[0:1, 0:8], indAc[:],
                                               sTc[:], start=True, stop=True))
                S("tensor", lambda e: e.matmul(pROW[0:1, 8:16], indBc[:],
                                               sTc[:], start=True, stop=True))
                S("tensor", lambda e: e.matmul(pROW[0:1, 16:24], indAc[:],
                                               sTc2[:], start=True,
                                               stop=True))
                S("tensor", lambda e: e.matmul(pROW[0:1, 24:32], indBc[:],
                                               sTc2[:], start=True,
                                               stop=True))
                S("vector", lambda e: e.tensor_copy(spay[:],
                                                    pROW[0:1, 0:32]))
                S("sync", lambda e: e.dma_start(aghin, spay[:]))
                ph.emit("gpsimd", lambda e: e.collective_compute(
                    "AllGather", ALU.bypass, replica_groups=[list(range(8))],
                    ins=[aghin.opt()], outs=[aghout.opt()]),
                    waits=[ph.last], inc=cc)
                S("sync", lambda e: e.dma_start(stab[:], aghout))

                S("tensor", lambda e: e.matmul(pSM[0:8, 0:8], sut8[:],
                                               stab[:, 0:8], start=True,
                                               stop=True))
                S("vector", lambda e: e.tensor_copy(sP1A[:], pSM[0:8, 0:8]))
                S("tensor", lambda e: e.matmul(pSM[0:1, 0:8], ones8c[:],
                                               stab[:, 0:8], start=True,
                                               stop=True))
                S("tensor", lambda e: e.matmul(pSM[0:1, 8:16], ones8c[:],
                                               stab[:, 8:16], start=True,
                                               stop=True))
                S("tensor", lambda e: e.matmul(pSM[0:1, 16:24], ones8c[:],
                                               stab[:, 16:24], start=True,
                                               stop=True))
                S("vector", lambda e: e.tensor_copy(stot[:],
                                                    pSM[0:1, 0:24]))
                S("tensor", lambda e: e.matmul(pSM[0:8, 0:8], sut8[:],
                                               stab[:, 8:16], start=True,
                                               stop=False))
                S("tensor", lambda e: e.matmul(pSM[0:8, 0:8], ones8r[:],
                                               stot[0:1, 0:8], start=False,
                                               stop=True))
                S("vector", lambda e: e.tensor_copy(sP1B[:], pSM[0:8, 0:8]))
                S("tensor", lambda e: e.matmul(pSM[0:8, 0:8], sut8[:],
                                               stab[:, 16:24], start=True,
                                               stop=True))
                S("vector", lambda e: e.tensor_copy(sP2A[:], pSM[0:8, 0:8]))
                S("tensor", lambda e: e.matmul(pSM[0:8, 0:8], sut8[:],
                                               stab[:, 24:32], start=True,
                                               stop=False))
                S("tensor", lambda e: e.matmul(pSM[0:8, 0:8], ones8r[:],
                                               stot[0:1, 16:24], start=False,
                                               stop=True))
                S("vector", lambda e: e.tensor_copy(sP2B[:], pSM[0:8, 0:8]))
                S("vector", lambda e: e.tensor_tensor(
                    counts[:], stot[0:1, 0:8], stot[0:1, 8:16], ALU.add))
                S("vector", lambda e: e.tensor_scalar_min(
                    counts[:], counts[:], float(CAP)))
                S("tensor", lambda e: e.matmul(pSM[0:8, 8:16], ones8r[:],
                                               counts[:], start=True,
                                               stop=True))
                S("vector", lambda e: e.tensor_copy(counts8[:],
                                                    pSM[0:8, 8:16]))

                def clampk(Pall, hsl):
                    S("vector", lambda e, P=Pall: e.tensor_scalar(
                        k88[:], P[:], -1.0, float(CAP), ALU.mult, ALU.add))
                    S("vector", lambda e: e.tensor_relu(k88[:], k88[:]))
                    S("vector", lambda e, h=hsl: e.tensor_tensor(
                        k88[:], k88[:], h, ALU.min))

                def clampk2(Pall, hsl):
                    S("vector", lambda e, P=Pall: e.tensor_tensor(
                        k88[:], P[:], counts8[:], ALU.add))
                    S("vector", lambda e: e.tensor_scalar(
                        k88[:], k88[:], -1.0, float(CAP), ALU.mult, ALU.add))
                    S("vector", lambda e: e.tensor_relu(k88[:], k88[:]))
                    S("vector", lambda e, h=hsl: e.tensor_tensor(
                        k88[:], k88[:], h, ALU.min))

                clampk(sP1A, stab[:, 0:8])
                S("vector", lambda e: e.tensor_copy(ktot[:], k88[:]))
                clampk(sP1B, stab[:, 8:16])
                S("vector", lambda e: e.tensor_tensor(ktot[:], ktot[:],
                                                      k88[:], ALU.add))
                S("tensor", lambda e: e.matmul(pSM[0:1, 0:8], myhot[:],
                                               ktot[:], start=True,
                                               stop=True))
                S("vector", lambda e: e.tensor_copy(myk0[:], pSM[0:1, 0:8]))
                clampk2(sP2A, stab[:, 16:24])
                S("vector", lambda e: e.tensor_tensor(ktot[:], ktot[:],
                                                      k88[:], ALU.add))
                clampk2(sP2B, stab[:, 24:32])
                S("vector", lambda e: e.tensor_tensor(ktot[:], ktot[:],
                                                      k88[:], ALU.add))
                S("tensor", lambda e: e.matmul(pSM[0:1, 0:8], ltc[:],
                                               ktot[:], start=True,
                                               stop=True))
                S("vector", lambda e: e.tensor_copy(kbrow[:], pSM[0:1, 0:8]))
                for j, src in enumerate([sP1A, sP1B, sP2A, sP2B]):
                    S("tensor", lambda e, src=src: e.matmul(
                        pSM[0:1, 0:8], myhot[:], src[:], start=True,
                        stop=True))
                    S("vector", lambda e, j=j: e.tensor_copy(
                        myrows[0:1, j * 8:(j + 1) * 8], pSM[0:1, 0:8]))

                def keep_pass(Tcs, ohf, oh, myA, myB, with_counts, keptdst):
                    S("tensor", lambda e: e.matmul(pSM[0:32, 0:8], suts[:],
                                                   Tcs[:], start=True,
                                                   stop=False))
                    S("tensor", lambda e: e.matmul(
                        pSM[0:32, 0:8], indAr[:],
                        myrows[0:1, myA * 8:myA * 8 + 8],
                        start=False, stop=False))
                    S("tensor", lambda e: e.matmul(
                        pSM[0:32, 0:8], indBr[:],
                        myrows[0:1, myB * 8:myB * 8 + 8],
                        start=False, stop=not with_counts))
                    if with_counts:
                        S("tensor", lambda e: e.matmul(
                            pSM[0:32, 0:8], ones32r[:], counts[:],
                            start=False, stop=True))
                    S("vector", lambda e: e.tensor_copy(sco[:],
                                                        pSM[0:32, 0:8]))
                    S("sync", lambda e: e.dma_start(bco, sco[:]))
                    S("sync", lambda e: e.dma_start(srow[:], bcoflat))
                    S("tensor", lambda e: e.matmul(pBIG[:], ones128[:],
                                                   srow[:], start=True,
                                                   stop=True))
                    S("tensor", lambda e, ohf=ohf: e.matmul(
                        pUT[:], ut128[:], ohf, start=True, stop=True))
                    S("vector", lambda e: e.tensor_copy(
                        sUT[:].rearrange("p a b -> p (a b)"), pUT[:]))
                    S("vector", lambda e: e.tensor_tensor(
                        spre[:].rearrange("p a b -> p (a b)"),
                        sUT[:].rearrange("p a b -> p (a b)"), pBIG[:],
                        ALU.add))
                    S("vector", lambda e: e.tensor_single_scalar(
                        spre[:], spre[:], CAP + 0.4, ALU.is_le))
                    S("vector", lambda e, oh=oh, kd=keptdst: e.tensor_tensor(
                        kd[:], spre[:], oh[:], ALU.mult))

                keep_pass(sTc, oh1f, oh1, 0, 1, False, kept0)
                keep_pass(sTc2, oh2f, oh2, 2, 3, True, kept1)

                def rank_pass(kept, extra_row, idxi, cwx, dump=False):
                    kf = kept[:].rearrange("p a b -> p (a b)")
                    S("tensor", lambda e, kf=kf: e.matmul(
                        pROW[:], onescol[:], kf, start=True, stop=True))
                    S("vector", lambda e: e.tensor_copy(srow[:], pROW[:]))
                    S("sync", lambda e: e.dma_start(bcoflat, srow[:]))
                    S("sync", lambda e: e.dma_start(sTk[:], bco))
                    S("vector", lambda e: e.tensor_tensor(
                        erow[:], ebase[:], kbrow[:], ALU.add))
                    if extra_row is not None:
                        S("vector", lambda e: e.tensor_tensor(
                            erow[:], erow[:], extra_row[:], ALU.add))
                    S("vector", lambda e: e.tensor_scalar_add(
                        erow[:], erow[:], -1.0))
                    S("tensor", lambda e: e.matmul(pSM[0:32, 0:8], sutc[:],
                                                   sTk[:], start=True,
                                                   stop=False))
                    S("tensor", lambda e: e.matmul(pSM[0:32, 0:8],
                                                   ones32r[:], erow[:],
                                                   start=False, stop=True))
                    S("vector", lambda e: e.tensor_copy(sco[:],
                                                        pSM[0:32, 0:8]))
                    if dump and DEBUG:
                        S("sync", lambda e: e.dma_start(dbgmap["dsco"],
                                                        sco[:]))
                        S("sync", lambda e: e.dma_start(dbgmap["dstk"],
                                                        sTk[:]))
                    S("sync", lambda e: e.dma_start(bco, sco[:]))
                    S("sync", lambda e: e.dma_start(srow[:], bcoflat))
                    S("tensor", lambda e: e.matmul(pBIG[:], ones128[:],
                                                   srow[:], start=True,
                                                   stop=True))
                    S("tensor", lambda e, kf=kf: e.matmul(
                        pUT[:], ut128[:], kf, start=True, stop=True))
                    S("vector", lambda e: e.tensor_copy(
                        sUT[:].rearrange("p a b -> p (a b)"), pUT[:]))
                    S("vector", lambda e: e.tensor_tensor(
                        spre[:].rearrange("p a b -> p (a b)"),
                        sUT[:].rearrange("p a b -> p (a b)"), pBIG[:],
                        ALU.add))
                    S("vector", lambda e, kept=kept: e.tensor_tensor(
                        spre[:], spre[:], kept[:], ALU.mult))
                    S("vector", lambda e: e.reduce_sum(
                        idxf[:].rearrange("p (c o) -> p c o", o=1), spre[:],
                        axis=AX.X))
                    S("vector", lambda e, kept=kept: e.reduce_sum(
                        any_[:].rearrange("p (c o) -> p c o", o=1), kept[:],
                        axis=AX.X))
                    S("vector", lambda e: e.tensor_scalar(
                        sa[:], any_[:], -SENT, SENT, ALU.mult, ALU.add))
                    S("vector", lambda e: e.tensor_tensor(
                        idxf[:], idxf[:], sa[:], ALU.add))
                    S("vector", lambda e, ii=idxi: e.tensor_copy(ii[:],
                                                                 idxf[:]))
                    S("vector", lambda e, cwx=cwx: e.tensor_tensor(
                        cwx[:], cwx[:], any_[:], ALU.mult))

                rank_pass(kept0, None, idx0i, cw0, dump=True)
                rank_pass(kept1, myk0, idx1i, cw1)

                if debug:
                    S("sync", lambda e: e.dma_start(dbgmap["dtab"], stab[:]))
                    S("vector", lambda e: e.tensor_copy(
                        spay[0:1, 0:8], counts[:]))
                    S("vector", lambda e: e.tensor_copy(
                        spay[0:1, 8:16], kbrow[:]))
                    S("vector", lambda e: e.tensor_copy(
                        spay[0:1, 16:24], myk0[:]))
                    S("sync", lambda e: e.dma_start(
                        dbgmap["drows"][0:1, 0:32], spay[:]))
                    S("sync", lambda e: e.dma_start(
                        dbgmap["drows"][0:1, 32:64], myrows[:]))
                    S("sync", lambda e: e.dma_start(dbgmap["dcw0"], cw0[:]))
                    S("sync", lambda e: e.dma_start(dbgmap["dcw1"], cw1[:]))
                    S("vector", lambda e: e.tensor_copy(sa[:], idx0i[:]))
                    S("sync", lambda e: e.dma_start(dbgmap["didx0"], sa[:]))
                    S("vector", lambda e: e.tensor_copy(sa[:], idx1i[:]))
                    S("sync", lambda e: e.dma_start(dbgmap["didx1"], sa[:]))
                    S("sync", lambda e: e.dma_start(
                        dbgmap["dlg"], lgp[:].rearrange("p a b -> p (a b)")))
                ph.run()

        def phase_scatter():
            with ExitStack() as pc:
                ph = Phase(nc, kb)
                lt = [pc.enter_context(nc.sbuf_tensor(f"sct{i}", [128, D],
                                                      BF16))
                      for i in range(2)]
                lv = lntm.rearrange("(n p) d -> n p d", p=128)
                xl_ld = []
                gp_done = []
                bcr = {}
                ph.emit("gpsimd", lambda e: bcr.__setitem__(
                    "r", e.to_reg(E * CAP - 1)))
                for i in range(NT):
                    w = []
                    if i >= 2:
                        w.append((kb.SC[i % 2], gp_done[i - 2]))
                    ph.emit("sync", lambda e, i=i: e.dma_start(
                        lt[i % 2][:], lv[i]), waits=w,
                        inc=kb.XL[i % 2], inck=16)
                    xl_ld.append(kb.XL[i % 2].n)
                    ph.emit("gpsimd", lambda e, i=i: e.indirect_dma_start(
                        stage, IOA(idx0i[:, i:i + 1], 0), lt[i % 2][:], None,
                        bounds_check=bcr["r"], oob_is_err=False),
                        waits=[(kb.XL[i % 2], xl_ld[i])],
                        inc=kb.SC[i % 2], inck=16)
                    ph.emit("gpsimd", lambda e, i=i: e.indirect_dma_start(
                        stage, IOA(idx1i[:, i:i + 1], 0), lt[i % 2][:], None,
                        bounds_check=bcr["r"], oob_is_err=False),
                        inc=kb.SC[i % 2], inck=16)
                    gp_done.append(kb.SC[i % 2].n)
                ph.run()

        def phase_expert():
            with ExitStack() as pc:
                def psb(n, s, d):
                    return pc.enter_context(nc.sbuf_tensor(n, s, d))

                def pps(n, s):
                    return pc.enter_context(nc.psum_tensor(n, s, F32))

                ph = Phase(nc, kb)
                b1r = psb("b1r", [1, H], BF16)
                b2rb = psb("b2rb", [1, D], BF16)
                w1sb = psb("w1sb", [128, FC, HCN, 128], BF16)
                w2sb = psb("w2sb", [128, HCN, DCN, 128], BF16)
                sfm = [psb(f"sfm{i}", [128, FC, 512], BF16) for i in range(2)]
                hsb = psb("hsb", [128, HCN, 512], BF16)
                u2 = psb("u2", [128, 512], F32)
                t2 = psb("t2", [128, 512], F32)
                sg = psb("sg", [128, 512], F32)
                yt = [psb(f"yt{i}", [128, 512], BF16) for i in range(2)]
                ytt = [psb(f"ytt{i}", [128, D], BF16) for i in range(2)]
                pH = [pps("pH0", [128, 512]), pps("pH1", [128, 512])]
                pW = [pps("pW0", [128, 512]), pps("pW1", [128, 512])]
                ccr = next_cc()
                ccy = next_cc()

                ph.emit("gpsimd", lambda e: e.dma_start(
                    b1r[:], b1.unsqueeze(0)), inc=GP, inck=16)
                ph.emit("gpsimd", lambda e: e.dma_start(
                    b2rb[:], b2.unsqueeze(0)), inc=GP, inck=16)
                w1v = w1.rearrange("(f p) (hc j) -> p f hc j", p=128, j=128)
                for f in range(FC):
                    ph.emit("gpsimd", lambda e, f=f: e.dma_start(
                        w1sb[:, f, :, :], w1v[:, f, :, :]), inc=GP, inck=16)
                w2v = w2.rearrange("(hc p) (dc j) -> p hc dc j", p=128, j=128)
                for q in range(4):
                    ph.emit("gpsimd", lambda e, q=q: e.dma_start(
                        w2sb[:, q * 8:(q + 1) * 8, :, :],
                        w2v[:, q * 8:(q + 1) * 8, :, :]), inc=GP, inck=16)
                gpw = GP.n
                ph.emit("gpsimd", lambda e: e.collective_compute(
                    "ReduceScatter", ALU.add, replica_groups=[list(range(8))],
                    ins=[stage.opt()], outs=[rsout.opt()]), inc=ccr)

                te0, ve0, se0 = TE.n, VE.n, SE.n
                te_h_end = {}
                te_w_end = {}
                ld_tr = {}
                se_evac = {}
                yt_store = {}

                for g in range(4):
                    for f in range(FC):
                        w = []
                        if g == 0 and f == 0:
                            w.append((ccr, 1))
                        if g >= 2 and f == 0:
                            w.append((TE, te_h_end[(g - 2, HCN - 1)]))
                        ph.emit("sync", lambda e, g=g, f=f: e.dma_start(
                            sfm[g % 2][:, f, :],
                            rsout[g * 512:(g + 1) * 512,
                                  f * 128:(f + 1) * 128],
                            transpose=True), waits=w,
                            inc=kb.XS[g % 2], inck=16)
                    ld_tr[g] = kb.XS[g % 2].n

                    for hc in range(HCN):
                        k = g * HCN + hc
                        w = [(kb.XS[g % 2], ld_tr[g])]
                        if k == 0:
                            w.append((GP, gpw))
                        if k >= 2:
                            w.append((VE, ve0 + 3 * (k - 1)))
                        if g >= 1 and hc == 0:
                            pass

                        def hmms(e, g=g, hc=hc):
                            p = pH[hc % 2]
                            e.matmul(p[:],
                                     b1r[0:1, hc * 128:(hc + 1) * 128],
                                     onesrowb[:], start=True, stop=False)
                            for f in range(FC):
                                mm = e.matmul(p[:], w1sb[:, f, hc, :],
                                              sfm[g % 2][:, f, :],
                                              start=False,
                                              stop=(f == FC - 1))
                            return mm
                        ph.emit("tensor", hmms, waits=w, inc=TE)
                        te_h_end[(g, hc)] = TE.n
                        ph.emit("scalar", lambda e, hc=hc: e.activation(
                            u2[:], pH[hc % 2][:], AF.Square),
                            waits=[(TE, TE.n)] +
                            ([(VE, ve0 + 3 * (k - 1) + 1)] if k >= 1 else []),
                            inc=SE)
                        wv = [(SE, SE.n)]
                        if g >= 1 and hc == 0:
                            wv.append((TE, te_w_end[g - 1]))
                        ph.emit("vector", lambda e: e.tensor_scalar(
                            u2[:], u2[:], GELU_A, 1.0, ALU.mult, ALU.add),
                            waits=wv)
                        ph.emit("vector", lambda e, hc=hc: e.tensor_tensor(
                            t2[:], u2[:], pH[hc % 2][:], ALU.mult), inc=VE)
                        ph.emit("scalar", lambda e: e.activation(
                            sg[:], t2[:], AF.Sigmoid, scale=GELU_2C),
                            waits=[(VE, VE.n)], inc=SE)
                        ph.emit("vector", lambda e, hc=hc: e.tensor_tensor(
                            t2[:], sg[:], pH[hc % 2][:], ALU.mult),
                            waits=[(SE, SE.n)], inc=VE)
                        ph.emit("vector", lambda e, hc=hc: e.tensor_copy(
                            hsb[:, hc, :], t2[:]), inc=VE)
                    ve_g_end = VE.n

                    for dc in range(DCN):
                        w = [(VE, ve_g_end)]
                        if (g, dc) in se_evac:
                            pass
                        prev = (g, dc - 2) if dc >= 2 else \
                            ((g - 1, dc + DCN - 2) if g >= 1 else None)
                        if prev is not None and prev in se_evac:
                            w.append((SE, se_evac[prev]))

                        def wmms(e, g=g, dc=dc):
                            p = pW[dc % 2]
                            e.matmul(p[:],
                                     b2rb[0:1, dc * 128:(dc + 1) * 128],
                                     onesrowb[:], start=True, stop=False)
                            for hc in range(HCN):
                                mm = e.matmul(p[:], w2sb[:, hc, dc, :],
                                              hsb[:, hc, :],
                                              start=False,
                                              stop=(hc == HCN - 1))
                            return mm
                        ph.emit("tensor", wmms, waits=w, inc=TE)
                        te_w_end[g] = TE.n
                        wv = [(TE, TE.n)]
                        if (g * DCN + dc) >= 2:
                            pk = (g, dc - 2) if dc >= 2 else \
                                (g - 1, dc + DCN - 2)
                            wv.append((kb.LNS[dc % 2], yt_store[pk]))
                        ph.emit("scalar", lambda e, dc=dc: e.copy(
                            yt[dc % 2][:], pW[dc % 2][:]), waits=wv, inc=SE)
                        se_evac[(g, dc)] = SE.n
                        ph.emit("sync", lambda e, g=g, dc=dc: e.dma_start(
                            yfm[dc * 128:(dc + 1) * 128,
                                g * 512:(g + 1) * 512],
                            yt[dc % 2][:]), waits=[(SE, SE.n)],
                            inc=kb.LNS[dc % 2], inck=16)
                        yt_store[(g, dc)] = kb.LNS[dc % 2].n

                lns_all = [(kb.LNS[0], kb.LNS[0].n), (kb.LNS[1], kb.LNS[1].n)]
                tr_val, st_val = {}, {}
                for kt in range(16):
                    w = list(lns_all)
                    if kt >= 2:
                        w.append((kb.TS[kt % 2], st_val[kt - 2]))
                    ph.emit("sync", lambda e, kt=kt: e.dma_start(
                        ytt[kt % 2][:], yfm[:, kt * 128:(kt + 1) * 128],
                        transpose=True), waits=w, inc=kb.TT[kt % 2], inck=16)
                    tr_val[kt] = kb.TT[kt % 2].n
                    ph.emit("sync", lambda e, kt=kt: e.dma_start(
                        yagin[kt * 128:(kt + 1) * 128, :], ytt[kt % 2][:]),
                        waits=[(kb.TT[kt % 2], tr_val[kt])],
                        inc=kb.TS[kt % 2], inck=16)
                    st_val[kt] = kb.TS[kt % 2].n
                ts_all = [(kb.TS[0], kb.TS[0].n), (kb.TS[1], kb.TS[1].n)]
                if DEBUG:
                    ph.emit("sync", lambda e: e.dma_start(
                        dbgmap["drs"], rsout), waits=[(ccr, 1)],
                        inc=LD, inck=16)
                    ph.emit("sync", lambda e: e.dma_start(
                        dbgmap["dyag"], yagin), waits=list(ts_all),
                        inc=LD, inck=16)
                ph.emit("gpsimd", lambda e: e.collective_compute(
                    "AllGather", ALU.bypass, replica_groups=[list(range(8))],
                    ins=[yagin.opt()], outs=[yall.opt()]),
                    waits=list(ts_all), inc=ccy)
                ph.emit("gpsimd", lambda e: e.nop(), waits=[(ccy, 1)])
                ph.run()

        SCALE = 1.0 / float(np.sqrt(96.0))
        CUSH = 20.0

        def phase_attn():
            with ExitStack() as pco:
                def osb(n, sh, d):
                    return pco.enter_context(nc.sbuf_tensor(n, sh, d))

                def ops(n, sh):
                    return pco.enter_context(nc.psum_tensor(n, sh, F32))

                ph = Phase(nc, kb)
                b = KBx()
                b.qc = osb("aqc", [128, 4, T], BF16)
                b.kvn = osb("akvn", [128, 2, T], BF16)
                b.Kh = osb("aKh", [128, 8, T], BF16)
                b.Vh = osb("aVh", [128, 16, 8, 65], BF16)
                b.ao = osb("aao", [128, 8, T], BF16)
                b.Qh = osb("aQh", [128, T], BF16)
                b.pT = [osb(f"apT{i}", [128, 512], BF16) for i in range(2)]
                b.scr = osb("ascr", [128, 512], F32)
                b.scr2 = osb("ascr2", [128, 512], F32)
                b.scr64 = b.scr[0:64, :]
                b.qsr = osb("aqsr", [1, T], F32)
                b.ksr = osb("aksr", [1, 4], F32)
                b.ksmax = osb("aksmax", [1, 1], F32)
                b.m16 = osb("am16", [1, 16], F32)
                b.mrowb = osb("amrowb", [1, T], BF16)
                b.lrow = osb("alrow", [1, 512], F32)
                b.wqdecsb = osb("awqdec", [128, 4, D], BF16)
                b.wqropesb = osb("awqrope", [128, 4, 512], BF16)
                b.pS = [ops("apS0", [128, 512]), ops("apS1", [128, 512])]
                b.pAO = ops("apAO", [65, 512])
                b.pB = ops("apB", [128, 512])
                b.pR1 = ops("apR1", [1, 512])
                b.pR1b = ops("apR1b", [1, 512])

                ph.seq("gpsimd", lambda e: e.dma_start(
                    b.wqdecsb[:],
                    wqdec.rearrange("(f p) j -> p f j", p=128)))
                ph.seq("gpsimd", lambda e: e.dma_start(
                    b.wqropesb[:],
                    wqropep.rearrange("(f p) j -> p f j", p=128)))

                for s_ in range(2):
                    _attn_seq(ph, s_, b)
                sv = stage.rearrange("(g p) d -> g p d", p=128)
                for g in range(E * CAP // 128):
                    ph.emit("gpsimd", lambda e, g=g:
                            e.dma_start(sv[g], ztile[:]),
                            inc=kb.SC[0], inck=16)
                ph.run()

        def _attn_seq(ph, s, b):
            S = ph.seq
            tok0 = s * T

            with ExitStack() as pa:
                def asb(n, sh, d):
                    return pa.enter_context(nc.sbuf_tensor(n, sh, d))

                af = asb(f"aaf{s}", [128, FC, 512], BF16)
                wkvsb = asb(f"awkv{s}", [128, FC, KVR], BF16)
                wkropesb = asb(f"awkrope{s}", [128, FC, 512], BF16)
                wqsb = asb(f"awq{s}", [128, FC, QR], BF16)
                krraw = asb(f"akrraw{s}", [128, 4, 512], BF16)
                qrraw = asb(f"aqrraw{s}", [128, 4, 512], BF16)
                rstage = asb(f"arstage{s}", [128, 4, 512], BF16)
                cosb = asb(f"acosb{s}", [128, T], BF16)
                sinb = asb(f"asinb{s}", [128, T], BF16)
                t1 = asb(f"at1{s}", [128, 512], F32)
                t2a = asb(f"at2a{s}", [128, 512], F32)
                rstdr = asb(f"arstdr{s}", [1, 512], F32)
                pPJ = pa.enter_context(nc.psum_tensor(f"apPJ{s}",
                                                      [128, 512], F32))

                S("gpsimd", lambda e: e.dma_start(cosb[:], ccosr))
                S("gpsimd", lambda e: e.dma_start(sinb[:], csinr))
                S("gpsimd", lambda e: e.dma_start(
                    wkvsb[:], wkv.rearrange("(f p) j -> p f j", p=128)))
                S("gpsimd", lambda e: e.dma_start(
                    wkropesb[:],
                    wkropep.rearrange("(f p) j -> p f j", p=128)))
                S("gpsimd", lambda e: e.dma_start(
                    wqsb[:], wq.rearrange("(f p) j -> p f j", p=128)))
                gpA = ph.last

                def rope(c0, raw, dstA, dstB, cA, cB):
                    cs = slice(c0, c0 + 512)
                    S("vector", lambda e: e.tensor_tensor(
                        t1[:], raw[:, cA, :], cosb[:, cs], ALU.mult))
                    S("vector", lambda e: e.tensor_tensor(
                        t2a[:], raw[:, cB, :], sinb[:, cs], ALU.mult))
                    S("vector", lambda e: e.tensor_tensor(
                        t1[:], t1[:], t2a[:], ALU.subtract))
                    S("vector", lambda e: e.tensor_tensor(
                        t2a[:], raw[:, cB, :], cosb[:, cs], ALU.mult))
                    S("vector", lambda e: e.tensor_copy(dstA, t1[:]))
                    S("vector", lambda e: e.tensor_tensor(
                        t1[:], raw[:, cA, :], sinb[:, cs], ALU.mult))
                    S("vector", lambda e: e.tensor_tensor(
                        t2a[:], t2a[:], t1[:], ALU.add))
                    S("vector", lambda e: e.tensor_copy(dstB, t2a[:]))

                for tg in range(4):
                    c0 = tg * 512
                    cs = slice(c0, c0 + 512)
                    for f in range(FC):
                        S("sync", lambda e, f=f, c0=c0: e.dma_start(
                            af[:, f, :],
                            lntm[tok0 + c0:tok0 + c0 + 512,
                                 f * 128:(f + 1) * 128], transpose=True))

                    def mmgroup(wsb, oc, wid=128):
                        def g(e, oc=oc):
                            for f in range(FC):
                                mm = e.matmul(
                                    pPJ[0:wid, :],
                                    wsb[:, f, oc * wid:(oc + 1) * wid],
                                    af[:, f, :],
                                    start=(f == 0), stop=(f == FC - 1))
                            return mm
                        return g

                    for oc in range(2):
                        S("tensor", mmgroup(wkvsb, oc),
                          extra_waits=[gpA])
                        S("scalar", lambda e, oc=oc, cs=cs: e.copy(
                            b.kvn[:, oc, cs], pPJ[:]))
                    S("scalar", lambda e, cs=cs: e.activation(
                        b.scr[:], b.kvn[:, 0, cs], AF.Square))
                    S("tensor", lambda e: e.matmul(
                        b.pR1[:], onescol[:], b.scr[:], start=True,
                        stop=False))
                    S("scalar", lambda e, cs=cs: e.activation(
                        b.scr[:], b.kvn[:, 1, cs], AF.Square))
                    S("tensor", lambda e: e.matmul(
                        b.pR1[:], onescol[:], b.scr[:], start=False,
                        stop=True))
                    S("vector", lambda e: e.tensor_scalar(
                        rstdr[:], b.pR1[:], 1.0 / KVR, 1e-6, ALU.mult,
                        ALU.add))
                    S("scalar", lambda e: e.activation(rstdr[:], rstdr[:],
                                                       AF.Ln))
                    S("scalar", lambda e: e.activation(rstdr[:], rstdr[:],
                                                       AF.Exp, scale=-0.5))
                    S("tensor", lambda e: e.matmul(
                        b.pB[:], ones128[:], rstdr[:], start=True,
                        stop=True))
                    for oc in range(2):
                        S("vector", lambda e, oc=oc, cs=cs: e.tensor_tensor(
                            t1[:], b.kvn[:, oc, cs], b.pB[:], ALU.mult))
                        S("scalar", lambda e, oc=oc, cs=cs: e.activation(
                            b.kvn[:, oc, cs], t1[:], AF.Copy,
                            scale=gkvc[:, oc:oc + 1]))
                    for oc in range(4):
                        S("tensor", mmgroup(wqsb, oc))
                        S("scalar", lambda e, oc=oc, cs=cs: e.copy(
                            b.qc[:, oc, cs], pPJ[:]))
                    if DEBUG and s == 0 and tg == 0:
                        S("sync", lambda e: e.dma_start(
                            dbgmap["dqc2"],
                            b.qc[:].rearrange("p a c -> p (a c)")))
                        S("sync", lambda e: e.dma_start(
                            dbgmap["dwq"],
                            wqsb[:].rearrange("p a c -> p (a c)")))
                        S("sync", lambda e: e.dma_start(
                            dbgmap["dwkv"],
                            wkvsb[:].rearrange("p a c -> p (a c)")))
                    for oc in range(4):
                        S("tensor", mmgroup(wkropesb, oc))
                        S("scalar", lambda e, oc=oc: e.copy(
                            krraw[:, oc, :], pPJ[:]))
                    for cp in range(2):
                        rope(c0, krraw, rstage[:, cp, :],
                             rstage[:, cp + 2, :], cp, cp + 2)
                    for ch in range(4):
                        S("sync", lambda e, ch=ch, c0=c0: e.dma_start(
                            kroped[ch * 128:(ch + 1) * 128, c0:c0 + 512],
                            rstage[:, ch, :]))
                    if DEBUG and s == 0 and tg == 0:
                        S("sync", lambda e: e.dma_start(
                            dbgmap["dqcF"],
                            b.qc[:].rearrange("p a c -> p (a c)")))
                    for oc in range(4):
                        def qrmm(e, oc=oc, cs=cs):
                            for f in range(4):
                                mm = e.matmul(
                                    pPJ[:],
                                    b.wqropesb[:, f,
                                               oc * 128:(oc + 1) * 128],
                                    b.qc[:, f, cs],
                                    start=(f == 0), stop=(f == 3))
                            return mm
                        S("tensor", qrmm)
                        S("scalar", lambda e, oc=oc: e.copy(
                            qrraw[:, oc, :], pPJ[:]))
                    for cp in range(2):
                        rope(c0, qrraw, rstage[:, cp, :],
                             rstage[:, cp + 2, :], cp, cp + 2)
                    for ch in range(4):
                        S("sync", lambda e, ch=ch, c0=c0: e.dma_start(
                            qroped[ch * 128:(ch + 1) * 128, c0:c0 + 512],
                            rstage[:, ch, :]))

            if DEBUG and s == 0:
                S("sync", lambda e: e.dma_start(
                    dbgmap["dqc"], b.qc[:].rearrange("p a c -> p (a c)")))
                S("sync", lambda e: e.dma_start(dbgmap["dkrop"], kroped))
                S("sync", lambda e: e.dma_start(dbgmap["dqrop"], qroped))

            for hh in range(2):
                with ExitStack() as pb_:
                    def bsb(n, sh, d):
                        return pb_.enter_context(nc.sbuf_tensor(n, sh, d))

                    wkdecsb = bsb(f"awkdec{s}{hh}", [128, 2, 512], BF16)
                    wvdecsb = bsb(f"awvdec{s}{hh}", [128, 2, 512], BF16)
                    vttp = [bsb(f"avtt{s}{hh}{i}", [128, 512], BF16)
                            for i in range(2)]
                    pPJ2 = [pb_.enter_context(nc.psum_tensor(
                        f"apPJ2{s}{hh}{i}", [64, 512], F32))
                        for i in range(2)]

                    S("gpsimd", lambda e, hh=hh: e.dma_start(
                        wkdecsb[:],
                        wkdec.rearrange("(f p) j -> p f j", p=128)
                        [:, :, hh * 512:(hh + 1) * 512]))
                    S("gpsimd", lambda e, hh=hh: e.dma_start(
                        wvdecsb[:],
                        wvdec.rearrange("(f p) j -> p f j", p=128)
                        [:, :, hh * 512:(hh + 1) * 512]))
                    gpB = ph.last

                    S("vector", lambda e: e.memset(b.Kh[:], 0.0))
                    S("vector", lambda e: e.memset(b.Qh[:], 0.0))
                    S("vector", lambda e: e.memset(b.Vh[:], 1.0))
                    join0 = ph.last
                    se_pa = SE.n
                    for hl in range(8):
                        ph.emit("gpsimd", lambda e, hl=hl: e.dma_start(
                            b.Kh[17:18, hl, :], conesb2048),
                            waits=[join0], inc=GP, inck=16)
                    pjev = {0: None, 1: None}
                    last_vcp = None
                    vst = {0: None, 1: None}
                    pj = 0
                    vv = 0
                    for hl in range(8):
                        h = hh * 8 + hl
                        for tg in range(4):
                            c0 = tg * 512
                            bk = pj % 2
                            w = [] if pjev[bk] is None else \
                                [(SE, pjev[bk])]
                            if pj < 2:
                                w += [gpB, (SE, se_pa), join0]

                            def kdmm(e, hl=hl, c0=c0, bk=bk):
                                for f in range(2):
                                    mm = e.matmul(
                                        pPJ2[bk][:],
                                        wkdecsb[:, f,
                                                hl * 64:(hl + 1) * 64],
                                        b.kvn[:, f, c0:c0 + 512],
                                        start=(f == 0), stop=(f == 1))
                                return mm
                            ph.emit("tensor", kdmm, waits=w, inc=TE)
                            ph.emit("scalar",
                                    lambda e, hl=hl, c0=c0, bk=bk: e.copy(
                                        b.Kh[64:128, hl, c0:c0 + 512],
                                        pPJ2[bk][:]),
                                    waits=[(TE, TE.n)], inc=SE)
                            pjev[bk] = SE.n
                            pj += 1
                        ph.emit("gpsimd", lambda e, h=h, hl=hl: e.dma_start(
                            b.Kh[0:16, hl, :],
                            kroped[(h // 8) * 128 + (h % 8) * 16:
                                   (h // 8) * 128 + (h % 8) * 16 + 16, :]),
                            waits=[join0], inc=GP, inck=16)
                        ph.emit("gpsimd", lambda e, h=h, hl=hl: e.dma_start(
                            b.Kh[32:48, hl, :],
                            kroped[256 + (h // 8) * 128 + (h % 8) * 16:
                                   256 + (h // 8) * 128 + (h % 8) * 16 + 16,
                                   :]),
                            waits=[join0], inc=GP, inck=16)
                        for tg in range(4):
                            c0 = tg * 512
                            bk = pj % 2
                            vb = vv % 2
                            w = [] if pjev[bk] is None else \
                                [(SE, pjev[bk])]

                            def vdmm(e, hl=hl, c0=c0, bk=bk):
                                for f in range(2):
                                    mm = e.matmul(
                                        pPJ2[bk][:],
                                        wvdecsb[:, f,
                                                hl * 64:(hl + 1) * 64],
                                        b.kvn[:, f, c0:c0 + 512],
                                        start=(f == 0), stop=(f == 1))
                                return mm
                            ph.emit("tensor", vdmm, waits=w, inc=TE)
                            wv = [(TE, TE.n)]
                            if last_vcp is not None:
                                wv.append((VE, last_vcp))
                            ph.emit("scalar", lambda e, bk=bk: e.copy(
                                b.scr64, pPJ2[bk][:]), waits=wv, inc=SE)
                            pjev[bk] = SE.n
                            wc = [(SE, SE.n)]
                            if vst[vb] is not None:
                                wc.append((kb.TS[vb], vst[vb]))
                            ph.emit("vector", lambda e, vb=vb: e.tensor_copy(
                                vttp[vb][0:64, :], b.scr64),
                                waits=wc, inc=VE)
                            last_vcp = VE.n
                            ph.emit("sync",
                                    lambda e, hl=hl, c0=c0, vb=vb:
                                    e.dma_start(
                                        vfmd[hl * 64:(hl + 1) * 64,
                                             c0:c0 + 512],
                                        vttp[vb][0:64, :]),
                                    waits=[(VE, VE.n)],
                                    inc=kb.TS[vb], inck=16)
                            vst[vb] = kb.TS[vb].n
                            pj += 1
                            vv += 1
                    ts_fin = [(kb.TS[0], kb.TS[0].n),
                              (kb.TS[1], kb.TS[1].n)]
                    trv = {}
                    vhc = {0: None, 1: None}
                    for kt in range(16):
                        vb = kt % 2
                        w = list(ts_fin)
                        if vhc[vb] is not None:
                            w.append((VE, vhc[vb]))
                        ph.emit("sync", lambda e, kt=kt, vb=vb: e.dma_start(
                            vttp[vb][:],
                            vfmd[0:512, kt * 128:(kt + 1) * 128],
                            transpose=True), waits=w,
                            inc=kb.TT[vb], inck=16)
                        trv[kt] = kb.TT[vb].n
                        ph.emit("vector", lambda e, kt=kt, vb=vb:
                                e.tensor_copy(
                                    b.Vh[:, kt, :, 0:64],
                                    vttp[vb][:].rearrange(
                                        "p (h d) -> p h d", d=64)),
                                waits=[(kb.TT[vb], trv[kt])], inc=VE)
                        vhc[vb] = VE.n
                    ph.emit("gpsimd", lambda e: e.nop(),
                            waits=[(TE, TE.n), (SE, SE.n), (VE, VE.n),
                                   (GP, GP.n)] + ts_fin +
                            [(kb.TT[0], kb.TT[0].n),
                             (kb.TT[1], kb.TT[1].n)],
                            inc=kb.PLS)

                    for hl in range(8):
                        _attn_head(ph, s, hh * 8 + hl, hl, b)

            with ExitStack() as pw:
                wosb = pw.enter_context(nc.sbuf_tensor(
                    f"awosb{s}", [128, 8, 8, 128], BF16))
                cfm = [pw.enter_context(nc.sbuf_tensor(
                    f"acfm{s}{i}", [128, 512], BF16)) for i in range(2)]
                pWO = [pw.enter_context(nc.psum_tensor(
                    f"apWO{s}{i}", [128, 512], F32)) for i in range(2)]
                S("gpsimd", lambda e: e.dma_start(
                    wosb[:],
                    wo.rearrange("(pr p) (dc j) -> p pr dc j", p=128,
                                 j=128)))
                gpW = ph.last
                se_heads = SE.n
                wo_ev = {0: None, 1: None}
                wo_st = {0: None, 1: None}
                for qg in range(4):
                    for dc in range(DCN):
                        bk = dc % 2
                        w = [gpW, (SE, se_heads)] if wo_ev[bk] is None \
                            else [(SE, wo_ev[bk])]

                        def womm(e, qg=qg, dc=dc, bk=bk):
                            for pr in range(8):
                                mm = e.matmul(
                                    pWO[bk][:], wosb[:, pr, dc, :],
                                    b.ao[:, pr, qg * 512:(qg + 1) * 512],
                                    start=(pr == 0), stop=(pr == 7))
                            return mm
                        ph.emit("tensor", womm, waits=w, inc=TE)
                        wv = [(TE, TE.n)]
                        if wo_st[bk] is not None:
                            wv.append((kb.TS[bk], wo_st[bk]))
                        ph.emit("scalar", lambda e, bk=bk: e.copy(
                            cfm[bk][:], pWO[bk][:]), waits=wv, inc=SE)
                        wo_ev[bk] = SE.n
                        ph.emit("sync", lambda e, qg=qg, dc=dc, bk=bk:
                                e.dma_start(
                                    contribfm[dc * 128:(dc + 1) * 128,
                                              tok0 + qg * 512:
                                              tok0 + qg * 512 + 512],
                                    cfm[bk][:]),
                                waits=[(SE, SE.n)],
                                inc=kb.TS[bk], inck=16)
                        wo_st[bk] = kb.TS[bk].n
                ph.emit("gpsimd", lambda e: e.nop(),
                        waits=[(TE, TE.n), (SE, SE.n),
                               (kb.TS[0], kb.TS[0].n),
                               (kb.TS[1], kb.TS[1].n)],
                        inc=kb.PLS)

        def _attn_head(ph, s, h, hl, b):
            S = ph.seq
            hd0 = ph.last
            qd_ev = {0: None, 1: None}
            for tg in range(4):
                c0 = tg * 512
                bk = tg % 2
                w = [hd0] if qd_ev[bk] is None else [(SE, qd_ev[bk])]

                def qdmm(e, c0=c0, bk=bk):
                    for f in range(4):
                        mm = e.matmul(
                            b.pS[bk][0:64, :],
                            b.wqdecsb[:, f, h * 64:(h + 1) * 64],
                            b.qc[:, f, c0:c0 + 512],
                            start=(f == 0), stop=(f == 3))
                    return mm
                ph.emit("tensor", qdmm, waits=w, inc=TE)
                ph.emit("scalar", lambda e, c0=c0, bk=bk: e.copy(
                    b.Qh[64:128, c0:c0 + 512], b.pS[bk][0:64, :]),
                    waits=[(TE, TE.n)], inc=SE)
                qd_ev[bk] = SE.n
            S("gpsimd", lambda e: e.dma_start(
                b.Qh[0:16, :],
                qroped[(h // 8) * 128 + (h % 8) * 16:
                       (h // 8) * 128 + (h % 8) * 16 + 16, :]))
            S("gpsimd", lambda e: e.dma_start(
                b.Qh[32:48, :],
                qroped[256 + (h // 8) * 128 + (h % 8) * 16:
                       256 + (h // 8) * 128 + (h % 8) * 16 + 16, :]))
            stats0 = ph.last
            mmQv = mmKv = krv = None
            for tg in range(4):
                c0 = tg * 512
                wq_ = [stats0] if tg == 0 else [(TE, mmQv)]
                ph.emit("scalar", lambda e, c0=c0: e.activation(
                    b.scr[:], b.Qh[:, c0:c0 + 512], AF.Square),
                    waits=wq_, inc=SE)
                ph.emit("tensor", lambda e: e.matmul(
                    b.pR1[:], qmaskc[:], b.scr[:], start=True, stop=True),
                    waits=[(SE, SE.n)], inc=TE)
                mmQv = TE.n
                ph.emit("scalar", lambda e, c0=c0: e.copy(
                    b.qsr[0:1, c0:c0 + 512], b.pR1[:]),
                    waits=[(TE, mmQv)], inc=SE)
                wk_ = [] if mmKv is None else [(TE, mmKv)]
                ph.emit("scalar", lambda e, c0=c0, hl=hl: e.activation(
                    b.scr2[:], b.Kh[:, hl, c0:c0 + 512], AF.Square),
                    waits=wk_, inc=SE)
                ph.emit("tensor", lambda e: e.matmul(
                    b.pR1b[:], qmaskc[:], b.scr2[:], start=True, stop=True),
                    waits=[(SE, SE.n)], inc=TE)
                mmKv = TE.n
                wl_ = [(TE, mmKv)] + ([] if krv is None else [(VE, krv)])
                ph.emit("scalar", lambda e: e.copy(b.lrow[:], b.pR1b[:]),
                        waits=wl_, inc=SE)
                ph.emit("vector", lambda e, tg=tg: e.reduce_max(
                    b.ksr[0:1, tg:tg + 1].unsqueeze(2),
                    b.lrow[:].unsqueeze(1), axis=AX.X),
                    waits=[(SE, SE.n)], inc=VE)
                krv = VE.n
            S("vector", lambda e: e.reduce_max(
                b.m16[:].unsqueeze(2),
                b.qsr[:].rearrange("o (a c) -> o a c", a=16), axis=AX.X),
              extra_waits=[(SE, SE.n)])
            S("vector", lambda e: e.reduce_max(
                b.ksmax[:].unsqueeze(2), b.ksr[:].unsqueeze(1), axis=AX.X))
            S("vector", lambda e: e.tensor_scalar(
                b.m16[:], b.m16[:], b.ksmax[0:1, 0:1], None, ALU.mult))
            S("scalar", lambda e: e.activation(b.m16[:], b.m16[:], AF.Ln))
            S("scalar", lambda e: e.activation(b.m16[:], b.m16[:], AF.Exp,
                                               scale=0.5))
            S("vector", lambda e: e.tensor_scalar(
                b.m16[:], b.m16[:], 1.0, CUSH, ALU.mult, ALU.add))
            S("vector", lambda e: e.tensor_scalar(
                b.mrowb[:].rearrange("o (a c) -> o a c", c=128),
                b.m16[:].unsqueeze(2).broadcast_to([1, 16, 128]),
                -1.0, None, ALU.mult))
            S("gpsimd", lambda e: e.dma_start(b.Qh[17:18, :], b.mrowb[:]))

            pao_free = None
            ao_mult = None
            hd0 = ph.last
            for qg in range(4):
                kts = list(range(4 * qg + 3, 4 * qg - 1, -1)) + \
                    list(range(0, 4 * qg))
                n = len(kts)
                exp_done = {}
                av_done = {}

                def emit_smm(j):
                    kt = kts[j]
                    wdt = 512 if kt < 4 * qg else \
                        512 - (kt - 4 * qg) * 128
                    q0 = qg * 512 + (512 - wdt)
                    wts = []
                    if j >= 2:
                        wts.append((SE, exp_done[j - 2]))

                    def smm(e, kt=kt, wdt=wdt, q0=q0, j=j):
                        return e.matmul(
                            b.pS[j % 2][:, 512 - wdt:],
                            b.Kh[:, hl, kt * 128:(kt + 1) * 128],
                            b.Qh[:, q0:q0 + wdt], start=True, stop=True)
                    ph.emit("tensor", smm, waits=wts, inc=TE)
                    smm_val = TE.n
                    if kt >= 4 * qg:
                        ph.emit("vector", lambda e, wdt=wdt, j=j:
                                e.tensor_tensor(
                                    b.pS[j % 2][:, 512 - wdt:512 - wdt + 128],
                                    b.pS[j % 2][:, 512 - wdt:512 - wdt + 128],
                                    trimaskb[:], ALU.add),
                                waits=[(TE, smm_val)], inc=VE)
                        wexp = [(VE, VE.n)]
                    else:
                        wexp = [(TE, smm_val)]
                    if j >= 2:
                        wexp.append((TE, av_done[j - 2]))

                    def expf(e, wdt=wdt, j=j):
                        return e.activation(
                            b.pT[j % 2][:, 512 - wdt:],
                            b.pS[j % 2][:, 512 - wdt:],
                            AF.Exp, scale=SCALE)
                    ph.emit("scalar", expf, waits=wexp, inc=SE)
                    exp_done[j] = SE.n

                def emit_av(j):
                    kt = kts[j]
                    wdt = 512 if kt < 4 * qg else \
                        512 - (kt - 4 * qg) * 128

                    def avmm(e, kt=kt, wdt=wdt, j=j):
                        return e.matmul(
                            b.pAO[:, 512 - wdt:], b.Vh[:, kt, hl, :],
                            b.pT[j % 2][:, 512 - wdt:],
                            start=False, stop=False)
                    ph.emit("tensor", avmm,
                            waits=[(SE, exp_done[j])], inc=TE)
                    av_done[j] = TE.n

                ph.emit("tensor", lambda e: e.matmul(
                    b.pAO[:, :], ztile[0:1, 0:65], ztile[0:1, 0:512],
                    start=True, stop=False),
                    waits=[pao_free if pao_free is not None else hd0],
                    inc=TE)
                emit_smm(0)
                for j in range(1, n):
                    emit_smm(j)
                    emit_av(j - 1)
                emit_av(n - 1)
                ph.emit("tensor", lambda e: e.matmul(
                    b.pAO[:, :], ztile[0:1, 0:65], ztile[0:1, 0:512],
                    start=False, stop=True), inc=TE)
                te_stop = TE.n
                S("scalar", lambda e: e.copy(b.lrow[:], b.pAO[64:65, :]),
                  extra_waits=[(TE, te_stop)])
                S("vector", lambda e: e.reciprocal(b.lrow[:], b.lrow[:]))
                S("tensor", lambda e: e.matmul(
                    b.pB[:], ones128[:], b.lrow[:], start=True, stop=True))
                S("scalar", lambda e: e.copy(b.scr64, b.pAO[0:64, :]))
                S("vector", lambda e: e.tensor_tensor(
                    b.scr64, b.scr64, b.pB[0:64, :], ALU.mult))
                S("scalar", lambda e, qg=qg: e.copy(
                    b.ao[(h % 2) * 64:(h % 2) * 64 + 64, h // 2,
                         qg * 512:(qg + 1) * 512], b.scr64))
                pao_free = (SE, SE.n)
            if DEBUG and s == 0 and h == 0:
                S("sync", lambda e: e.dma_start(dbgmap["dm16"], b.m16[:]))
                S("sync", lambda e: e.dma_start(dbgmap["dqsr"], b.qsr[:]))
                S("sync", lambda e: e.dma_start(dbgmap["dksr"], b.ksr[:]))
                S("sync", lambda e: e.dma_start(dbgmap["dqh"], b.Qh[:]))
                S("sync", lambda e: e.dma_start(dbgmap["dkh"],
                                                b.Kh[:, 0, :]))
                S("sync", lambda e: e.dma_start(
                    dbgmap["dao0"], b.ao[:, 0, :]))
                S("sync", lambda e: e.dma_start(
                    dbgmap["dkvn"],
                    b.kvn[:].rearrange("p a c -> p (a c)")))

        plist = [
            phase_init,
            lambda: phase_boundary("1", x_in, None, write_ln=True),
            phase_gating,
            lambda: phase_routing(debug=DEBUG),
            phase_scatter,
            phase_expert,
        ]
        if STAGE == 1:
            plist.append(lambda: phase_boundary("2", x_in, out,
                                                write_ln=False, gather=True))
        elif STAGE == 2:
            plist += [
                lambda: phase_boundary("2", x_in, xdram, write_ln=True,
                                       gather=True),
                phase_attn,
                lambda: phase_boundary("3", xdram, out, write_ln=False,
                                       attn=True),
            ]
        else:
            plist += [
                lambda: phase_boundary("2", x_in, xdram, write_ln=True,
                                       gather=True),
                phase_attn,
                lambda: phase_boundary("3", xdram, xdram, write_ln=True,
                                       attn=True),
                phase_gating,
                lambda: phase_routing(),
                phase_scatter,
                phase_expert,
                lambda: phase_boundary("4", xdram, out, write_ln=False,
                                       gather=True),
            ]
        for i, p in enumerate(plist):
            if i < nph:
                p()

    return nc


def _consts(cid):
    ut = np.triu(np.ones((128, 128), np.float32))
    sut_split = np.zeros((32, 32), np.float32)
    for i in range(32):
        for k in range(32):
            if (k < i) and ((k < 16) == (i < 16)):
                sut_split[k, i] = 1.0
    sut_cont = np.triu(np.ones((32, 32), np.float32), 1)
    sut8 = np.triu(np.ones((8, 8), np.float32), 1)
    indA = np.zeros((1, 32), np.float32)
    indA[0, :16] = 1
    indB = np.zeros((1, 32), np.float32)
    indB[0, 16:] = 1
    ltc = (np.arange(8) < cid).astype(np.float32).reshape(8, 1)
    myh = np.zeros((8, 1), np.float32)
    myh[cid, 0] = 1
    tri = np.where(np.arange(128)[:, None] <= np.arange(128)[None, :],
                   0.0, -1.0e30).astype(np.float32)
    inv_freq = 1.0 / 10000.0 ** (np.arange(0, RD, 2, dtype=np.float32) / RD)
    t = np.arange(T, dtype=np.float32)
    freqs = np.outer(t, inv_freq)
    cosr = np.tile(np.cos(freqs).T, (8, 1)).astype(np.float32)
    sinr = np.tile(np.sin(freqs).T, (8, 1)).astype(np.float32)
    qmask = np.ones((128, 1), np.float32)
    qmask[16:32] = 0
    qmask[48:64] = 0
    return dict(
        cut128=ut, cident=np.eye(128, dtype=np.float32), csuts=sut_split, csutc=sut_cont, csut8=sut8,
        cones128=np.ones((1, 128), np.float32),
        conescol=np.ones((128, 1), np.float32),
        cindAr=indA, cindBr=indB,
        cindAc=np.ascontiguousarray(indA.reshape(32, 1)),
        cindBc=np.ascontiguousarray(indB.reshape(32, 1)),
        cones32r=np.ones((1, 32), np.float32),
        cones8c=np.ones((8, 1), np.float32),
        cones8r=np.ones((1, 8), np.float32),
        cltc=ltc, cmyhot=myh,
        cebase=(np.arange(8, dtype=np.float32) * CAP).reshape(1, 8),
        conesrowb=np.ones((1, 512), ml_dtypes.bfloat16),
        ctrimask=tri, ccosr=cosr, csinr=sinr, cqmask=qmask,
        conesb2048=np.ones((1, T), ml_dtypes.bfloat16),
    )


def _rope_perm():
    idx = np.zeros(NH * RD, np.int64)
    for h in range(NH):
        for r in range(RD):
            old = h * RD + r
            if r < 16:
                new = (h // 8) * 128 + (h % 8) * 16 + r
            else:
                new = 256 + (h // 8) * 128 + (h % 8) * 16 + (r - 16)
            idx[new] = old
    return idx


def kernel(**inputs):
    f32 = np.float32
    x = np.ascontiguousarray(np.asarray(inputs["L0_attn_x"], f32))
    perm = _rope_perm()
    shared = dict(
        wg=np.ascontiguousarray(np.asarray(inputs["L0_ffn_Wg"], f32)),
        wkv=np.ascontiguousarray(np.asarray(inputs["L0_attn_W_kv"], f32)),
        gkv=np.ascontiguousarray(np.asarray(inputs["L0_attn_g_kv"], f32)),
        wkdec=np.ascontiguousarray(np.asarray(inputs["L0_attn_W_kdec"], f32)),
        wvdec=np.ascontiguousarray(np.asarray(inputs["L0_attn_W_vdec"], f32)),
        wq=np.ascontiguousarray(np.asarray(inputs["L0_attn_W_q"], f32)),
        wqdec=np.ascontiguousarray(np.asarray(inputs["L0_attn_W_qdec"], f32)),
        wkropep=np.ascontiguousarray(
            np.asarray(inputs["L0_attn_W_krope"], f32)[:, perm]),
        wqropep=np.ascontiguousarray(
            np.asarray(inputs["L0_attn_W_qrope"], f32)[:, perm]),
        wo=np.ascontiguousarray(np.asarray(inputs["L0_attn_W_o"], f32)),
    )
    W1 = np.asarray(inputs["L0_ffn_W1"], f32)
    B1 = np.asarray(inputs["L0_ffn_b1"], f32)
    W2 = np.asarray(inputs["L0_ffn_W2"], f32)
    B2 = np.asarray(inputs["L0_ffn_b2"], f32)

    import os
    nc = build(int(os.environ.get("KNP", "99")))
    in_maps = []
    for c in range(8):
        shard = np.ascontiguousarray(np.concatenate([x[c], x[c + 8]], axis=0))
        m = dict(shared)
        m.update(_consts(c))
        m.update(x=shard,
                 w1=np.ascontiguousarray(W1[c]),
                 b1=np.ascontiguousarray(B1[c]),
                 w2=np.ascontiguousarray(W2[c]),
                 b2=np.ascontiguousarray(B2[c]))
        in_maps.append(m)
    res = run_bass_kernel_spmd(nc, in_maps, core_ids=list(range(8)))
    o = np.empty((B, T, D), f32)
    for c in range(8):
        y = np.asarray(res.results[c]["out"], f32)
        o[c] = y[:T]
        o[c + 8] = y[T:]
    if DEBUG:
        kernel.dbg = [{k: np.asarray(v) for k, v in r.items()
                       if k.startswith("d")} for r in res.results]
    return o
